# version 03
# baseline (speedup 1.0000x reference)
import math
import numpy as np
import ml_dtypes
from contextlib import ExitStack

import concourse.bass as bass
import concourse.mybir as mybir
from concourse.bass_utils import run_bass_kernel_spmd

F32 = mybir.dt.float32
BF16 = mybir.dt.bfloat16
I32 = mybir.dt.int32
AF = mybir.ActivationFunctionType
ALU = mybir.AluOpType
AX = mybir.AxisListType
FP = np.float32
DETECT_RACES = False

D = 1024
S = 2048
NT = 16
H = 16
HD = 64
FF0 = 4096
FF1 = 3072
C_QKV = float((3.0 * D * HD * H) ** (-0.25))
C_OUT = float((D * D) ** (-0.25))
SM = float(HD ** (-0.5))
NEG = -1e9
SKW = 2176

ENGINES = ("sync", "scalar", "vector", "tensor", "gpsimd")

DBG = {"stats": True, "prep": True, "main_tiles": NT, "fcs": 8, "epi": True, "newton": True}

ALL_PHASES = ("l0ffn_a", "l0attn", "l0ffn_b", "l1ffn_a", "l1attn", "l1ffn_b")


class Em:

    def __init__(self, nc, ctx):
        self.nc = nc
        self.ctx = ctx
        self.q = {e: [] for e in ENGINES}
        self.last_real = {e: None for e in ENGINES}
        self.semh = {}
        self.cnt = {}
        self.lastw = {}
        self.dma_eng = {}

    def sem(self, name):
        if name not in self.semh:
            self.semh[name] = self.ctx.enter_context(self.nc.semaphore(name))
            self.cnt[name] = 0
        return name

    def ev(self, name):
        return (name, self.cnt[name])

    AUTO_SER = {"vector": "ve", "scalar": "se", "gpsimd": "po"}

    def emit(self, eng, fn, waits=(), inc=None, inc_n=1, real=True, _dma=False):
        waits = list(waits)
        if real and not _dma and eng in self.AUTO_SER:
            cs = self.AUTO_SER[eng]
            self.sem(cs)
            waits.append((cs, self.cnt[cs]))
            if inc is None:
                inc = cs
            else:
                assert inc == cs, (eng, inc)
        rw = []
        seen = {}
        for w in waits:
            if w is None:
                continue
            s, v = w
            if v <= 0:
                continue
            if seen.get(s, 0) >= v:
                continue
            seen[s] = v
        rw = [(s, v) for s, v in seen.items()]
        tgt = None
        if inc is not None:
            assert real
            self.sem(inc)
            self.cnt[inc] += inc_n
            tgt = (inc, self.cnt[inc], inc_n)
        entry = [rw, fn, tgt, real]
        self.q[eng].append(entry)
        if real:
            self.last_real[eng] = entry
        return self.ev(inc) if inc is not None else None

    def dma(self, eng, out, in_, waits=(), inc=None, **kw):
        if inc is not None:
            self.dma_eng[inc] = eng
        return self.emit(eng, lambda e: e.dma_start(out=out, in_=in_, **kw),
                         waits=waits, inc=inc, inc_n=16, _dma=True)

    def drain_dmas(self):
        for sname, eng in list(self.dma_eng.items()):
            self.emit(eng, lambda e: None, waits=[self.ev(sname)], real=False)

    def engine_done_ev(self, e):
        entry = self.last_real[e]
        if entry is None:
            return None
        if entry[2] is None:
            sname = {"sync": "qS", "scalar": "qA", "vector": "ve", "tensor": "pe",
                     "gpsimd": "po"}[e]
            self.sem(sname)
            self.cnt[sname] += 1
            entry[2] = (sname, self.cnt[sname], 1)
        return (entry[2][0], entry[2][1])

    def barrier(self):
        evs = [self.engine_done_ev(e) for e in ENGINES]
        evs += [self.ev(s) for s in self.dma_eng]
        for e in ENGINES:
            self.emit(e, lambda eng: None, waits=evs, real=False)

    def play(self, blk):
        semh = self.semh

        def run(eng, stream):
            for rw, fn, tgt, _real in stream:
                for (s, v) in rw:
                    eng.wait_ge(semh[s], v)
                inst = fn(eng)
                if tgt is not None and inst is not None:
                    inst.then_inc(semh[tgt[0]], tgt[2])

        for e in ENGINES:
            stream = self.q[e]
            if not stream:
                continue
            getattr(blk, e)(lambda eng, stream=stream: run(eng, stream))


def rsqrt_newton(em, g, out, qcol, scratch, iters=2):
    r = scratch[:, 0:1]
    t = scratch[:, 1:2]
    fa = scratch[:, 2:3]
    em.emit("vector", lambda e: e.tensor_copy(fa, qcol.bitcast(I32)))
    em.emit("vector", lambda e: e.tensor_scalar(
        fa, fa, g["cc"][:, CVALS.index(-0.5):CVALS.index(-0.5) + 1],
        g["cc"][:, CVALS.index(1597463007.0):CVALS.index(1597463007.0) + 1],
        ALU.mult, ALU.add))
    em.emit("vector", lambda e: e.tensor_copy(r.bitcast(I32), fa))
    for _ in range(iters):
        em.emit("vector", lambda e: e.tensor_mul(t, r, r))
        em.emit("vector", lambda e: e.tensor_mul(t, t, qcol))
        em.emit("vector", lambda e: e.tensor_scalar(t, t, g["cc"][:, 4:5], g["cc"][:, 5:6], ALU.mult, ALU.add))
        em.emit("vector", lambda e: e.tensor_mul(r, r, t))
    return em.emit("vector", lambda e: e.tensor_copy(out, r), inc="ve")


def emit_ln_stats(em, g, tiles=range(NT)):
    x = g["x_res"]
    bn = g["bn12"]
    mv = g["mv"]
    evs = []
    for t in tiles:
        xt = x[:, t * D:(t + 1) * D]
        w = [g["xready"][t]] if "xready" in g else []
        em.emit("vector", lambda e, xt=xt: e.bn_stats(bn[:, 0:6], xt[:, 0:512]), waits=w)
        em.emit("vector", lambda e, xt=xt: e.bn_stats(bn[:, 6:12], xt[:, 512:1024]))
        evs.append(em.emit("vector", lambda e, t=t: e.bn_aggr(
            mv[:, 2 * t:2 * t + 2], bn[:, 0:12]), inc="ve"))
    return evs


def emit_scale_chain_l0ffn(em, g, stats_evs):
    mv, sc = g["mv"], g["scratch8"]
    scol, bcol = g["scol"], g["bcol"]
    ra, rb = g["c16a"], g["c16b"]
    for t in range(NT):
        v = mv[:, 2 * t + 1:2 * t + 2]
        lv1 = sc[:, 0:1]
        lv0 = sc[:, 1:2]
        w = [stats_evs[t]]
        em.emit("scalar", lambda e, v=v, lv1=lv1: e.activation(
            lv1, v, AF.Ln, bias=FP(1e-5)), waits=w)
        em.emit("scalar", lambda e, v=v, lv0=lv0: e.activation(lv0, v, AF.Ln))
        em.emit("scalar", lambda e, lv1=lv1, t=t: e.activation(
            ra[:, t:t + 1], lv1, AF.Exp, scale=FP(-0.5)))
        se = em.emit("scalar", lambda e, lv0=lv0, t=t: e.activation(
            rb[:, t:t + 1], lv0, AF.Exp, scale=FP(0.5)),
            inc="se" if t == NT - 1 else None)
    for t in range(NT):
        m = mv[:, 2 * t:2 * t + 1]
        em.emit("vector", lambda e, t=t: e.tensor_mul(
            sc[:, 4:5], rb[:, t:t + 1], ra[:, t:t + 1]), waits=[se])
        em.emit("vector", lambda e: e.tensor_scalar_add(sc[:, 4:5], sc[:, 4:5], cval(g, 1e-8)))
        em.emit("vector", lambda e: e.reciprocal(sc[:, 6:7], sc[:, 4:5]))
        em.emit("vector", lambda e, t=t: e.tensor_mul(
            scol[:, t:t + 1], ra[:, t:t + 1], sc[:, 6:7]))
        em.emit("vector", lambda e, m=m, t=t: e.tensor_mul(
            sc[:, 7:8], m, scol[:, t:t + 1]))
        em.emit("vector", lambda e, t=t: e.tensor_scalar_mul(
            bcol[:, t:t + 1], sc[:, 7:8], cval(g, -1.0)), inc="ve")
    return em.ev("ve")


def emit_scale_chain_ln(em, g, stats_evs, extra=None):
    mv, sc = g["mv"], g["scratch8"]
    scol, bcol = g["scol"], g["bcol"]
    ra, rb = g["c16a"], g["c16b"]
    for t in range(NT):
        v = mv[:, 2 * t + 1:2 * t + 2]
        lv1 = sc[:, 0:1]
        w = [stats_evs[t]]
        em.emit("scalar", lambda e, v=v, lv1=lv1: e.activation(
            lv1, v, AF.Ln, bias=FP(1e-5)), waits=w)
        em.emit("scalar", lambda e, lv1=lv1, t=t: e.activation(
            ra[:, t:t + 1], lv1, AF.Exp, scale=FP(-0.5)))
        if extra == "l1attn":
            lv0 = sc[:, 1:2]
            em.emit("scalar", lambda e, v=v, lv0=lv0: e.activation(lv0, v, AF.Ln))
            em.emit("scalar", lambda e, lv0=lv0, t=t: e.activation(
                rb[:, t:t + 1], lv0, AF.Exp, scale=FP(0.5)))
    se = em.engine_done_ev("scalar")
    for t in range(NT):
        m = mv[:, 2 * t:2 * t + 1]
        if extra == "l1attn":
            em.emit("vector", lambda e, t=t: e.tensor_mul(
                sc[:, 4:5], rb[:, t:t + 1], ra[:, t:t + 1]), waits=[se])
            em.emit("vector", lambda e: e.tensor_scalar_max(
                sc[:, 4:5], sc[:, 4:5], cval(g, 1e-12 / 32.0)))
            em.emit("vector", lambda e: e.reciprocal(sc[:, 5:6], sc[:, 4:5]))
            em.emit("vector", lambda e, t=t: e.tensor_mul(
                scol[:, t:t + 1], ra[:, t:t + 1], sc[:, 5:6]))
        else:
            em.emit("vector", lambda e, t=t: e.tensor_copy(
                scol[:, t:t + 1], ra[:, t:t + 1]), waits=[se])
        em.emit("vector", lambda e, m=m, t=t: e.tensor_mul(
            sc[:, 7:8], m, scol[:, t:t + 1]))
        em.emit("vector", lambda e, t=t: e.tensor_scalar_mul(
            bcol[:, t:t + 1], sc[:, 7:8], cval(g, -1.0)), inc="ve")
    return em.ev("ve")


def emit_xn_tile(em, g, t, xnf, par_ev=None, gamma=None, dc_major=False):
    x = g["x_res"]
    xt_bf = g["xt_bf"]
    scol, bcol = g["scol"], g["bcol"]
    ps0, ps1 = g["ps_tr"]
    ident = g["ident"]
    ve = em.emit("vector", lambda e: e.tensor_scalar(
        xt_bf[:, :], x[:, t * D:(t + 1) * D], scol[:, t:t + 1], bcol[:, t:t + 1],
        ALU.mult, ALU.add), waits=[par_ev] if par_ev else [], inc="ve")
    pe_evs = []
    for half in range(2):
        ps = (ps0, ps1)[half]
        for j in range(4):
            dc = half * 4 + j
            pe = em.emit("tensor", lambda e, ps=ps, j=j, dc=dc: e.transpose(
                ps[:, j * 128:(j + 1) * 128], xt_bf[:, dc * 128:(dc + 1) * 128],
                ident[:, :]),
                waits=[ve, g["ps_tr_free"][half]],
                inc="pe" if j == 3 else None)
        if dc_major:
            dst = xnf.rearrange("p (dc n) -> p dc n", dc=8)[
                :, half * 4:(half + 1) * 4, t * 128:(t + 1) * 128]
            veo = em.emit("vector", lambda e, ps=ps, dst=dst: e.tensor_copy(
                dst, ps[:, :].rearrange("p (c n) -> p c n", c=4)),
                waits=[pe], inc="ve")
        elif gamma is None:
            veo = em.emit("vector", lambda e, ps=ps, half=half: e.tensor_copy(
                xnf[:, half * 512:(half + 1) * 512], ps[:, :]), waits=[pe], inc="ve")
        else:
            gcol, gev = gamma
            for j in range(4):
                dc = half * 4 + j
                veo = em.emit("vector", lambda e, ps=ps, j=j, dc=dc: e.tensor_scalar(
                    xnf[:, dc * 128:(dc + 1) * 128], ps[:, j * 128:(j + 1) * 128],
                    gcol[:, dc:dc + 1], None, ALU.mult),
                    waits=[pe, gev], inc="ve")
        g["ps_tr_free"][half] = veo
        pe_evs.append(veo)
    return pe_evs[-1]


def emit_l0ffn_prep(em, g, write_scratch=True):
    acc = g["tern_acc"]
    th_ps = g["ps_b"]
    stgs = [g["stg_w"][:, 0:256], g["stg_w"][:, 256:512],
            g["stg_w2"][:, 0:256], g["stg_w2"][:, 256:512]]
    queues = ["sync", "scalar", "sync", "scalar"]
    for wi, (src, n_chunks, res, scr) in enumerate([
            (g["in_w_up"], 8, g["wup_sb"], g["scr_wup"]),
            (g["in_w_dn"], 32, g["wdn_sb"], g["scr_wdn"])]):
        ncols = FF0 if wi == 0 else D
        PW = 256
        npieces = ncols // PW
        pieces = [(c, p) for c in range(n_chunks) for p in range(npieces)]
        mz = em.emit("vector", lambda e: e.memset(acc[:, 0:2], 0.0), inc="ve")
        ve = mz
        for i, (c, p) in enumerate(pieces):
            par = i % 4
            dsem = em.sem(f"dW{wi}{par}")
            sl = stgs[par]
            ld = em.dma(queues[par], sl,
                        src[c * 128:(c + 1) * 128, p * PW:(p + 1) * PW],
                        waits=[g["tern_stg_free"][par]], inc=dsem)
            em.emit("vector", lambda e, sl=sl: e.tensor_reduce(
                acc[:, 1:2], sl, axis=AX.X, op=ALU.add,
                apply_absolute_value=True), waits=[ld])
            ve = em.emit("vector", lambda e: e.tensor_add(
                acc[:, 0:1], acc[:, 0:1], acc[:, 1:2]), inc="ve")
            g["tern_stg_free"][par] = ve
        pe = em.emit("tensor", lambda e: e.matmul(
            th_ps[0:1, 0:1], acc[:, 0:1], g["ones_col_f"][:, :],
            start=True, stop=True), waits=[ve, g["ps_b_free"]], inc="pe")
        vth = em.emit("vector", lambda e: e.tensor_scalar_mul(
            g["thr_row"][0:1, 0:1], th_ps[0:1, 0:1],
            cval(g, 0.5 / (8 * 128 * FF0))[0:1, :]), waits=[pe], inc="ve")
        pe2 = em.emit("tensor", lambda e: e.matmul(
            th_ps[:, 4:5], g["ones_row_f"][0:1, :], g["thr_row"][0:1, 0:1],
            start=True, stop=True), waits=[vth], inc="pe")
        em.emit("vector", lambda e: e.tensor_copy(
            g["thr_col"][:, 0:1], th_ps[:, 4:5]), waits=[pe2])
        vcol = em.emit("vector", lambda e: e.tensor_scalar_mul(
            g["nthr_col"][:, 0:1], g["thr_col"][:, 0:1], cval(g, -1.0)), inc="ve")
        g["ps_b_free"] = vcol
        for i, (c, p) in enumerate(pieces):
            par = i % 4
            dsem = em.sem(f"dW{wi}{par}")
            sl = stgs[par]
            ld = em.dma(queues[par], sl,
                        src[c * 128:(c + 1) * 128, p * PW:(p + 1) * PW],
                        waits=[g["tern_stg_free"][par]], inc=dsem)
            sgn = g["tern_sgn"][:, par * PW:par * PW + PW]
            gtb = g["tern_gt"][:, par * PW:par * PW + PW]
            em.emit("vector", lambda e, sl=sl, sgn=sgn: e.tensor_single_scalar(
                sgn, sl, g["thr_col"][:, 0:1], ALU.is_gt),
                waits=[ld, vcol, g["tern_sgn_free"][par]])
            em.emit("vector", lambda e, sl=sl, gtb=gtb: e.tensor_single_scalar(
                gtb, sl, g["nthr_col"][:, 0:1], ALU.is_lt))
            dst = res[:, c * ncols + p * PW: c * ncols + p * PW + PW]
            ve = em.emit("vector", lambda e, dst=dst, sgn=sgn, gtb=gtb: e.tensor_sub(
                dst, sgn, gtb), inc="ve")
            g["tern_sgn_free"][par] = ve
            g["tern_stg_free"][par] = ve
            if write_scratch:
                em.dma("gpsimd", scr[c * 128:(c + 1) * 128, p * PW:(p + 1) * PW],
                       dst, waits=[ve], inc=em.sem("dWS"))
    return em.ev("ve")


def emit_l0ffn(em, g, do_prep, from_scratch):
    nc = g["nc"]
    if DBG["stats"]:
        stats = emit_ln_stats(em, g)
        sev = emit_scale_chain_l0ffn(em, g, stats)
    if do_prep and DBG["prep"]:
        emit_l0ffn_prep(em, g, write_scratch=g["write_scratch"])
    if from_scratch:
        dW = em.sem("dWL")
        for c in range(8):
            em.dma("sync", g["wup_sb"][:, c * FF0:(c + 1) * FF0],
                   g["scr_wup"][c * 128:(c + 1) * 128, :], inc=dW)
        for c in range(32):
            em.dma("sync", g["wdn_sb"][:, c * D:(c + 1) * D],
                   g["scr_wdn"][c * 128:(c + 1) * 128, :], inc=dW)
        wready = em.ev(("dWL"))
    else:
        wready = em.ev("ve")
    wup, wdn = g["wup_sb"], g["wdn_sb"]
    ps_up = g["ps_up"]
    ps_y = g["ps_y"]
    xnf2 = g["xnf2"]
    hbf = g["hbf"]
    htf = g["htf"]
    qacc = g["qacc"]
    sc = g["scratch8"]

    xnf_free = [em.ev("pe"), em.ev("pe")]
    gelu_done = [None, None]
    sumsq_done = [None, None]
    htf_free = [em.ev("pe"), em.ev("pe")]
    epi_ev = [None, None]
    for t in range(DBG["main_tiles"]):
        par = t % 2
        xnf = xnf2[par]
        xe = emit_xn_tile(em, g, t, xnf, par_ev=xnf_free[par])
        sumsq_evs = []
        for fc in range(DBG["fcs"]):
            fp = fc % 2
            for dc in range(8):
                w = [xe, wready]
                if gelu_done[fp] is not None and dc == 0:
                    w.append(gelu_done[fp])
                pe_up = em.emit("tensor", lambda e, fp=fp, dc=dc, fc=fc, xnf=xnf: e.matmul(
                    ps_up[fp][:, :], xnf[:, dc * 128:(dc + 1) * 128],
                    wup[:, dc * FF0 + fc * 512: dc * FF0 + fc * 512 + 512],
                    start=(dc == 0), stop=(dc == 7)), waits=w,
                    inc="pe" if dc == 7 else None)
            gw = [pe_up]
            if sumsq_done[fp] is not None:
                gw.append(sumsq_done[fp])
            ge = em.emit("scalar", lambda e, fp=fp: e.activation(
                hbf[fp][:, :], ps_up[fp][:, :], AF.Gelu), waits=gw, inc="se")
            gelu_done[fp] = ge
            sq = em.emit("vector", lambda e, fp=fp, fc=fc: e.scalar_tensor_tensor(
                htf[fp][:, :], hbf[fp][:, :], cval(g, 1.0), hbf[fp][:, :], ALU.mult, ALU.mult,
                accum_out=qacc[:, fc:fc + 1]), waits=[ge], inc="ve")
            sumsq_evs.append(sq)
            sumsq_done[fp] = sq
            for j in range(4):
                pe_t = em.emit("tensor", lambda e, fp=fp, j=j: e.transpose(
                    g["ps_tr"][fp][:, j * 128:(j + 1) * 128],
                    hbf[fp][:, j * 128:(j + 1) * 128], g["ident"][:, :]),
                    waits=[ge, g["ps_tr_free"][fp]],
                    inc="pe" if j == 3 else None)
            ve_t = em.emit("vector", lambda e, fp=fp: e.tensor_copy(
                htf[fp][:, :], g["ps_tr"][fp][:, :]),
                waits=[pe_t, htf_free[fp]], inc="ve")
            g["ps_tr_free"][fp] = ve_t
            for fs in range(4):
                fcc = fc * 4 + fs
                for nh in range(2):
                    w = [ve_t, wready]
                    if fc == 0 and fs == 0 and epi_ev[par] is not None:
                        w.append(epi_ev[par])
                    last = (fc == DBG["fcs"] - 1 and fs == 3 and nh == 1)
                    mm = em.emit("tensor",
                            lambda e, par=par, fs=fs, nh=nh, fcc=fcc, fc=fc: e.matmul(
                                ps_y[par][:, nh * 512:(nh + 1) * 512],
                                htf[fc % 2][:, fs * 128:(fs + 1) * 128],
                                wdn[:, fcc * D + nh * 512: fcc * D + nh * 512 + 512],
                                start=(fc == 0 and fs == 0),
                                stop=(fc == DBG["fcs"] - 1 and fs == 3)), waits=w,
                            inc="pe" if last else None)
                    if last:
                        pe_dn = mm
        htf_free = [pe_dn, pe_dn]
        xnf_free[par] = pe_dn
        if not DBG["epi"]:
            epi_ev[par] = pe_dn
            continue
        em.emit("vector", lambda e: e.tensor_reduce(
            sc[:, 0:1], qacc[:, :], axis=AX.X, op=ALU.add), waits=sumsq_evs[-1:])
        em.emit("vector", lambda e: e.tensor_scalar_mul(sc[:, 0:1], sc[:, 0:1], cval(g, 1.0 / FF0)))
        em.emit("vector", lambda e: e.tensor_scalar_max(sc[:, 0:1], sc[:, 0:1], cval(g, 1e-30)))
        if DBG["newton"]:
            rsqrt_newton(em, g, sc[:, 1:2], sc[:, 0:1], g["nsc"])
        else:
            em.emit("vector", lambda e: e.tensor_copy(sc[:, 1:2], sc[:, 0:1]))
        em.emit("vector", lambda e: e.tensor_mul(sc[:, 2:3], sc[:, 0:1], sc[:, 1:2]))
        em.emit("vector", lambda e: e.tensor_scalar_add(sc[:, 2:3], sc[:, 2:3], cval(g, 1e-8)))
        em.emit("vector", lambda e: e.reciprocal(sc[:, 3:4], sc[:, 2:3]))
        em.emit("vector", lambda e: e.tensor_scalar_mul(sc[:, 3:4], sc[:, 3:4], cval(g, 0.5)))
        ep = em.emit("vector", lambda e, t=t, par=par: e.scalar_tensor_tensor(
            g["x_res"][:, t * D:(t + 1) * D], ps_y[par][:, :], sc[:, 3:4],
            g["x_res"][:, t * D:(t + 1) * D], ALU.mult, ALU.add),
            waits=[pe_dn], inc="ve")
        epi_ev[par] = ep
    if DBG.get("dump"):
        lastve = em.ev("ve")
        em.dma("gpsimd", g["dbg1"], wup[:, 0:4096], waits=[lastve], inc="dDB")
        em.dma("gpsimd", g["dbg2"], wdn[:, 0:4096], waits=[lastve], inc="dDB")
        em.dma("gpsimd", g["dbg3"], xnf2[(DBG["main_tiles"] - 1) % 2][:, :],
               waits=[lastve], inc="dDB")
        em.dma("gpsimd", g["dbg4"][:, 0:8], qacc[:, :], waits=[lastve], inc="dDB", allow_slow_non_contiguous=True)
        em.dma("gpsimd", g["dbg4"][:, 8:9], g["thr_col"][:, 0:1], waits=[lastve], inc="dDB", allow_slow_non_contiguous=True)
        em.dma("gpsimd", g["dbg4"][:, 9:10], sc[:, 3:4], waits=[lastve], inc="dDB", allow_slow_non_contiguous=True)
        em.dma("gpsimd", g["dbg4"][:, 10:11], g["scol"][:, 0:1], waits=[lastve], inc="dDB", allow_slow_non_contiguous=True)
        em.dma("gpsimd", g["dbg4"][:, 11:12], g["bcol"][:, 0:1], waits=[lastve], inc="dDB", allow_slow_non_contiguous=True)
        em.dma("gpsimd", g["dbg4"][:, 12:14], g["mv"][:, 0:2], waits=[lastve], inc="dDB", allow_slow_non_contiguous=True)
        em.dma("gpsimd", g["dbg4"][:, 14:15], g["c16a"][:, 0:1], waits=[lastve], inc="dDB", allow_slow_non_contiguous=True)
        em.dma("gpsimd", g["dbg4"][:, 15:16], g["c16b"][:, 0:1], waits=[lastve], inc="dDB", allow_slow_non_contiguous=True)
    return em.ev("ve")


CVALS = [0.0, 1.0, -1.0, 0.5, -0.5, 1.5, 1e-8, 1.0 / FF0, 1e-12 / 32.0,
         0.5 / (8 * 128 * FF0), 1.0 / 127.0, 0.5 / (127.0 * 127.0),
         1597463007.0, 1e-30, 8.0, 1e-12, 0.125, 32.0, 0.03125]


def cval(g, v):
    i = CVALS.index(v)
    return g["cc"][:, i:i + 1]


def emit_l1ffn(em, g):
    stats = emit_ln_stats(em, g)
    emit_scale_chain_ln(em, g, stats)
    xnf_all = g["xnf_all"]
    wg_sb, wu_sb, wd_sb = g["wg_sb"], g["wu_sb"], g["wd_sb"]
    ps_g, ps_u = g["ps_gu"][0], g["ps_gu"][1]
    ps_y = g["ps_y"]
    hbf = g["hbf"]
    htf = g["htf"]
    HH = FF1 // 2
    xn_evs = []
    for t in range(NT):
        xn_evs.append(emit_xn_tile(em, g, t, xnf_all[:, t * D:(t + 1) * D]))
    epi = None
    gu_done = [None, None]
    dn_done = None
    for hh in range(2):
        dW = em.sem(f"dL1{hh}")
        wfree = [g.get("l1w_free")] if g.get("l1w_free") else []
        for dc in range(8):
            em.dma("gpsimd", wg_sb[:, dc * HH:(dc + 1) * HH],
                   g["in_wg"][dc * 128:(dc + 1) * 128, hh * HH:(hh + 1) * HH],
                   waits=wfree, inc=dW)
            em.dma("gpsimd", wu_sb[:, dc * HH:(dc + 1) * HH],
                   g["in_wu"][dc * 128:(dc + 1) * 128, hh * HH:(hh + 1) * HH],
                   waits=wfree, inc=dW)
        for hc in range(12):
            em.dma("gpsimd", wd_sb[:, hc * D:(hc + 1) * D],
                   g["in_wd"][hh * HH + hc * 128: hh * HH + (hc + 1) * 128, :],
                   waits=wfree, inc=dW)
        wready = em.ev(dW)
        for t in range(NT):
            xnf = xnf_all[:, t * D:(t + 1) * D]
            for hc in range(3):
                hp = hc % 2
                for dc in range(8):
                    w = [xn_evs[t], wready]
                    if gu_done[hp] is not None and dc == 0:
                        w.append(gu_done[hp])
                    em.emit("tensor", lambda e, hp=hp, dc=dc, hc=hc, xnf=xnf: e.matmul(
                        ps_g[hp][:, :], xnf[:, dc * 128:(dc + 1) * 128],
                        wg_sb[:, dc * HH + hc * 512: dc * HH + hc * 512 + 512],
                        start=(dc == 0), stop=(dc == 7)), waits=w)
                for dc in range(8):
                    last = dc == 7
                    mm = em.emit("tensor", lambda e, hp=hp, dc=dc, hc=hc, xnf=xnf: e.matmul(
                        ps_u[hp][:, :], xnf[:, dc * 128:(dc + 1) * 128],
                        wu_sb[:, dc * HH + hc * 512: dc * HH + hc * 512 + 512],
                        start=(dc == 0), stop=(dc == 7)),
                        waits=[xn_evs[t], wready],
                        inc="pe" if last else None)
                pe_gu = mm
                ge = em.emit("scalar", lambda e, hp=hp: e.activation(
                    hbf[hp][:, :], ps_g[hp][:, :], AF.Silu, scale=FP(1.0 / 127.0)),
                    waits=[pe_gu])
                mu = em.emit("vector", lambda e, hp=hp: e.tensor_tensor(
                    hbf[hp][:, :], hbf[hp][:, :], ps_u[hp][:, :], ALU.mult),
                    waits=[ge])
                gu_done[hp] = mu
                for j in range(4):
                    pe_t = em.emit("tensor", lambda e, hp=hp, j=j: e.transpose(
                        g["ps_tr"][hp][:, j * 128:(j + 1) * 128],
                        hbf[hp][:, j * 128:(j + 1) * 128], g["ident"][:, :]),
                        waits=[mu, g["ps_tr_free"][hp]],
                        inc="pe" if j == 3 else None)
                ve_t = em.emit("vector", lambda e, hp=hp: e.tensor_copy(
                    htf[hp][:, :], g["ps_tr"][hp][:, :]), waits=[pe_t])
                g["ps_tr_free"][hp] = ve_t
                for fs in range(4):
                    hcc = hc * 4 + fs
                    for nh in range(2):
                        w = [ve_t, wready]
                        if hc == 0 and fs == 0 and epi is not None:
                            w.append(epi)
                        last = (hc == 2 and fs == 3 and nh == 1)
                        mm = em.emit("tensor",
                                lambda e, fs=fs, nh=nh, hcc=hcc, hc=hc, hp=hp: e.matmul(
                                    ps_y[:, nh * 512:(nh + 1) * 512],
                                    htf[hp][:, fs * 128:(fs + 1) * 128],
                                    wd_sb[:, hcc * D + nh * 512:
                                          hcc * D + nh * 512 + 512],
                                    start=(hc == 0 and fs == 0),
                                    stop=(hc == 2 and fs == 3)), waits=w,
                                inc="pe" if last else None)
                        if last:
                            dn_done = mm
            epi = em.emit("vector", lambda e, t=t: e.scalar_tensor_tensor(
                g["x_res"][:, t * D:(t + 1) * D], ps_y[:, :],
                cval(g, 0.5 / (127.0 * 127.0)),
                g["x_res"][:, t * D:(t + 1) * D], ALU.mult, ALU.add),
                waits=[dn_done])
        g["l1w_free"] = em.engine_done_ev("tensor")
    return epi


def emit_l1attn_alloc(g, em, psb, pps):
    g["ps_tr"] = (pps("ps_tr0", [128, 512], BF16), pps("ps_tr1", [128, 512], BF16))
    g["ps_tr_free"] = [em.ev("pe"), em.ev("pe")]
    g["ps_qk"] = pps("ps_qkA", [128, 1024])[:, :]
    g["ps_sim"] = (pps("ps_simA", [64, 512]), pps("ps_simB", [64, 512]))
    g["ps_ss"] = pps("ps_ss", [128, 512])[:, :]
    g["ps_pv"] = pps("ps_pv", [128, 512])[:, :]
    g["ps_ss_free"] = em.ev("pe")
    g["xnf_free"] = [em.ev("pe"), em.ev("pe")]
    g["wq_sb"] = psb("wq_sb", [128, 8 * 3072], BF16)
    g["wo_sb"] = psb("wo_sb", [128, 8 * D], BF16)
    g["v_fm"] = psb("v_fm", [128, 8 * 2048], BF16)
    g["o_f"] = psb("o_f", [128, 8 * 2048], BF16)[:, :]
    g["qksq"] = g["o_f"][:, 0:2048]
    g["qk_t"] = g["o_f"][:, 2048:4096]
    g["xnf2"] = (g["o_f"][:, 4096:5120], g["o_f"][:, 5120:6144])
    g["t16"] = psb("t16", [16, 2])
    g["rowbuf"] = psb("rowbuf", [1, 16])
    g["expT_bc"] = psb("expT_bc", [64, 16])
    g["sqcol"] = psb("sqcol", [64, 16])
    g["skb"] = psb("skb", [64, 64])
    g["simf"] = psb("simf", [64, 64])
    g["mxc"] = psb("mxc", [64, 4])
    g["attn_bf"] = psb("attn_bf", [64, 64], BF16)
    g["attnT"] = psb("attnT", [128, 64], BF16)
    g["ones128_bf"] = psb("ones128_bf", [128, 128], BF16)
    g["identf"] = psb("identf", [128, 128])
    g["gcol"] = psb("gcol", [128, 8])
    g["ss_sb"] = psb("ss_sb", [1, 2048])


def emit_l1attn(em, g):
    stats = emit_ln_stats(em, g)
    emit_scale_chain_ln(em, g, stats)
    wq, wo = g["wq_sb"], g["wo_sb"]
    qk_t, qksq = g["qk_t"], g["qksq"]
    v_fm, o_f = g["v_fm"], g["o_f"]
    ps_qk = g["ps_qk"]
    ps_sim = g["ps_sim"]
    ps_ss = g["ps_ss"]
    ps_pv = g["ps_pv"]
    xnf2 = g["xnf2"]
    ones128 = g["ones128_bf"]
    em.emit("gpsimd", lambda e: e.memset(ones128[:, :], 1.0))
    em.emit("vector", lambda e: e.tensor_copy(g["identf"][:, :], g["ident"][:, :]),
            waits=[g["consts_ev"]])
    dG = em.sem("dG1")
    gld = em.dma("sync", g["gcol"][:, :], g["in_gamma"], inc=dG)
    dW = em.sem("dL1A")
    for dc in range(8):
        em.dma("gpsimd", wq[:, dc * 3072:(dc + 1) * 3072],
               g["in_wqkv"][dc * 128:(dc + 1) * 128, :], inc=dW)
    for c in range(8):
        em.dma("gpsimd", wo[:, c * D:(c + 1) * D],
               g["in_wo"][c * 128:(c + 1) * 128, :], inc=dW)
    wready = em.ev(dW)
    dT = em.sem("dT")
    tld = em.dma("sync", g["t16"][0:16, 0:1],
                 g["in_temp"].rearrange("(a b) -> a b", b=1), inc=dT)
    eT = em.emit("scalar", lambda e: e.activation(
        g["t16"][0:16, 1:2], g["t16"][0:16, 0:1], AF.Exp), waits=[tld])
    idf = em.engine_done_ev("vector")
    peT = em.emit("tensor", lambda e: e.transpose(
        ps_ss[0:1, 0:16], g["t16"][0:16, 1:2], g["identf"][0:16, 0:16]),
        waits=[eT, idf, g["ps_ss_free"]], inc="pe")
    veT = em.emit("vector", lambda e: e.tensor_copy(
        g["rowbuf"][0:1, 0:16], ps_ss[0:1, 0:16]), waits=[peT])
    peB = em.emit("tensor", lambda e: e.matmul(
        ps_ss[0:64, 16:32], g["ones_row_f"][0:1, 0:64], g["rowbuf"][0:1, 0:16],
        start=True, stop=True), waits=[veT], inc="pe")
    eBC = em.emit("vector", lambda e: e.tensor_copy(
        g["expT_bc"][:, :], ps_ss[0:64, 16:32]), waits=[peB])
    g["ps_ss_free"] = eBC

    sim_last = None
    qk_free = em.ev("pe")
    ss_free = eBC
    ss_add_ev = None
    for t in range(NT):
        par = t % 2
        xe = emit_xn_tile(em, g, t, xnf2[par], par_ev=g["xnf_free"][par],
                          gamma=(g["gcol"], (dG, 16)))
        for gq in range(2):
            for nh in range(2):
                for dc in range(8):
                    w = [xe, wready, qk_free]
                    last = dc == 7 and nh == 1
                    mm = em.emit("tensor", lambda e, gq=gq, nh=nh, dc=dc, par=par: e.matmul(
                        ps_qk[:, nh * 512:(nh + 1) * 512],
                        xnf2[par][:, dc * 128:(dc + 1) * 128],
                        wq[:, dc * 3072 + gq * 1024 + nh * 512:
                           dc * 3072 + gq * 1024 + nh * 512 + 512],
                        start=(dc == 0), stop=(dc == 7)), waits=w,
                        inc="pe" if last else None)
            ev_qk = em.emit("vector", lambda e, gq=gq: e.tensor_copy(
                qk_t[:, gq * 1024:(gq + 1) * 1024], ps_qk[:, :]), waits=[mm])
            qk_free = ev_qk
            em.emit("vector", lambda e, gq=gq: e.scalar_tensor_tensor(
                qksq[:, gq * 1024:(gq + 1) * 1024], qk_t[:, gq * 1024:(gq + 1) * 1024],
                cval(g, 1.0), qk_t[:, gq * 1024:(gq + 1) * 1024], ALU.mult, ALU.mult))
        sq_ev = em.engine_done_ev("vector")
        for h in range(H):
            b = h // 8
            last = h % 8 == 7 and b == 1
            mm = em.emit("tensor", lambda e, h=h, b=b, t=t: e.matmul(
                ps_sim[b][0:64, (h % 8) * 64:(h % 8) * 64 + 64],
                qk_t[:, h * 64:h * 64 + 64],
                qk_t[:, 1024 + h * 64:1024 + h * 64 + 64],
                start=(t == 0 and h % 8 == 0), stop=(t == NT - 1),
                skip_group_check=True), waits=[sq_ev],
                inc="pe" if last and t == NT - 1 else None)
            if last and t == NT - 1:
                sim_last = mm
        for p in range(4):
            pm = em.emit("tensor", lambda e, p=p: e.matmul(
                ps_ss[:, 0:512], ones128[:, :], qksq[:, p * 512:(p + 1) * 512],
                start=True, stop=True), waits=[sq_ev, ss_free], inc="pe")
            if t == 0:
                ae = em.emit("vector", lambda e, p=p: e.tensor_copy(
                    g["ss_sb"][0:1, p * 512:(p + 1) * 512], ps_ss[0:1, 0:512]),
                    waits=[pm])
            else:
                ae = em.emit("vector", lambda e, p=p: e.tensor_add(
                    g["ss_sb"][0:1, p * 512:(p + 1) * 512],
                    g["ss_sb"][0:1, p * 512:(p + 1) * 512], ps_ss[0:1, 0:512]),
                    waits=[pm])
            ss_free = ae
        ss_add_ev = ae
        for half in range(2):
            for ci in range(4):
                c = half * 4 + ci
                for dc in range(8):
                    w = [xe, wready, qk_free]
                    last = dc == 7 and ci == 3
                    mm = em.emit("tensor", lambda e, c=c, ci=ci, half=half, dc=dc, par=par: e.matmul(
                        ps_qk[:, half * 512 + ci * 128: half * 512 + ci * 128 + 128],
                        wq[:, dc * 3072 + 2048 + c * 128: dc * 3072 + 2048 + (c + 1) * 128],
                        xnf2[par][:, dc * 128:(dc + 1) * 128],
                        start=(dc == 0 and ci == 0),
                        stop=(dc == 7), skip_group_check=True), waits=w,
                        inc="pe" if last else None)
            vo = em.emit("vector", lambda e, half=half, t=t: e.tensor_copy(
                v_fm[:, :].rearrange("p (c n) -> p c n", c=8)
                [:, half * 4:(half + 1) * 4, t * 128:(t + 1) * 128],
                ps_qk[:, half * 512:(half + 1) * 512].rearrange(
                    "p (c n) -> p c n", c=4)), waits=[mm])
            qk_free = vo
        g["xnf_free"][par] = em.engine_done_ev("tensor")
    srow = g["ss_sb"]
    lnr = em.emit("scalar", lambda e: e.activation(
        srow[0:1, 0:2048], srow[0:1, 0:2048], AF.Ln), waits=[ss_add_ev])
    sqr = em.emit("scalar", lambda e: e.activation(
        srow[0:1, 0:2048], srow[0:1, 0:2048], AF.Exp, scale=FP(0.5)), waits=[lnr])
    em.emit("vector", lambda e: e.tensor_scalar_max(
        srow[0:1, 0:2048], srow[0:1, 0:2048], cval(g, 1e-12)[0:1, :]),
        waits=[sqr])
    em.emit("vector", lambda e: e.reciprocal(
        srow[0:1, 0:2048], srow[0:1, 0:2048]))
    for h in range(H):
        ve_now = em.ev("ve")
        pt = em.emit("tensor", lambda e, h=h: e.transpose(
            ps_ss[0:64, 32:33], g["ss_sb"][0:1, h * 64:(h + 1) * 64],
            g["identf"][0:1, 0:1]),
            waits=[(("ve"), ve_now[1]), g["ps_ss_free"]], inc="pe")
        ce = em.emit("vector", lambda e, h=h: e.tensor_copy(
            g["sqcol"][0:64, h:h + 1], ps_ss[0:64, 32:33]), waits=[pt])
        g["ps_ss_free"] = ce
    em.emit("vector", lambda e: e.tensor_mul(
        g["sqcol"][0:64, 0:16], g["sqcol"][0:64, 0:16], g["expT_bc"][:, :]))
    sqc_ev = em.emit("vector", lambda e: e.tensor_scalar_mul(
        g["sqcol"][0:64, 0:16], g["sqcol"][0:64, 0:16], cval(g, 8.0)[0:64, :]))
    pv_free = em.ev("ve")
    at_free = em.ev("ve")
    for h in range(H):
        b, col = h // 8, (h % 8) * 64
        sim_ap = ps_sim[b][0:64, col:col + 64]
        pb = em.emit("tensor", lambda e, h=h: e.matmul(
            ps_ss[0:64, 64:128], g["ones_row_f"][0:1, 0:64],
            g["ss_sb"][0:1, 1024 + h * 64:1024 + (h + 1) * 64],
            start=True, stop=True), waits=[sqc_ev, g["ps_ss_free"]], inc="pe")
        cb = em.emit("vector", lambda e: e.tensor_copy(
            g["skb"][0:64, 0:64], ps_ss[0:64, 64:128]), waits=[pb])
        g["ps_ss_free"] = cb
        em.emit("vector", lambda e, h=h, sim_ap=sim_ap: e.tensor_scalar(
            g["simf"][0:64, 0:64], sim_ap, g["sqcol"][0:64, h:h + 1], None,
            ALU.mult), waits=[sim_last, sqc_ev])
        em.emit("vector", lambda e: e.tensor_mul(
            g["simf"][0:64, 0:64], g["simf"][0:64, 0:64], g["skb"][0:64, 0:64]))
        mx = em.emit("vector", lambda e: e.reduce_max(
            g["mxc"][0:64, 0:1], g["simf"][0:64, 0:64], axis=AX.X, negate=True))
        ex = em.emit("scalar", lambda e: e.activation(
            g["simf"][0:64, 0:64], g["simf"][0:64, 0:64], AF.Exp,
            bias=g["mxc"][0:64, 0:1], accum_out=g["mxc"][0:64, 1:2]),
            waits=[mx])
        iv = em.emit("vector", lambda e: e.reciprocal(
            g["mxc"][0:64, 2:3], g["mxc"][0:64, 1:2]), waits=[ex])
        nm = em.emit("vector", lambda e: e.tensor_scalar(
            g["attn_bf"][0:64, 0:64], g["simf"][0:64, 0:64], g["mxc"][0:64, 2:3],
            None, ALU.mult), waits=[at_free])
        ro = (h % 2) * 64
        ttp = (0, 64) if ro else None
        ptr = em.emit("tensor", lambda e, ro=ro, ttp=ttp: e.transpose(
            g["ps_tr"][0][ro:ro + 64, 0:64], g["attn_bf"][0:64, 0:64],
            g["ident"][0:64, 0:64], tile_position=ttp),
            waits=[nm, g["ps_tr_free"][0]], inc="pe")
        ct = em.emit("vector", lambda e, ro=ro: e.tensor_copy(
            g["attnT"][ro:ro + 64, 0:64], g["ps_tr"][0][ro:ro + 64, 0:64]),
            waits=[ptr])
        g["ps_tr_free"][0] = ct
        at_free = ct
        for nchunk in range(4):
            tp = (64, 64) if ro else None
            pv = em.emit("tensor", lambda e, h=h, nchunk=nchunk, tp=tp, ro=ro: e.matmul(
                ps_pv[ro:ro + 64, 0:512],
                g["attnT"][ro:ro + 64, 0:64],
                v_fm[ro:ro + 64,
                     (h // 2) * 2048 + nchunk * 512:(h // 2) * 2048 + (nchunk + 1) * 512],
                start=True, stop=True, tile_position=tp),
                waits=[ct, pv_free], inc="pe")
            oe = em.emit("vector", lambda e, h=h, nchunk=nchunk, ro=ro: e.tensor_copy(
                o_f[ro:ro + 64,
                    (h // 2) * 2048 + nchunk * 512:(h // 2) * 2048 + (nchunk + 1) * 512],
                ps_pv[ro:ro + 64, 0:512]), waits=[pv])
            pv_free = oe
    of_ev = em.engine_done_ev("vector")
    epi = None
    for t in range(NT):
        for c in range(8):
            for nh in range(2):
                w = [of_ev, wready, qk_free]
                if epi is not None and c == 0 and nh == 0:
                    w.append(epi)
                last = c == 7 and nh == 1
                mm = em.emit("tensor", lambda e, c=c, nh=nh, t=t: e.matmul(
                    ps_qk[:, nh * 512:(nh + 1) * 512],
                    o_f[:, c * 2048 + t * 128: c * 2048 + (t + 1) * 128],
                    wo[:, c * D + nh * 512: c * D + nh * 512 + 512],
                    start=(c == 0), stop=(c == 7)), waits=w,
                    inc="pe" if last else None)
        epi = em.emit("vector", lambda e, t=t: e.scalar_tensor_tensor(
            g["x_res"][:, t * D:(t + 1) * D], ps_qk[:, :], cval(g, 1.0),
            g["x_res"][:, t * D:(t + 1) * D], ALU.mult, ALU.add), waits=[mm])
    return epi


def emit_l0attn_alloc(g, em, psb, pps):
    g["ps_S"] = pps("ps_S", [128, 2048])[:, :]
    g["ps_tr"] = (pps("ps_tr0", [128, 512], BF16), pps("ps_tr1", [128, 512], BF16))
    g["ps_tr_free"] = [em.ev("pe"), em.ev("pe")]
    g["ps_o"] = pps("ps_o", [128, 128])[:, :]
    g["ps_pr"] = pps("ps_pr", [128, 512])[:, :]
    g["wqkv_sb"] = psb("wqkv_sb", [128, 8 * 3072], BF16)
    g["xnf_dc"] = psb("xnf_dc", [128, NT * D], BF16)[:, :]
    g["q_sb"] = psb("q_sb", [128, 2048], BF16)
    g["k_sb"] = psb("k_sb", [128, 2048], BF16)
    g["v_sb"] = psb("v_sb", [128, 2048], BF16)
    g["embr_sb"] = psb("embr_sb", [128, 2048], BF16)
    g["PR_sb"] = psb("PR_sb", [128, 2048], BF16)
    g["A_sb"] = psb("A_sb", [128, 2048], BF16)
    g["At_sb"] = psb("At_sb", [128, 2048], BF16)
    g["Aadd"] = (psb("AaddA", [128, 2048], BF16), psb("AaddB", [128, 2048], BF16))
    g["o_st"] = psb("o_st", [128, 2048], BF16)
    g["o_ld"] = (psb("o_ldA", [128, 1024], BF16), psb("o_ldB", [128, 1024], BF16))
    g["negt"] = psb("negt", [128, 512], BF16)
    g["qb8"] = psb("qb8", [128, 8])
    g["obrow"] = psb("obrow", [1, 1024], BF16)
    g["mxl"] = psb("mxl", [128, 4])
    g["pr_pos"] = psb("pr_pos", [128, 2048], BF16)
    g["pr_sins"] = psb("pr_sins", [128, 1024], BF16)
    g["pr_stage"] = psb("pr_stage", [128, 512], BF16)
    g["wo0_sb"] = g["xnf_dc"][:, 0:8 * D]


def emit_l0attn(em, g, do_prep):
    stats = emit_ln_stats(em, g)
    emit_scale_chain_ln(em, g, stats)
    wq = g["wqkv_sb"]
    xnf = g["xnf_dc"]
    q_sb, k_sb, v_sb = g["q_sb"], g["k_sb"], g["v_sb"]
    ps_S, ps_pr, ps_o = g["ps_S"], g["ps_pr"], g["ps_o"]
    A_sb, At_sb, PR_sb = g["A_sb"], g["At_sb"], g["PR_sb"]
    Aadd = g["Aadd"]
    ident = g["ident"]
    CQ = float(C_QKV)
    R = 2560

    ng = em.emit("gpsimd", lambda e: e.memset(g["negt"][:, :], NEG))
    dW = em.sem("dL0A")
    for dc in range(8):
        em.dma("gpsimd", wq[:, dc * 3072:(dc + 1) * 3072],
               g["in_qkv"][dc * 128:(dc + 1) * 128, :], inc=dW)
    wready = em.ev(dW)
    qld = em.dma("sync", g["qb8"][:, :], g["in_qb"], inc=em.sem("dQB"))
    obld = em.dma("gpsimd", g["obrow"][0:1, :], g["in_ob"], inc=em.sem("dOB"))
    em.emit("vector", lambda e: e.tensor_scalar_mul(
        g["qb8"][:, :], g["qb8"][:, :], cval(g, 0.125)), waits=[qld])
    qbe = em.emit("vector", lambda e: e.tensor_scalar_mul(
        g["obrow"][0:1, :], g["obrow"][0:1, :], cval(g, 32.0)[0:1, :]),
        waits=[obld], inc="ve")

    dPD = em.sem("dPAD")
    for par in range(2):
        skw = g["skw"][par].rearrange("(p a) -> p a", a=R)
        em.dma("scalar", skw[:, 2048:2560],
               g["negt"][:, :].rearrange("p (a b) -> p a b", a=1)[:, 0, 0:512],
               waits=[ng], inc=dPD)
    pad_ev = em.ev(dPD)

    if do_prep:
        dE, dSin, dEW = em.sem("dEM"), em.sem("dSin"), em.sem("dEW")
        for fc in range(2):
            em.dma("gpsimd", g["pr_pos"][:, fc * 1024:(fc + 1) * 1024],
                   g["in_pos"][fc * 128:(fc + 1) * 128, :], inc=dE)
        posev = em.ev(dE)
        sin_free = em.ev("pe")
        st_free = em.ev("dEW")
        for jc in range(4):
            s2 = em.dma("gpsimd",
                        g["pr_sins"][:, :].rearrange("p (c n) -> p c n", c=2),
                        g["in_sins"].rearrange("(c p) n -> p c n", c=2)
                        [:, :, jc * 512:(jc + 1) * 512],
                        waits=[sin_free], inc=dSin)
            for hp in range(8):
                for fc in range(2):
                    mm = em.emit("tensor", lambda e, hp=hp, fc=fc: e.matmul(
                        ps_pr[:, :],
                        g["pr_pos"][:, fc * 1024 + hp * 128: fc * 1024 + (hp + 1) * 128],
                        g["pr_sins"][:, fc * 512:(fc + 1) * 512],
                        start=(fc == 0), stop=(fc == 1)),
                        waits=[s2, posev, g.get("pr_free")],
                        inc="pe" if fc == 1 else None)
                ev = em.emit("vector", lambda e: e.tensor_copy(
                    g["pr_stage"][:, :], ps_pr[:, :]),
                    waits=[mm, st_free])
                g["pr_free"] = ev
                st_free = em.dma("scalar",
                                 g["embr_s"].rearrange("(hp p n) -> hp p n", hp=8, p=128)
                                 [hp, :, jc * 512:(jc + 1) * 512],
                                 g["pr_stage"][:, :], waits=[ev], inc=dEW)
            sin_free = em.engine_done_ev("tensor")
        g["embr_ready"] = em.ev("dEW")

    xn_evs = []
    for t in range(NT):
        xn_evs.append(emit_xn_tile(em, g, t, xnf, dc_major=True))
    xn_all = xn_evs[-1]

    dEL = em.sem("dEL")
    dSW = [em.sem("dSW0"), em.sem("dSW1")]
    dSR = [em.sem("dSR0"), em.sem("dSR1")]
    dOS = em.sem("dOS")
    embr_free = em.ev("pe")
    pair_done = em.ev("pe")
    exp_prev = None
    aadd_used = [None, None]
    g["o_st_free"] = em.ev("dOS")
    for hp in range(DBG.get("l0a_pairs", 8)):
        eld = em.dma("sync", g["embr_sb"][:, :],
                     g["embr_s"].rearrange("(hp p n) -> hp p n", hp=8, p=128)[hp, :, :],
                     waits=[g["embr_ready"], embr_free], inc=dEL)
        pw = [wready, xn_all, pair_done]
        qk_ev = {}
        for which, base, scale in (("q", hp * 128, 0.125 * CQ),
                                   ("k", 1024 + hp * 128, CQ)):
            dst = q_sb if which == "q" else k_sb
            for tc in range(4):
                bank = (tc % 2) * 512
                w = list(pw) + [qk_ev.get((which, tc - 2)), qbe, exp_prev]
                for dc in range(8):
                    mm = em.emit("tensor",
                            lambda e, dc=dc, tc=tc, base=base, bank=bank: e.matmul(
                                ps_S[:, bank:bank + 512],
                                wq[:, dc * 3072 + base: dc * 3072 + base + 128],
                                xnf[:, dc * 2048 + tc * 512: dc * 2048 + (tc + 1) * 512],
                                start=(dc == 0), stop=(dc == 7),
                                skip_group_check=True),
                            waits=w, inc="pe" if dc == 7 else None)
                if which == "q":
                    ev = em.emit("scalar", lambda e, tc=tc, bank=bank, hp=hp, dst=dst: e.activation(
                        dst[:, tc * 512:(tc + 1) * 512], ps_S[:, bank:bank + 512],
                        AF.Identity, scale=FP(0.125 * CQ),
                        bias=g["qb8"][:, hp:hp + 1]), waits=[mm])
                else:
                    ev = em.emit("scalar", lambda e, tc=tc, bank=bank, dst=dst: e.activation(
                        dst[:, tc * 512:(tc + 1) * 512], ps_S[:, bank:bank + 512],
                        AF.Identity, scale=FP(CQ)), waits=[mm])
                qk_ev[(which, tc)] = ev
        for tg in range(4):
            bank = 1024 + (tg % 2) * 512
            w = list(pw) + [qk_ev.get(("v", tg - 2)), exp_prev]
            for ti in range(4):
                t = tg * 4 + ti
                for dc in range(8):
                    mm = em.emit("tensor",
                            lambda e, dc=dc, ti=ti, t=t, bank=bank, hp=hp: e.matmul(
                                ps_S[:, bank + ti * 128: bank + ti * 128 + 128],
                                xnf[:, dc * 2048 + t * 128: dc * 2048 + (t + 1) * 128],
                                wq[:, dc * 3072 + 2048 + hp * 128:
                                   dc * 3072 + 2048 + (hp + 1) * 128],
                                start=(dc == 0 and ti == 0), stop=(dc == 7),
                                skip_group_check=True),
                            waits=w, inc="pe" if (dc == 7 and ti == 3) else None)
            ev = em.emit("scalar", lambda e, tg=tg, bank=bank: e.activation(
                v_sb[:, tg * 512:(tg + 1) * 512], ps_S[:, bank:bank + 512],
                AF.Identity, scale=FP(CQ)), waits=[mm])
            qk_ev[("v", tg)] = ev
        qke = em.engine_done_ev("scalar")

        for hodd in range(DBG.get("l0a_heads", 2)):
            h = 2 * hp + hodd
            ro = hodd * 64
            rtp = (64, 0) if hodd else None

            def emit_pr(j, qke=qke, eld=eld, ro=ro, rtp=rtp, hp=hp):
                jmin = 1920 - 128 * j
                width = 2048 - jmin
                par = j % 2
                npc = (width + 511) // 512
                prev = None
                for pc in range(npc):
                    c0 = jmin + pc * 512
                    cw = min(512, 2048 - c0)
                    mm = em.emit("tensor", lambda e, j=j, c0=c0, cw=cw, ro=ro, rtp=rtp: e.matmul(
                        ps_pr[:, 0:cw],
                        q_sb[ro:ro + 64, j * 128:(j + 1) * 128],
                        g["embr_sb"][ro:ro + 64, c0:c0 + cw],
                        start=True, stop=True, tile_position=rtp),
                        waits=[qke, eld, g.get("pr_free")], inc="pe")
                    prev = em.emit("vector", lambda e, c0=c0, cw=cw: e.tensor_copy(
                        PR_sb[:, c0:c0 + cw], ps_pr[:, 0:cw]),
                        waits=[mm, g.get("prsb_free")])
                    g["pr_free"] = prev
                skw = g["skw"][par].rearrange("(p a) -> p a", a=R)
                wrev = em.dma("scalar", skw[:, jmin:2048], PR_sb[:, jmin:2048],
                              waits=[prev, pad_ev,
                                     g.get(f"skr_done{par}")], inc=dSW[par])
                g["prsb_free"] = wrev
                W = 128 * j + 128
                Wc = ((W + 511) // 512) * 512
                bview = g["skw"][par][0:128 * (R - 1)].rearrange("(p a) -> p a", a=R - 1)
                rdev = em.dma("sync", Aadd[par][:, 0:Wc],
                              bview[:, 2047 - 128 * j: 2047 - 128 * j + Wc],
                              waits=[em.ev(dSW[par]), aadd_used[par]], inc=dSR[par])
                g[f"skr_done{par}"] = em.ev(dSR[par])
                return rdev

            rd = {0: emit_pr(0), 1: emit_pr(1)}
            pend = None

            def do_tpv(i, nm):
                nkc = i + 1
                for kb in range((nkc + 3) // 4):
                    nj = min(4, nkc - kb * 4)
                    for jj in range(nj):
                        kc = kb * 4 + jj
                        pt = em.emit("tensor", lambda e, kb=kb, jj=jj, kc=kc: e.transpose(
                            g["ps_tr"][kb % 2][:, jj * 128:(jj + 1) * 128],
                            A_sb[:, kc * 128:(kc + 1) * 128], ident[:, :]),
                            waits=[nm, g["ps_tr_free"][kb % 2]],
                            inc="pe" if jj == nj - 1 else None)
                    te = em.emit("scalar", lambda e, kb=kb, nj=nj: e.copy(
                        At_sb[:, kb * 512:kb * 512 + nj * 128],
                        g["ps_tr"][kb % 2][:, 0:nj * 128]), waits=[pt])
                    g["ps_tr_free"][kb % 2] = te
                g["asb_free"] = em.engine_done_ev("tensor")
                ate = em.engine_done_ev("scalar")
                for kc in range(nkc):
                    otp = (0, 64) if hodd else None
                    mm = em.emit("tensor", lambda e, kc=kc, ro=ro, otp=otp, i=i: e.matmul(
                        ps_o[ro:ro + 64, 0:128],
                        v_sb[:, kc * 128 + ro: kc * 128 + ro + 64],
                        At_sb[:, kc * 128:(kc + 1) * 128],
                        start=(kc == 0), stop=(kc == nkc - 1),
                        skip_group_check=True, tile_position=otp),
                        waits=[ate, g.get("po_free")], inc="pe" if kc == nkc - 1 else None)
                oe = em.emit("vector", lambda e, ro=ro, i=i: e.tensor_copy(
                    g["o_st"][ro:ro + 64, i * 128:(i + 1) * 128],
                    ps_o[ro:ro + 64, 0:128]), waits=[mm, g["o_st_free"]])
                g["po_free"] = oe

            for i in range(DBG.get("l0a_tiles", NT)):
                W = 128 * i + 128
                Wc = ((W + 511) // 512) * 512
                nch = Wc // 512
                par = i % 2
                for c in range(nch):
                    em.emit("tensor", lambda e, i=i, c=c, ro=ro, rtp=rtp: e.matmul(
                        ps_S[:, c * 512:(c + 1) * 512],
                        q_sb[ro:ro + 64, i * 128:(i + 1) * 128],
                        k_sb[ro:ro + 64, c * 512:(c + 1) * 512],
                        start=True, stop=False, skip_group_check=True,
                        tile_position=rtp),
                        waits=[qke, exp_prev])
                    last = c == nch - 1
                    mm = em.emit("tensor", lambda e, i=i, c=c, par=par: e.matmul(
                        ps_S[:, c * 512:(c + 1) * 512], ident[:, :],
                        Aadd[par][:, c * 512:(c + 1) * 512],
                        start=False, stop=True, skip_group_check=True),
                        waits=[rd[i]], inc="pe" if last else None)
                aadd_used[par] = mm
                if i + 2 < DBG.get("l0a_tiles", NT):
                    rd[i + 2] = emit_pr(i + 2)
                if pend is not None:
                    do_tpv(*pend)
                mx = em.emit("vector", lambda e, Wc=Wc: e.reduce_max(
                    g["mxl"][:, 0:1], ps_S[:, 0:Wc], axis=AX.X, negate=True),
                    waits=[mm])
                ex = em.emit("scalar", lambda e, Wc=Wc: e.activation(
                    A_sb[:, 0:Wc], ps_S[:, 0:Wc], AF.Exp,
                    bias=g["mxl"][:, 0:1], accum_out=g["mxl"][:, 1:2]),
                    waits=[mx, g.get("asb_free")])
                exp_prev = ex
                em.emit("vector", lambda e: e.reciprocal(
                    g["mxl"][:, 2:3], g["mxl"][:, 1:2]), waits=[ex])
                nm = em.emit("vector", lambda e, Wc=Wc: e.tensor_scalar(
                    A_sb[:, 0:Wc], A_sb[:, 0:Wc], g["mxl"][:, 2:3], None,
                    ALU.mult))
                pend = (i, nm)
            do_tpv(*pend)
        ose = em.dma("scalar",
                     g["o_s"].rearrange("(hp p n) -> hp p n", hp=8, p=128)[hp, :, :],
                     g["o_st"][:, :], waits=[em.ev("ve")], inc=dOS)
        g["o_st_free"] = ose
        embr_free = em.engine_done_ev("tensor")
        pair_done = embr_free
    if DBG.get("dump"):
        lastall = [em.engine_done_ev(e) for e in ("vector", "scalar", "tensor")]
        em.dma("gpsimd", g["dbg1"][:, 0:2048], q_sb[:, :], waits=lastall, inc="dDB")
        em.dma("gpsimd", g["dbg1"][:, 2048:4096], k_sb[:, :], waits=lastall, inc="dDB")
        em.dma("gpsimd", g["dbg2"][:, 0:2048], Aadd[0][:, :], waits=lastall, inc="dDB")
        em.dma("gpsimd", g["dbg2"][:, 2048:4096], A_sb[:, :], waits=lastall, inc="dDB")
        em.dma("gpsimd", g["dbg3"][:, 0:1024], g["o_st"][:, 0:1024], waits=lastall, inc="dDB")
        em.dma("gpsimd", g["dbg4"][:, 0:4], g["mxl"][:, :], waits=lastall, inc="dDB",
               allow_slow_non_contiguous=True)
        em.dma("gpsimd", g["dbg3"][:, 1024:3072], g["embr_sb"][:, :], waits=lastall, inc="dDB")
        em.dma("gpsimd", g["dbg5"],
               g["skw"][0].rearrange("(p a) -> p a", a=2560),
               waits=lastall, inc="dDB")
    dWO = em.sem("dWO")
    for c in range(8):
        em.dma("gpsimd", g["wo0_sb"][:, c * D:(c + 1) * D],
               g["in_wo0"][c * 128:(c + 1) * 128, :],
               waits=[pair_done], inc=dWO)
    wo_ready = em.ev(dWO)
    dOL = em.sem("dOL")
    os_view = g["o_s"].rearrange("(hp p n) -> hp p n", hp=8, p=128).transpose([1, 0, 2])
    epi = None
    old_ev = [None, None]
    for t in range(NT):
        par = t % 2
        lw = [em.ev(dOS), old_ev[par]]
        lde = em.dma("sync", g["o_ld"][par][:, :].rearrange("p (hp n) -> p hp n", hp=8),
                     os_view[:, :, t * 128:(t + 1) * 128], waits=lw, inc=dOL)
        for c in range(8):
            for nh in range(2):
                w = [lde, wo_ready, qbe, exp_prev]
                if epi is not None and c == 0 and nh == 0:
                    w.append(epi)
                em.emit("tensor", lambda e, c=c, nh=nh, par=par, t=t: e.matmul(
                    ps_S[:, par * 1024 + nh * 512: par * 1024 + nh * 512 + 512],
                    g["o_ld"][par][:, c * 128:(c + 1) * 128],
                    g["wo0_sb"][:, c * D + nh * 512: c * D + nh * 512 + 512],
                    start=(c == 0), stop=False, skip_group_check=True), waits=w)
        for nh in range(2):
            mm = em.emit("tensor", lambda e, nh=nh, par=par: e.matmul(
                ps_S[:, par * 1024 + nh * 512: par * 1024 + nh * 512 + 512],
                g["ones_row_bf"][0:1, :],
                g["obrow"][0:1, nh * 512:(nh + 1) * 512],
                start=False, stop=True, skip_group_check=True),
                waits=[qbe], inc="pe" if nh == 1 else None)
        old_ev[par] = mm
        epi = em.emit("vector", lambda e, t=t, par=par: e.scalar_tensor_tensor(
            g["x_res"][:, t * D:(t + 1) * D], ps_S[:, par * 1024: par * 1024 + 1024],
            cval(g, 0.03125), g["x_res"][:, t * D:(t + 1) * D],
            ALU.mult, ALU.add), waits=[mm])
    return epi


def build_consts(em, g):
    for i, v in enumerate(CVALS):
        em.emit("gpsimd", lambda e, i=i, v=v: e.memset(g["cc"][:, i:i + 1], v))
    mz = em.emit("gpsimd", lambda e: e.memset(g["ident"][:, :], 0.0), inc="po")
    em.emit("gpsimd", lambda e: e.affine_select(
        out=g["ident"][:, :], in_=g["ident"][:, :], compare_op=ALU.not_equal,
        fill=1.0, base=0, pattern=[[-1, 128]], channel_multiplier=1), waits=[mz])
    em.emit("gpsimd", lambda e: e.memset(g["ones_col_bf"][:, :], 1.0))
    em.emit("gpsimd", lambda e: e.memset(g["ones_row_bf"][:, :], 1.0))
    em.emit("gpsimd", lambda e: e.memset(g["ones_col_f"][:, :], 1.0))
    em.emit("gpsimd", lambda e: e.memset(g["ones_row_f"][:, :], 1.0), inc="po")
    return em.ev("po")


def build_program(phases, unfused_io=True):
    nc = bass.Bass("TRN2", target_bir_lowering=False, detect_race_conditions=DETECT_RACES)
    g = {"nc": nc}

    x_in = nc.dram_tensor("x_in", [S, D], F32, kind="ExternalInput").ap()
    x_out = nc.dram_tensor("x_out", [S, D], F32, kind="ExternalOutput").ap()
    if DBG.get("dump"):
        g["dbg1"] = nc.dram_tensor("dbg1", [128, 4096], F32, kind="ExternalOutput").ap()
        g["dbg2"] = nc.dram_tensor("dbg2", [128, 4096], F32, kind="ExternalOutput").ap()
        g["dbg3"] = nc.dram_tensor("dbg3", [128, 3072], F32, kind="ExternalOutput").ap()
        g["dbg4"] = nc.dram_tensor("dbg4", [128, 16], F32, kind="ExternalOutput").ap()
        g["dbg5"] = nc.dram_tensor("dbg5", [128, 2560], F32, kind="ExternalOutput").ap()
    need_l0ffn = any(p.startswith("l0ffn") for p in phases)
    if "l0attn" in phases:
        g["in_qkv"] = nc.dram_tensor("qkv_r", [D, 3 * D], F32, kind="ExternalInput").ap()
        g["in_qb"] = nc.dram_tensor("qb_r", [128, 8], F32, kind="ExternalInput").ap()
        g["in_pos"] = nc.dram_tensor("pos_r", [256, D], F32, kind="ExternalInput").ap()
        g["in_sins"] = nc.dram_tensor("sins_r", [256, S], F32, kind="ExternalInput").ap()
        g["in_wo0"] = nc.dram_tensor("wo0", [D, D], F32, kind="ExternalInput").ap()
        g["in_ob"] = nc.dram_tensor("ob_r", [1, D], F32, kind="ExternalInput").ap()
        g["embr_s"] = nc.dram_tensor("embr_s", [8 * 128 * S], BF16, kind="Internal").ap()
        g["skw"] = (nc.dram_tensor("skw0", [128 * 2560], BF16, kind="Internal").ap(),
                    nc.dram_tensor("skw1", [128 * 2560], BF16, kind="Internal").ap())
        g["o_s"] = nc.dram_tensor("o_s", [8 * 128 * S], BF16, kind="Internal").ap()
    if "l1attn" in phases:
        g["in_wqkv"] = nc.dram_tensor("wqkv_g", [D, 3 * D], F32, kind="ExternalInput").ap()
        g["in_wo"] = nc.dram_tensor("wo1", [D, D], F32, kind="ExternalInput").ap()
        g["in_temp"] = nc.dram_tensor("temp_r", [16], F32, kind="ExternalInput").ap()
        g["in_gamma"] = nc.dram_tensor("gamma_r", [128, 8], F32, kind="ExternalInput").ap()
    if any(p.startswith("l1ffn") for p in phases):
        g["in_wg"] = nc.dram_tensor("wg_t", [D, FF1], I32, kind="ExternalInput").ap()
        g["in_wu"] = nc.dram_tensor("wu_t", [D, FF1], I32, kind="ExternalInput").ap()
        g["in_wd"] = nc.dram_tensor("wd_t", [FF1, D], I32, kind="ExternalInput").ap()
    if need_l0ffn:
        g["in_w_up"] = nc.dram_tensor("w_up_t", [D, FF0], F32, kind="ExternalInput").ap()
        g["in_w_dn"] = nc.dram_tensor("w_dn_t", [FF0, D], F32, kind="ExternalInput").ap()
        g["scr_wup"] = nc.dram_tensor("scr_wup", [D, FF0], BF16, kind="Internal").ap()
        g["scr_wdn"] = nc.dram_tensor("scr_wdn", [FF0, D], BF16, kind="Internal").ap()

    with ExitStack() as ctx:
        em = Em(nc, ctx)
        for s in ("pe", "ve", "se", "po", "dX"):
            em.sem(s)

        def sb(name, shape, dtype=F32):
            return ctx.enter_context(nc.sbuf_tensor(name, shape, dtype))

        g["x_res"] = sb("x_res", [128, NT * D])
        g["ident"] = sb("ident", [128, 128], BF16)
        g["ones_col_bf"] = sb("ones_col_bf", [128, 1], BF16)
        g["ones_row_bf"] = sb("ones_row_bf", [1, 128], BF16)
        g["ones_col_f"] = sb("ones_col_f", [128, 1])
        g["ones_row_f"] = sb("ones_row_f", [1, 128])
        g["bn12"] = sb("bn12", [128, 12])
        g["mv"] = sb("mv", [128, 2 * NT])
        g["scol"] = sb("scol", [128, NT])
        g["bcol"] = sb("bcol", [128, NT])
        g["c16a"] = sb("c16a", [128, NT])
        g["c16b"] = sb("c16b", [128, NT])
        g["scratch8"] = sb("scratch8", [128, 8])
        g["nsc"] = sb("nsc", [128, 4])
        g["cc"] = sb("cc", [128, len(CVALS)])
        g["shared2k"] = sb("shared2k", [128, 512])[:, :]
        g["xt_bf"] = sb("xt_bf", [128, D], BF16)[:, :]

        dX = "dX"
        for t in range(NT):
            em.dma("sync", g["x_res"][:, t * D:(t + 1) * D],
                   x_in[t * 128:(t + 1) * 128, :], inc=dX)
        xall = em.ev(dX)
        for e in ENGINES:
            em.emit(e, lambda eng: None, waits=[xall], real=False)

        cev = build_consts(em, g)
        g["consts_ev"] = cev
        for e in ("tensor", "vector", "scalar"):
            em.emit(e, lambda eng: None, waits=[cev], real=False)

        first_l0ffn_done = False
        for phase_i, ph in enumerate(phases):
            with ExitStack() as pctx:
                def psb(name, shape, dtype=F32, phase_i=phase_i):
                    return pctx.enter_context(
                        nc.sbuf_tensor(f"p{phase_i}_{name}", shape, dtype))

                def pps(name, shape, dtype=F32, phase_i=phase_i):
                    return pctx.enter_context(
                        nc.psum_tensor(f"p{phase_i}_{name}", shape, dtype))

                if ph.startswith("l0ffn"):
                    g["ps_tr"] = (pps("ps_tr0", [128, 512], BF16),
                                  pps("ps_tr1", [128, 512], BF16))
                    g["ps_tr_free"] = [em.ev("pe"), em.ev("pe")]
                    g["ps_up"] = (pps("ps_up0", [128, 512]), pps("ps_up1", [128, 512]))
                    g["ps_y"] = (pps("ps_y0", [128, 1024]), pps("ps_y1", [128, 1024]))
                    g["wup_sb"] = psb("wup_sb", [128, 8 * FF0], BF16)
                    g["wdn_sb"] = psb("wdn_sb", [128, 32 * D], BF16)
                    g["xnf2"] = (psb("xnf0", [128, D], BF16), psb("xnf1", [128, D], BF16))
                    g["hbf"] = (psb("hbf0", [128, 512], BF16), psb("hbf1", [128, 512], BF16))
                    g["htf"] = (psb("htf0", [128, 512], BF16), psb("htf1", [128, 512], BF16))
                    g["qacc"] = psb("qacc", [128, 8])
                    do_prep = not first_l0ffn_done
                    if do_prep:
                        g["ps_b"] = g["ps_up"][1]
                        g["ps_b_free"] = em.ev("pe")
                        g["stg_w"] = g["shared2k"]
                        g["stg_w2"] = g["xt_bf"].bitcast(F32)
                        g["tern_acc"] = g["nsc"]
                        g["thr_row"] = psb("thr_row", [1, 1])
                        g["thr_col"] = psb("thr_col", [128, 1])
                        g["nthr_col"] = psb("nthr_col", [128, 1])
                        g["tern_sgn"] = g["xnf2"][0]
                        g["tern_gt"] = g["xnf2"][1]
                        g["tern_stg_free"] = [em.ev("ve")] * 4
                        g["tern_sgn_free"] = [em.ev("ve")] * 4
                        g["write_scratch"] = True
                    emit_l0ffn(em, g, do_prep=do_prep,
                               from_scratch=first_l0ffn_done)
                    first_l0ffn_done = True
                elif ph.startswith("l1ffn"):
                    g["ps_tr"] = (pps("ps_tr0", [128, 512], BF16),
                                  pps("ps_tr1", [128, 512], BF16))
                    g["ps_tr_free"] = [em.ev("pe"), em.ev("pe")]
                    g["ps_gu"] = ((pps("ps_g0", [128, 512]), pps("ps_g1", [128, 512])),
                                  (pps("ps_u0", [128, 512]), pps("ps_u1", [128, 512])))
                    g["ps_y"] = pps("ps_y", [128, 1024])[:, :]
                    g["wg_sb"] = psb("wg_sb", [128, 8 * (FF1 // 2)], BF16)
                    g["wu_sb"] = psb("wu_sb", [128, 8 * (FF1 // 2)], BF16)
                    g["wd_sb"] = psb("wd_sb", [128, 12 * D], BF16)
                    g["xnf_all"] = psb("xnf_all", [128, NT * D], BF16)
                    g["hbf"] = (psb("hbf0", [128, 512], BF16), psb("hbf1", [128, 512], BF16))
                    g["htf"] = (psb("htf0", [128, 512], BF16), psb("htf1", [128, 512], BF16))
                    g.pop("l1w_free", None)
                    emit_l1ffn(em, g)
                elif ph == "l0attn":
                    emit_l0attn_alloc(g, em, psb, pps)
                    emit_l0attn(em, g, do_prep=not g.get("l0attn_prep_done"))
                    g["l0attn_prep_done"] = True
                elif ph == "l1attn":
                    emit_l1attn_alloc(g, em, psb, pps)
                    emit_l1attn(em, g)
                else:
                    raise NotImplementedError(ph)
                em.barrier()

        dO = em.sem("dO")
        for t in range(NT):
            em.dma("sync", x_out[t * 128:(t + 1) * 128, :],
                   g["x_res"][:, t * D:(t + 1) * D], inc=dO)
        em.emit("sync", lambda e: None, waits=[em.ev("dO")], real=False)
        em.drain_dmas()

        blk = ctx.enter_context(nc.Block())
        em.play(blk)
    return nc



def prep_inputs(inputs):
    d = {}
    d["w_up_t"] = np.ascontiguousarray(inputs["L0_ffn_w_up"].T)
    d["w_dn_t"] = np.ascontiguousarray(inputs["L0_ffn_w_down"].T)
    d["wg_t"] = np.ascontiguousarray(inputs["L1_ffn_w_gate"].T)
    d["wu_t"] = np.ascontiguousarray(inputs["L1_ffn_w_up"].T)
    d["wd_t"] = np.ascontiguousarray(inputs["L1_ffn_w_down"].T)
    HEADS, FREQS, MAX_PERIOD = 16, 256, 10000
    pos = np.arange(S, dtype=np.float64)[:, None]
    idx = np.arange(FREQS)[None, :]
    inv_freq = MAX_PERIOD ** (-(idx // 2 * 2) / FREQS)
    angle = pos * inv_freq
    sins = np.where(idx % 2 == 0, np.sin(angle), np.cos(angle)).astype(np.float32)
    sins = np.float32(np.sqrt(2.0)) * sins
    sins_scaled = sins * np.float32(FREQS ** (-0.5))
    d["sins_r"] = np.ascontiguousarray(sins_scaled[::-1, :].T)
    d["qkv_r"] = np.ascontiguousarray(inputs["L0_attn_qkv"].reshape(D, 3 * D))
    d["qb_r"] = np.ascontiguousarray(inputs["L0_attn_q_bias"].reshape(1024).reshape(8, 128).T)
    d["pos_r"] = np.ascontiguousarray(inputs["L0_attn_positional"].reshape(256, D))
    d["wo0"] = np.ascontiguousarray(inputs["L0_attn_out_kernel"])
    d["ob_r"] = np.ascontiguousarray(inputs["L0_attn_out_bias"].reshape(1, D))
    d["wqkv_g"] = np.ascontiguousarray(inputs["L1_attn_w_qkv"])
    d["wo1"] = np.ascontiguousarray(inputs["L1_attn_w_out"])
    d["temp_r"] = np.ascontiguousarray(inputs["L1_attn_temperature"].reshape(16))
    d["gamma_r"] = np.ascontiguousarray(inputs["L1_attn_gamma"].reshape(8, 128).T)
    return d


def kernel(**inputs):
    x = np.ascontiguousarray(inputs["L0_attn_input"])
    host = prep_inputs(inputs)
    phases = list(ALL_PHASES)
    nc = build_program(phases)
    in_maps = []
    for c in range(8):
        m = {"x_in": np.ascontiguousarray(x[c])}
        m.update(host)
        in_maps.append(m)
    res = run_bass_kernel_spmd(nc, in_maps, core_ids=list(range(8)))
    out = np.stack([r["x_out"] for r in res.results], axis=0)
    return out.astype(np.float32)


# version 04
# speedup vs baseline: 1.1565x; 1.1565x over previous
import math
import numpy as np
import ml_dtypes
from contextlib import ExitStack

import concourse.bass as bass
import concourse.mybir as mybir
from concourse.bass_utils import run_bass_kernel_spmd

F32 = mybir.dt.float32
BF16 = mybir.dt.bfloat16
I32 = mybir.dt.int32
AF = mybir.ActivationFunctionType
ALU = mybir.AluOpType
AX = mybir.AxisListType
FP = np.float32
DETECT_RACES = False

D = 1024
S = 2048
NT = 16
H = 16
HD = 64
FF0 = 4096
FF1 = 3072
C_QKV = float((3.0 * D * HD * H) ** (-0.25))
C_OUT = float((D * D) ** (-0.25))
SM = float(HD ** (-0.5))
NEG = -1e9
SKW = 2176

ENGINES = ("sync", "scalar", "vector", "tensor", "gpsimd")

DBG = {"stats": True, "prep": True, "main_tiles": NT, "fcs": 8, "epi": True, "newton": True}

ALL_PHASES = ("l0ffn_a", "l0attn", "l0ffn_b", "l1ffn_a", "l1attn", "l1ffn_b")


class Em:

    def __init__(self, nc, ctx):
        self.nc = nc
        self.ctx = ctx
        self.q = {e: [] for e in ENGINES}
        self.last_real = {e: None for e in ENGINES}
        self.semh = {}
        self.cnt = {}
        self.lastw = {}
        self.dma_eng = {}

    def sem(self, name):
        if name not in self.semh:
            self.semh[name] = self.ctx.enter_context(self.nc.semaphore(name))
            self.cnt[name] = 0
        return name

    def ev(self, name):
        return (name, self.cnt[name])

    AUTO_SER = {"vector": "ve", "scalar": "se", "gpsimd": "po"}

    def emit(self, eng, fn, waits=(), inc=None, inc_n=1, real=True, _dma=False):
        waits = list(waits)
        if real and not _dma and eng in self.AUTO_SER:
            cs = self.AUTO_SER[eng]
            self.sem(cs)
            waits.append((cs, self.cnt[cs]))
            if inc is None:
                inc = cs
            else:
                assert inc == cs, (eng, inc)
        rw = []
        for w in waits:
            if w is None:
                continue
            s, v = w
            if v <= 0:
                continue
            if self.lastw.get((eng, s), 0) >= v:
                continue
            self.lastw[(eng, s)] = v
            rw.append((s, v))
        tgt = None
        if inc is not None:
            assert real
            self.sem(inc)
            self.cnt[inc] += inc_n
            tgt = (inc, self.cnt[inc], inc_n)
        entry = [rw, fn, tgt, real]
        self.q[eng].append(entry)
        if real:
            self.last_real[eng] = entry
        return self.ev(inc) if inc is not None else None

    def dma(self, eng, out, in_, waits=(), inc=None, **kw):
        if inc is not None:
            self.dma_eng[inc] = eng
        return self.emit(eng, lambda e: e.dma_start(out=out, in_=in_, **kw),
                         waits=waits, inc=inc, inc_n=16, _dma=True)

    def drain_dmas(self):
        for sname, eng in list(self.dma_eng.items()):
            self.emit(eng, lambda e: None, waits=[self.ev(sname)], real=False)

    def engine_done_ev(self, e):
        entry = self.last_real[e]
        if entry is None:
            return None
        if entry[2] is None:
            sname = {"sync": "qS", "scalar": "qA", "vector": "ve", "tensor": "pe",
                     "gpsimd": "po"}[e]
            self.sem(sname)
            self.cnt[sname] += 1
            entry[2] = (sname, self.cnt[sname], 1)
        return (entry[2][0], entry[2][1])

    def barrier(self):
        evs = [self.engine_done_ev(e) for e in ENGINES]
        evs += [self.ev(s) for s in self.dma_eng]
        for e in ENGINES:
            self.emit(e, lambda eng: None, waits=evs, real=False)

    def play(self, blk):
        semh = self.semh

        def run(eng, stream):
            for rw, fn, tgt, _real in stream:
                for (s, v) in rw:
                    eng.wait_ge(semh[s], v)
                inst = fn(eng)
                if tgt is not None and inst is not None:
                    inst.then_inc(semh[tgt[0]], tgt[2])

        for e in ENGINES:
            stream = self.q[e]
            if not stream:
                continue
            getattr(blk, e)(lambda eng, stream=stream: run(eng, stream))


def rsqrt_newton(em, g, out, qcol, scratch, iters=2):
    r = scratch[:, 0:1]
    t = scratch[:, 1:2]
    fa = scratch[:, 2:3]
    em.emit("vector", lambda e: e.tensor_copy(fa, qcol.bitcast(I32)))
    em.emit("vector", lambda e: e.tensor_scalar(
        fa, fa, g["cc"][:, CVALS.index(-0.5):CVALS.index(-0.5) + 1],
        g["cc"][:, CVALS.index(1597463007.0):CVALS.index(1597463007.0) + 1],
        ALU.mult, ALU.add))
    em.emit("vector", lambda e: e.tensor_copy(r.bitcast(I32), fa))
    for _ in range(iters):
        em.emit("vector", lambda e: e.tensor_mul(t, r, r))
        em.emit("vector", lambda e: e.tensor_mul(t, t, qcol))
        em.emit("vector", lambda e: e.tensor_scalar(t, t, g["cc"][:, 4:5], g["cc"][:, 5:6], ALU.mult, ALU.add))
        em.emit("vector", lambda e: e.tensor_mul(r, r, t))
    return em.emit("vector", lambda e: e.tensor_copy(out, r), inc="ve")


def emit_ln_stats(em, g, tiles=range(NT)):
    x = g["x_res"]
    bn = g["bn12"]
    mv = g["mv"]
    evs = []
    for t in tiles:
        xt = x[:, t * D:(t + 1) * D]
        w = [g["xready"][t]] if "xready" in g else []
        em.emit("vector", lambda e, xt=xt: e.bn_stats(bn[:, 0:6], xt[:, 0:512]), waits=w)
        em.emit("vector", lambda e, xt=xt: e.bn_stats(bn[:, 6:12], xt[:, 512:1024]))
        evs.append(em.emit("vector", lambda e, t=t: e.bn_aggr(
            mv[:, 2 * t:2 * t + 2], bn[:, 0:12]), inc="ve"))
    return evs


def emit_scale_chain_l0ffn(em, g, stats_evs):
    mv, sc = g["mv"], g["scratch8"]
    scol, bcol = g["scol"], g["bcol"]
    ra, rb = g["c16a"], g["c16b"]
    for t in range(NT):
        v = mv[:, 2 * t + 1:2 * t + 2]
        lv1 = sc[:, 0:1]
        lv0 = sc[:, 1:2]
        w = [stats_evs[t]]
        em.emit("scalar", lambda e, v=v, lv1=lv1: e.activation(
            lv1, v, AF.Ln, bias=FP(1e-5)), waits=w)
        em.emit("scalar", lambda e, v=v, lv0=lv0: e.activation(lv0, v, AF.Ln))
        em.emit("scalar", lambda e, lv1=lv1, t=t: e.activation(
            ra[:, t:t + 1], lv1, AF.Exp, scale=FP(-0.5)))
        se = em.emit("scalar", lambda e, lv0=lv0, t=t: e.activation(
            rb[:, t:t + 1], lv0, AF.Exp, scale=FP(0.5)),
            inc="se" if t == NT - 1 else None)
    for t in range(NT):
        m = mv[:, 2 * t:2 * t + 1]
        em.emit("vector", lambda e, t=t: e.tensor_mul(
            sc[:, 4:5], rb[:, t:t + 1], ra[:, t:t + 1]), waits=[se])
        em.emit("vector", lambda e: e.tensor_scalar_add(sc[:, 4:5], sc[:, 4:5], cval(g, 1e-8)))
        em.emit("vector", lambda e: e.reciprocal(sc[:, 6:7], sc[:, 4:5]))
        em.emit("vector", lambda e, t=t: e.tensor_mul(
            scol[:, t:t + 1], ra[:, t:t + 1], sc[:, 6:7]))
        em.emit("vector", lambda e, m=m, t=t: e.tensor_mul(
            sc[:, 7:8], m, scol[:, t:t + 1]))
        em.emit("vector", lambda e, t=t: e.tensor_scalar_mul(
            bcol[:, t:t + 1], sc[:, 7:8], cval(g, -1.0)), inc="ve")
    return em.ev("ve")


def emit_scale_chain_ln(em, g, stats_evs, extra=None):
    mv, sc = g["mv"], g["scratch8"]
    scol, bcol = g["scol"], g["bcol"]
    ra, rb = g["c16a"], g["c16b"]
    for t in range(NT):
        v = mv[:, 2 * t + 1:2 * t + 2]
        lv1 = sc[:, 0:1]
        w = [stats_evs[t]]
        em.emit("scalar", lambda e, v=v, lv1=lv1: e.activation(
            lv1, v, AF.Ln, bias=FP(1e-5)), waits=w)
        em.emit("scalar", lambda e, lv1=lv1, t=t: e.activation(
            ra[:, t:t + 1], lv1, AF.Exp, scale=FP(-0.5)))
        if extra == "l1attn":
            lv0 = sc[:, 1:2]
            em.emit("scalar", lambda e, v=v, lv0=lv0: e.activation(lv0, v, AF.Ln))
            em.emit("scalar", lambda e, lv0=lv0, t=t: e.activation(
                rb[:, t:t + 1], lv0, AF.Exp, scale=FP(0.5)))
    se = em.engine_done_ev("scalar")
    for t in range(NT):
        m = mv[:, 2 * t:2 * t + 1]
        if extra == "l1attn":
            em.emit("vector", lambda e, t=t: e.tensor_mul(
                sc[:, 4:5], rb[:, t:t + 1], ra[:, t:t + 1]), waits=[se])
            em.emit("vector", lambda e: e.tensor_scalar_max(
                sc[:, 4:5], sc[:, 4:5], cval(g, 1e-12 / 32.0)))
            em.emit("vector", lambda e: e.reciprocal(sc[:, 5:6], sc[:, 4:5]))
            em.emit("vector", lambda e, t=t: e.tensor_mul(
                scol[:, t:t + 1], ra[:, t:t + 1], sc[:, 5:6]))
        else:
            em.emit("vector", lambda e, t=t: e.tensor_copy(
                scol[:, t:t + 1], ra[:, t:t + 1]), waits=[se])
        em.emit("vector", lambda e, m=m, t=t: e.tensor_mul(
            sc[:, 7:8], m, scol[:, t:t + 1]))
        em.emit("vector", lambda e, t=t: e.tensor_scalar_mul(
            bcol[:, t:t + 1], sc[:, 7:8], cval(g, -1.0)), inc="ve")
    return em.ev("ve")


def emit_xn_tile(em, g, t, xnf, par_ev=None, gamma=None, dc_major=False):
    x = g["x_res"]
    xt_bf = g["xt_bf"]
    scol, bcol = g["scol"], g["bcol"]
    ps0, ps1 = g["ps_tr"]
    ident = g["ident"]
    ve = em.emit("vector", lambda e: e.tensor_scalar(
        xt_bf[:, :], x[:, t * D:(t + 1) * D], scol[:, t:t + 1], bcol[:, t:t + 1],
        ALU.mult, ALU.add), waits=[par_ev] if par_ev else [], inc="ve")
    pe_evs = []
    for half in range(2):
        ps = (ps0, ps1)[half]
        for j in range(4):
            dc = half * 4 + j
            pe = em.emit("tensor", lambda e, ps=ps, j=j, dc=dc: e.transpose(
                ps[:, j * 128:(j + 1) * 128], xt_bf[:, dc * 128:(dc + 1) * 128],
                ident[:, :]),
                waits=[ve, g["ps_tr_free"][half]],
                inc="pe" if j == 3 else None)
        if dc_major:
            dst = xnf.rearrange("p (dc n) -> p dc n", dc=8)[
                :, half * 4:(half + 1) * 4, t * 128:(t + 1) * 128]
            veo = em.emit("vector", lambda e, ps=ps, dst=dst: e.tensor_copy(
                dst, ps[:, :].rearrange("p (c n) -> p c n", c=4)),
                waits=[pe], inc="ve")
        elif gamma is None:
            veo = em.emit("vector", lambda e, ps=ps, half=half: e.tensor_copy(
                xnf[:, half * 512:(half + 1) * 512], ps[:, :]), waits=[pe], inc="ve")
        else:
            gcol, gev = gamma
            for j in range(4):
                dc = half * 4 + j
                veo = em.emit("vector", lambda e, ps=ps, j=j, dc=dc: e.tensor_scalar(
                    xnf[:, dc * 128:(dc + 1) * 128], ps[:, j * 128:(j + 1) * 128],
                    gcol[:, dc:dc + 1], None, ALU.mult),
                    waits=[pe, gev], inc="ve")
        g["ps_tr_free"][half] = veo
        pe_evs.append(veo)
    return pe_evs[-1]


def emit_l0ffn_prep(em, g, write_scratch=True):
    acc = g["tern_acc"]
    th_ps = g["ps_b"]
    stgs = [g["stg_w"][:, 0:256], g["stg_w"][:, 256:512],
            g["stg_w2"][:, 0:256], g["stg_w2"][:, 256:512]]
    queues = ["sync", "scalar", "sync", "scalar"]
    for wi, (src, n_chunks, res, scr) in enumerate([
            (g["in_w_up"], 8, g["wup_sb"], g["scr_wup"]),
            (g["in_w_dn"], 32, g["wdn_sb"], g["scr_wdn"])]):
        ncols = FF0 if wi == 0 else D
        PW = 256
        npieces = ncols // PW
        pieces = [(c, p) for c in range(n_chunks) for p in range(npieces)]
        mz = em.emit("vector", lambda e: e.memset(acc[:, 0:2], 0.0), inc="ve")
        ve = mz
        for i, (c, p) in enumerate(pieces):
            par = i % 4
            dsem = em.sem(f"dW{wi}{par}")
            sl = stgs[par]
            ld = em.dma(queues[par], sl,
                        src[c * 128:(c + 1) * 128, p * PW:(p + 1) * PW],
                        waits=[g["tern_stg_free"][par]], inc=dsem)
            em.emit("vector", lambda e, sl=sl: e.tensor_reduce(
                acc[:, 1:2], sl, axis=AX.X, op=ALU.add,
                apply_absolute_value=True), waits=[ld])
            ve = em.emit("vector", lambda e: e.tensor_add(
                acc[:, 0:1], acc[:, 0:1], acc[:, 1:2]), inc="ve")
            g["tern_stg_free"][par] = ve
        pe = em.emit("tensor", lambda e: e.matmul(
            th_ps[0:1, 0:1], acc[:, 0:1], g["ones_col_f"][:, :],
            start=True, stop=True), waits=[ve, g["ps_b_free"]], inc="pe")
        vth = em.emit("vector", lambda e: e.tensor_scalar_mul(
            g["thr_row"][0:1, 0:1], th_ps[0:1, 0:1],
            cval(g, 0.5 / (8 * 128 * FF0))[0:1, :]), waits=[pe], inc="ve")
        pe2 = em.emit("tensor", lambda e: e.matmul(
            th_ps[:, 4:5], g["ones_row_f"][0:1, :], g["thr_row"][0:1, 0:1],
            start=True, stop=True), waits=[vth], inc="pe")
        em.emit("vector", lambda e: e.tensor_copy(
            g["thr_col"][:, 0:1], th_ps[:, 4:5]), waits=[pe2])
        vcol = em.emit("vector", lambda e: e.tensor_scalar_mul(
            g["nthr_col"][:, 0:1], g["thr_col"][:, 0:1], cval(g, -1.0)), inc="ve")
        g["ps_b_free"] = vcol
        for i, (c, p) in enumerate(pieces):
            par = i % 4
            dsem = em.sem(f"dW{wi}{par}")
            sl = stgs[par]
            ld = em.dma(queues[par], sl,
                        src[c * 128:(c + 1) * 128, p * PW:(p + 1) * PW],
                        waits=[g["tern_stg_free"][par]], inc=dsem)
            sgn = g["tern_sgn"][:, par * PW:par * PW + PW]
            gtb = g["tern_gt"][:, par * PW:par * PW + PW]
            em.emit("vector", lambda e, sl=sl, sgn=sgn: e.tensor_single_scalar(
                sgn, sl, g["thr_col"][:, 0:1], ALU.is_gt),
                waits=[ld, vcol, g["tern_sgn_free"][par]])
            em.emit("vector", lambda e, sl=sl, gtb=gtb: e.tensor_single_scalar(
                gtb, sl, g["nthr_col"][:, 0:1], ALU.is_lt))
            dst = res[:, c * ncols + p * PW: c * ncols + p * PW + PW]
            ve = em.emit("vector", lambda e, dst=dst, sgn=sgn, gtb=gtb: e.tensor_sub(
                dst, sgn, gtb), inc="ve")
            g["tern_sgn_free"][par] = ve
            g["tern_stg_free"][par] = ve
            if write_scratch:
                em.dma("gpsimd", scr[c * 128:(c + 1) * 128, p * PW:(p + 1) * PW],
                       dst, waits=[ve], inc=em.sem("dWS"))
    return em.ev("ve")


def emit_l0ffn(em, g, do_prep, from_scratch):
    nc = g["nc"]
    if DBG["stats"]:
        stats = emit_ln_stats(em, g)
        sev = emit_scale_chain_l0ffn(em, g, stats)
    if do_prep and DBG["prep"]:
        emit_l0ffn_prep(em, g, write_scratch=g["write_scratch"])
    if from_scratch:
        dW = em.sem("dWL")
        for c in range(8):
            em.dma("sync", g["wup_sb"][:, c * FF0:(c + 1) * FF0],
                   g["scr_wup"][c * 128:(c + 1) * 128, :], inc=dW)
        for c in range(32):
            em.dma("sync", g["wdn_sb"][:, c * D:(c + 1) * D],
                   g["scr_wdn"][c * 128:(c + 1) * 128, :], inc=dW)
        wready = em.ev(("dWL"))
    else:
        wready = em.ev("ve")
    wup, wdn = g["wup_sb"], g["wdn_sb"]
    ps_up = g["ps_up"]
    ps_y = g["ps_y"]
    xnf2 = g["xnf2"]
    hbf = g["hbf"]
    htf = g["htf"]
    qacc = g["qacc"]
    sc = g["scratch8"]

    xnf_free = [em.ev("pe"), em.ev("pe")]
    gelu_done = [None, None]
    sumsq_done = [None, None]
    htf_free = [em.ev("pe"), em.ev("pe")]
    epi_ev = [None, None]
    for t in range(DBG["main_tiles"]):
        par = t % 2
        xnf = xnf2[par]
        xe = emit_xn_tile(em, g, t, xnf, par_ev=xnf_free[par])
        sumsq_evs = []
        for fc in range(DBG["fcs"]):
            fp = fc % 2
            for dc in range(8):
                w = [xe, wready]
                if gelu_done[fp] is not None and dc == 0:
                    w.append(gelu_done[fp])
                pe_up = em.emit("tensor", lambda e, fp=fp, dc=dc, fc=fc, xnf=xnf: e.matmul(
                    ps_up[fp][:, :], xnf[:, dc * 128:(dc + 1) * 128],
                    wup[:, dc * FF0 + fc * 512: dc * FF0 + fc * 512 + 512],
                    start=(dc == 0), stop=(dc == 7)), waits=w,
                    inc="pe" if dc == 7 else None)
            gw = [pe_up]
            if sumsq_done[fp] is not None:
                gw.append(sumsq_done[fp])
            ge = em.emit("scalar", lambda e, fp=fp: e.activation(
                hbf[fp][:, :], ps_up[fp][:, :], AF.Gelu), waits=gw, inc="se")
            gelu_done[fp] = ge
            sq = em.emit("vector", lambda e, fp=fp, fc=fc: e.scalar_tensor_tensor(
                htf[fp][:, :], hbf[fp][:, :], cval(g, 1.0), hbf[fp][:, :], ALU.mult, ALU.mult,
                accum_out=qacc[:, fc:fc + 1]), waits=[ge], inc="ve")
            sumsq_evs.append(sq)
            sumsq_done[fp] = sq
            for j in range(4):
                pe_t = em.emit("tensor", lambda e, fp=fp, j=j: e.transpose(
                    g["ps_tr"][fp][:, j * 128:(j + 1) * 128],
                    hbf[fp][:, j * 128:(j + 1) * 128], g["ident"][:, :]),
                    waits=[ge, g["ps_tr_free"][fp]],
                    inc="pe" if j == 3 else None)
            ve_t = em.emit("vector", lambda e, fp=fp: e.tensor_copy(
                htf[fp][:, :], g["ps_tr"][fp][:, :]),
                waits=[pe_t, htf_free[fp]], inc="ve")
            g["ps_tr_free"][fp] = ve_t
            for fs in range(4):
                fcc = fc * 4 + fs
                for nh in range(2):
                    w = [ve_t, wready]
                    if fc == 0 and fs == 0 and epi_ev[par] is not None:
                        w.append(epi_ev[par])
                    last = (fc == DBG["fcs"] - 1 and fs == 3 and nh == 1)
                    mm = em.emit("tensor",
                            lambda e, par=par, fs=fs, nh=nh, fcc=fcc, fc=fc: e.matmul(
                                ps_y[par][:, nh * 512:(nh + 1) * 512],
                                htf[fc % 2][:, fs * 128:(fs + 1) * 128],
                                wdn[:, fcc * D + nh * 512: fcc * D + nh * 512 + 512],
                                start=(fc == 0 and fs == 0),
                                stop=(fc == DBG["fcs"] - 1 and fs == 3)), waits=w,
                            inc="pe" if last else None)
                    if last:
                        pe_dn = mm
        htf_free = [pe_dn, pe_dn]
        xnf_free[par] = pe_dn
        if not DBG["epi"]:
            epi_ev[par] = pe_dn
            continue
        em.emit("vector", lambda e: e.tensor_reduce(
            sc[:, 0:1], qacc[:, :], axis=AX.X, op=ALU.add), waits=sumsq_evs[-1:])
        em.emit("vector", lambda e: e.tensor_scalar_mul(sc[:, 0:1], sc[:, 0:1], cval(g, 1.0 / FF0)))
        em.emit("vector", lambda e: e.tensor_scalar_max(sc[:, 0:1], sc[:, 0:1], cval(g, 1e-30)))
        if DBG["newton"]:
            rsqrt_newton(em, g, sc[:, 1:2], sc[:, 0:1], g["nsc"])
        else:
            em.emit("vector", lambda e: e.tensor_copy(sc[:, 1:2], sc[:, 0:1]))
        em.emit("vector", lambda e: e.tensor_mul(sc[:, 2:3], sc[:, 0:1], sc[:, 1:2]))
        em.emit("vector", lambda e: e.tensor_scalar_add(sc[:, 2:3], sc[:, 2:3], cval(g, 1e-8)))
        em.emit("vector", lambda e: e.reciprocal(sc[:, 3:4], sc[:, 2:3]))
        em.emit("vector", lambda e: e.tensor_scalar_mul(sc[:, 3:4], sc[:, 3:4], cval(g, 0.5)))
        ep = em.emit("vector", lambda e, t=t, par=par: e.scalar_tensor_tensor(
            g["x_res"][:, t * D:(t + 1) * D], ps_y[par][:, :], sc[:, 3:4],
            g["x_res"][:, t * D:(t + 1) * D], ALU.mult, ALU.add),
            waits=[pe_dn], inc="ve")
        epi_ev[par] = ep
    if DBG.get("dump"):
        lastve = em.ev("ve")
        em.dma("gpsimd", g["dbg1"], wup[:, 0:4096], waits=[lastve], inc="dDB")
        em.dma("gpsimd", g["dbg2"], wdn[:, 0:4096], waits=[lastve], inc="dDB")
        em.dma("gpsimd", g["dbg3"], xnf2[(DBG["main_tiles"] - 1) % 2][:, :],
               waits=[lastve], inc="dDB")
        em.dma("gpsimd", g["dbg4"][:, 0:8], qacc[:, :], waits=[lastve], inc="dDB", allow_slow_non_contiguous=True)
        em.dma("gpsimd", g["dbg4"][:, 8:9], g["thr_col"][:, 0:1], waits=[lastve], inc="dDB", allow_slow_non_contiguous=True)
        em.dma("gpsimd", g["dbg4"][:, 9:10], sc[:, 3:4], waits=[lastve], inc="dDB", allow_slow_non_contiguous=True)
        em.dma("gpsimd", g["dbg4"][:, 10:11], g["scol"][:, 0:1], waits=[lastve], inc="dDB", allow_slow_non_contiguous=True)
        em.dma("gpsimd", g["dbg4"][:, 11:12], g["bcol"][:, 0:1], waits=[lastve], inc="dDB", allow_slow_non_contiguous=True)
        em.dma("gpsimd", g["dbg4"][:, 12:14], g["mv"][:, 0:2], waits=[lastve], inc="dDB", allow_slow_non_contiguous=True)
        em.dma("gpsimd", g["dbg4"][:, 14:15], g["c16a"][:, 0:1], waits=[lastve], inc="dDB", allow_slow_non_contiguous=True)
        em.dma("gpsimd", g["dbg4"][:, 15:16], g["c16b"][:, 0:1], waits=[lastve], inc="dDB", allow_slow_non_contiguous=True)
    return em.ev("ve")


CVALS = [0.0, 1.0, -1.0, 0.5, -0.5, 1.5, 1e-8, 1.0 / FF0, 1e-12 / 32.0,
         0.5 / (8 * 128 * FF0), 1.0 / 127.0, 0.5 / (127.0 * 127.0),
         1597463007.0, 1e-30, 8.0, 1e-12, 0.125, 32.0, 0.03125]


def cval(g, v):
    i = CVALS.index(v)
    return g["cc"][:, i:i + 1]


def emit_l1ffn(em, g):
    stats = emit_ln_stats(em, g)
    emit_scale_chain_ln(em, g, stats)
    xnf_all = g["xnf_all"]
    wg_sb, wu_sb, wd_sb = g["wg_sb"], g["wu_sb"], g["wd_sb"]
    ps_g, ps_u = g["ps_gu"][0], g["ps_gu"][1]
    ps_y = g["ps_y"]
    hbf = g["hbf"]
    htf = g["htf"]
    HH = FF1 // 2
    xn_evs = []
    for t in range(NT):
        xn_evs.append(emit_xn_tile(em, g, t, xnf_all[:, t * D:(t + 1) * D]))
    epi = None
    gu_done = [None, None]
    dn_done = None
    for hh in range(2):
        dW = em.sem(f"dL1{hh}")
        wfree = [g.get("l1w_free")] if g.get("l1w_free") else []
        for dc in range(8):
            em.dma("gpsimd", wg_sb[:, dc * HH:(dc + 1) * HH],
                   g["in_wg"][dc * 128:(dc + 1) * 128, hh * HH:(hh + 1) * HH],
                   waits=wfree, inc=dW)
            em.dma("gpsimd", wu_sb[:, dc * HH:(dc + 1) * HH],
                   g["in_wu"][dc * 128:(dc + 1) * 128, hh * HH:(hh + 1) * HH],
                   waits=wfree, inc=dW)
        for hc in range(12):
            em.dma("gpsimd", wd_sb[:, hc * D:(hc + 1) * D],
                   g["in_wd"][hh * HH + hc * 128: hh * HH + (hc + 1) * 128, :],
                   waits=wfree, inc=dW)
        wready = em.ev(dW)
        for t in range(NT):
            xnf = xnf_all[:, t * D:(t + 1) * D]
            for hc in range(3):
                hp = hc % 2
                for dc in range(8):
                    w = [xn_evs[t], wready]
                    if gu_done[hp] is not None and dc == 0:
                        w.append(gu_done[hp])
                    em.emit("tensor", lambda e, hp=hp, dc=dc, hc=hc, xnf=xnf: e.matmul(
                        ps_g[hp][:, :], xnf[:, dc * 128:(dc + 1) * 128],
                        wg_sb[:, dc * HH + hc * 512: dc * HH + hc * 512 + 512],
                        start=(dc == 0), stop=(dc == 7)), waits=w)
                for dc in range(8):
                    last = dc == 7
                    mm = em.emit("tensor", lambda e, hp=hp, dc=dc, hc=hc, xnf=xnf: e.matmul(
                        ps_u[hp][:, :], xnf[:, dc * 128:(dc + 1) * 128],
                        wu_sb[:, dc * HH + hc * 512: dc * HH + hc * 512 + 512],
                        start=(dc == 0), stop=(dc == 7)),
                        waits=[xn_evs[t], wready],
                        inc="pe" if last else None)
                pe_gu = mm
                ge = em.emit("scalar", lambda e, hp=hp: e.activation(
                    hbf[hp][:, :], ps_g[hp][:, :], AF.Silu, scale=FP(1.0 / 127.0)),
                    waits=[pe_gu])
                mu = em.emit("vector", lambda e, hp=hp: e.tensor_tensor(
                    hbf[hp][:, :], hbf[hp][:, :], ps_u[hp][:, :], ALU.mult),
                    waits=[ge])
                gu_done[hp] = mu
                for j in range(4):
                    pe_t = em.emit("tensor", lambda e, hp=hp, j=j: e.transpose(
                        g["ps_tr"][hp][:, j * 128:(j + 1) * 128],
                        hbf[hp][:, j * 128:(j + 1) * 128], g["ident"][:, :]),
                        waits=[mu, g["ps_tr_free"][hp]],
                        inc="pe" if j == 3 else None)
                ve_t = em.emit("vector", lambda e, hp=hp: e.tensor_copy(
                    htf[hp][:, :], g["ps_tr"][hp][:, :]), waits=[pe_t])
                g["ps_tr_free"][hp] = ve_t
                for fs in range(4):
                    hcc = hc * 4 + fs
                    for nh in range(2):
                        w = [ve_t, wready]
                        if hc == 0 and fs == 0 and epi is not None:
                            w.append(epi)
                        last = (hc == 2 and fs == 3 and nh == 1)
                        mm = em.emit("tensor",
                                lambda e, fs=fs, nh=nh, hcc=hcc, hc=hc, hp=hp: e.matmul(
                                    ps_y[:, nh * 512:(nh + 1) * 512],
                                    htf[hp][:, fs * 128:(fs + 1) * 128],
                                    wd_sb[:, hcc * D + nh * 512:
                                          hcc * D + nh * 512 + 512],
                                    start=(hc == 0 and fs == 0),
                                    stop=(hc == 2 and fs == 3)), waits=w,
                                inc="pe" if last else None)
                        if last:
                            dn_done = mm
            epi = em.emit("vector", lambda e, t=t: e.scalar_tensor_tensor(
                g["x_res"][:, t * D:(t + 1) * D], ps_y[:, :],
                cval(g, 0.5 / (127.0 * 127.0)),
                g["x_res"][:, t * D:(t + 1) * D], ALU.mult, ALU.add),
                waits=[dn_done])
        g["l1w_free"] = em.engine_done_ev("tensor")
    return epi


def emit_l1attn_alloc(g, em, psb, pps):
    g["ps_tr"] = (pps("ps_tr0", [128, 512], BF16), pps("ps_tr1", [128, 512], BF16))
    g["ps_tr_free"] = [em.ev("pe"), em.ev("pe")]
    g["ps_qk"] = pps("ps_qkA", [128, 1024])[:, :]
    g["ps_sim"] = (pps("ps_simA", [64, 512]), pps("ps_simB", [64, 512]))
    g["ps_ss"] = pps("ps_ss", [128, 512])[:, :]
    g["ps_pv"] = pps("ps_pv", [128, 512])[:, :]
    g["ps_ss_free"] = em.ev("pe")
    g["xnf_free"] = [em.ev("pe"), em.ev("pe")]
    g["wq_sb"] = psb("wq_sb", [128, 8 * 3072], BF16)
    g["wo_sb"] = psb("wo_sb", [128, 8 * D], BF16)
    g["v_fm"] = psb("v_fm", [128, 8 * 2048], BF16)
    g["o_f"] = psb("o_f", [128, 8 * 2048], BF16)[:, :]
    g["qksq"] = g["o_f"][:, 0:2048]
    g["qk_t"] = g["o_f"][:, 2048:4096]
    g["xnf2"] = (g["o_f"][:, 4096:5120], g["o_f"][:, 5120:6144])
    g["t16"] = psb("t16", [16, 2])
    g["rowbuf"] = psb("rowbuf", [1, 16])
    g["expT_bc"] = psb("expT_bc", [64, 16])
    g["sqcol"] = psb("sqcol", [64, 16])
    g["skb"] = psb("skb", [64, 64])
    g["simf"] = psb("simf", [64, 64])
    g["mxc"] = psb("mxc", [64, 4])
    g["attn_bf"] = psb("attn_bf", [64, 64], BF16)
    g["attnT"] = psb("attnT", [128, 64], BF16)
    g["ones128_bf"] = psb("ones128_bf", [128, 128], BF16)
    g["identf"] = psb("identf", [128, 128])
    g["gcol"] = psb("gcol", [128, 8])
    g["ss_sb"] = psb("ss_sb", [1, 2048])


def emit_l1attn(em, g):
    stats = emit_ln_stats(em, g)
    emit_scale_chain_ln(em, g, stats)
    wq, wo = g["wq_sb"], g["wo_sb"]
    qk_t, qksq = g["qk_t"], g["qksq"]
    v_fm, o_f = g["v_fm"], g["o_f"]
    ps_qk = g["ps_qk"]
    ps_sim = g["ps_sim"]
    ps_ss = g["ps_ss"]
    ps_pv = g["ps_pv"]
    xnf2 = g["xnf2"]
    ones128 = g["ones128_bf"]
    em.emit("gpsimd", lambda e: e.memset(ones128[:, :], 1.0))
    em.emit("vector", lambda e: e.tensor_copy(g["identf"][:, :], g["ident"][:, :]),
            waits=[g["consts_ev"]])
    dG = em.sem("dG1")
    gld = em.dma("sync", g["gcol"][:, :], g["in_gamma"], inc=dG)
    dW = em.sem("dL1A")
    for dc in range(8):
        em.dma("gpsimd", wq[:, dc * 3072:(dc + 1) * 3072],
               g["in_wqkv"][dc * 128:(dc + 1) * 128, :], inc=dW)
    for c in range(8):
        em.dma("gpsimd", wo[:, c * D:(c + 1) * D],
               g["in_wo"][c * 128:(c + 1) * 128, :], inc=dW)
    wready = em.ev(dW)
    dT = em.sem("dT")
    tld = em.dma("sync", g["t16"][0:16, 0:1],
                 g["in_temp"].rearrange("(a b) -> a b", b=1), inc=dT)
    eT = em.emit("scalar", lambda e: e.activation(
        g["t16"][0:16, 1:2], g["t16"][0:16, 0:1], AF.Exp), waits=[tld])
    idf = em.engine_done_ev("vector")
    peT = em.emit("tensor", lambda e: e.transpose(
        ps_ss[0:1, 0:16], g["t16"][0:16, 1:2], g["identf"][0:16, 0:16]),
        waits=[eT, idf, g["ps_ss_free"]], inc="pe")
    veT = em.emit("vector", lambda e: e.tensor_copy(
        g["rowbuf"][0:1, 0:16], ps_ss[0:1, 0:16]), waits=[peT])
    peB = em.emit("tensor", lambda e: e.matmul(
        ps_ss[0:64, 16:32], g["ones_row_f"][0:1, 0:64], g["rowbuf"][0:1, 0:16],
        start=True, stop=True), waits=[veT], inc="pe")
    eBC = em.emit("vector", lambda e: e.tensor_copy(
        g["expT_bc"][:, :], ps_ss[0:64, 16:32]), waits=[peB])
    g["ps_ss_free"] = eBC

    sim_last = None
    qk_free = em.ev("pe")
    ss_free = eBC
    ss_add_ev = None
    for t in range(NT):
        par = t % 2
        xe = emit_xn_tile(em, g, t, xnf2[par], par_ev=g["xnf_free"][par],
                          gamma=(g["gcol"], (dG, 16)))
        for gq in range(2):
            for nh in range(2):
                for dc in range(8):
                    w = [xe, wready, qk_free]
                    last = dc == 7 and nh == 1
                    mm = em.emit("tensor", lambda e, gq=gq, nh=nh, dc=dc, par=par: e.matmul(
                        ps_qk[:, nh * 512:(nh + 1) * 512],
                        xnf2[par][:, dc * 128:(dc + 1) * 128],
                        wq[:, dc * 3072 + gq * 1024 + nh * 512:
                           dc * 3072 + gq * 1024 + nh * 512 + 512],
                        start=(dc == 0), stop=(dc == 7)), waits=w,
                        inc="pe" if last else None)
            ev_qk = em.emit("vector", lambda e, gq=gq: e.tensor_copy(
                qk_t[:, gq * 1024:(gq + 1) * 1024], ps_qk[:, :]), waits=[mm])
            qk_free = ev_qk
            em.emit("vector", lambda e, gq=gq: e.scalar_tensor_tensor(
                qksq[:, gq * 1024:(gq + 1) * 1024], qk_t[:, gq * 1024:(gq + 1) * 1024],
                cval(g, 1.0), qk_t[:, gq * 1024:(gq + 1) * 1024], ALU.mult, ALU.mult))
        sq_ev = em.engine_done_ev("vector")
        for h in range(H):
            b = h // 8
            last = h % 8 == 7 and b == 1
            mm = em.emit("tensor", lambda e, h=h, b=b, t=t: e.matmul(
                ps_sim[b][0:64, (h % 8) * 64:(h % 8) * 64 + 64],
                qk_t[:, h * 64:h * 64 + 64],
                qk_t[:, 1024 + h * 64:1024 + h * 64 + 64],
                start=(t == 0 and h % 8 == 0), stop=(t == NT - 1),
                skip_group_check=True), waits=[sq_ev],
                inc="pe" if last and t == NT - 1 else None)
            if last and t == NT - 1:
                sim_last = mm
        for p in range(4):
            pm = em.emit("tensor", lambda e, p=p: e.matmul(
                ps_ss[:, 0:512], ones128[:, :], qksq[:, p * 512:(p + 1) * 512],
                start=True, stop=True), waits=[sq_ev, ss_free], inc="pe")
            if t == 0:
                ae = em.emit("vector", lambda e, p=p: e.tensor_copy(
                    g["ss_sb"][0:1, p * 512:(p + 1) * 512], ps_ss[0:1, 0:512]),
                    waits=[pm])
            else:
                ae = em.emit("vector", lambda e, p=p: e.tensor_add(
                    g["ss_sb"][0:1, p * 512:(p + 1) * 512],
                    g["ss_sb"][0:1, p * 512:(p + 1) * 512], ps_ss[0:1, 0:512]),
                    waits=[pm])
            ss_free = ae
        ss_add_ev = ae
        for half in range(2):
            for ci in range(4):
                c = half * 4 + ci
                for dc in range(8):
                    w = [xe, wready, qk_free]
                    last = dc == 7 and ci == 3
                    mm = em.emit("tensor", lambda e, c=c, ci=ci, half=half, dc=dc, par=par: e.matmul(
                        ps_qk[:, half * 512 + ci * 128: half * 512 + ci * 128 + 128],
                        wq[:, dc * 3072 + 2048 + c * 128: dc * 3072 + 2048 + (c + 1) * 128],
                        xnf2[par][:, dc * 128:(dc + 1) * 128],
                        start=(dc == 0 and ci == 0),
                        stop=(dc == 7), skip_group_check=True), waits=w,
                        inc="pe" if last else None)
            vo = em.emit("vector", lambda e, half=half, t=t: e.tensor_copy(
                v_fm[:, :].rearrange("p (c n) -> p c n", c=8)
                [:, half * 4:(half + 1) * 4, t * 128:(t + 1) * 128],
                ps_qk[:, half * 512:(half + 1) * 512].rearrange(
                    "p (c n) -> p c n", c=4)), waits=[mm])
            qk_free = vo
        g["xnf_free"][par] = em.engine_done_ev("tensor")
    srow = g["ss_sb"]
    lnr = em.emit("scalar", lambda e: e.activation(
        srow[0:1, 0:2048], srow[0:1, 0:2048], AF.Ln), waits=[ss_add_ev])
    sqr = em.emit("scalar", lambda e: e.activation(
        srow[0:1, 0:2048], srow[0:1, 0:2048], AF.Exp, scale=FP(0.5)), waits=[lnr])
    em.emit("vector", lambda e: e.tensor_scalar_max(
        srow[0:1, 0:2048], srow[0:1, 0:2048], cval(g, 1e-12)[0:1, :]),
        waits=[sqr])
    em.emit("vector", lambda e: e.reciprocal(
        srow[0:1, 0:2048], srow[0:1, 0:2048]))
    for h in range(H):
        ve_now = em.ev("ve")
        pt = em.emit("tensor", lambda e, h=h: e.transpose(
            ps_ss[0:64, 32:33], g["ss_sb"][0:1, h * 64:(h + 1) * 64],
            g["identf"][0:1, 0:1]),
            waits=[(("ve"), ve_now[1]), g["ps_ss_free"]], inc="pe")
        ce = em.emit("vector", lambda e, h=h: e.tensor_copy(
            g["sqcol"][0:64, h:h + 1], ps_ss[0:64, 32:33]), waits=[pt])
        g["ps_ss_free"] = ce
    em.emit("vector", lambda e: e.tensor_mul(
        g["sqcol"][0:64, 0:16], g["sqcol"][0:64, 0:16], g["expT_bc"][:, :]))
    sqc_ev = em.emit("vector", lambda e: e.tensor_scalar_mul(
        g["sqcol"][0:64, 0:16], g["sqcol"][0:64, 0:16], cval(g, 8.0)[0:64, :]))
    pv_free = em.ev("ve")
    at_free = em.ev("ve")
    for h in range(H):
        b, col = h // 8, (h % 8) * 64
        sim_ap = ps_sim[b][0:64, col:col + 64]
        pb = em.emit("tensor", lambda e, h=h: e.matmul(
            ps_ss[0:64, 64:128], g["ones_row_f"][0:1, 0:64],
            g["ss_sb"][0:1, 1024 + h * 64:1024 + (h + 1) * 64],
            start=True, stop=True), waits=[sqc_ev, g["ps_ss_free"]], inc="pe")
        cb = em.emit("vector", lambda e: e.tensor_copy(
            g["skb"][0:64, 0:64], ps_ss[0:64, 64:128]), waits=[pb])
        g["ps_ss_free"] = cb
        em.emit("vector", lambda e, h=h, sim_ap=sim_ap: e.tensor_scalar(
            g["simf"][0:64, 0:64], sim_ap, g["sqcol"][0:64, h:h + 1], None,
            ALU.mult), waits=[sim_last, sqc_ev])
        em.emit("vector", lambda e: e.tensor_mul(
            g["simf"][0:64, 0:64], g["simf"][0:64, 0:64], g["skb"][0:64, 0:64]))
        mx = em.emit("vector", lambda e: e.reduce_max(
            g["mxc"][0:64, 0:1], g["simf"][0:64, 0:64], axis=AX.X, negate=True))
        ex = em.emit("scalar", lambda e: e.activation(
            g["simf"][0:64, 0:64], g["simf"][0:64, 0:64], AF.Exp,
            bias=g["mxc"][0:64, 0:1], accum_out=g["mxc"][0:64, 1:2]),
            waits=[mx])
        iv = em.emit("vector", lambda e: e.reciprocal(
            g["mxc"][0:64, 2:3], g["mxc"][0:64, 1:2]), waits=[ex])
        nm = em.emit("vector", lambda e: e.tensor_scalar(
            g["attn_bf"][0:64, 0:64], g["simf"][0:64, 0:64], g["mxc"][0:64, 2:3],
            None, ALU.mult), waits=[at_free])
        ro = (h % 2) * 64
        ttp = (0, 64) if ro else None
        ptr = em.emit("tensor", lambda e, ro=ro, ttp=ttp: e.transpose(
            g["ps_tr"][0][ro:ro + 64, 0:64], g["attn_bf"][0:64, 0:64],
            g["ident"][0:64, 0:64], tile_position=ttp),
            waits=[nm, g["ps_tr_free"][0]], inc="pe")
        ct = em.emit("vector", lambda e, ro=ro: e.tensor_copy(
            g["attnT"][ro:ro + 64, 0:64], g["ps_tr"][0][ro:ro + 64, 0:64]),
            waits=[ptr])
        g["ps_tr_free"][0] = ct
        at_free = ct
        for nchunk in range(4):
            tp = (64, 64) if ro else None
            pv = em.emit("tensor", lambda e, h=h, nchunk=nchunk, tp=tp, ro=ro: e.matmul(
                ps_pv[ro:ro + 64, 0:512],
                g["attnT"][ro:ro + 64, 0:64],
                v_fm[ro:ro + 64,
                     (h // 2) * 2048 + nchunk * 512:(h // 2) * 2048 + (nchunk + 1) * 512],
                start=True, stop=True, tile_position=tp),
                waits=[ct, pv_free], inc="pe")
            oe = em.emit("vector", lambda e, h=h, nchunk=nchunk, ro=ro: e.tensor_copy(
                o_f[ro:ro + 64,
                    (h // 2) * 2048 + nchunk * 512:(h // 2) * 2048 + (nchunk + 1) * 512],
                ps_pv[ro:ro + 64, 0:512]), waits=[pv])
            pv_free = oe
    of_ev = em.engine_done_ev("vector")
    epi = None
    for t in range(NT):
        for c in range(8):
            for nh in range(2):
                w = [of_ev, wready, qk_free]
                if epi is not None and c == 0 and nh == 0:
                    w.append(epi)
                last = c == 7 and nh == 1
                mm = em.emit("tensor", lambda e, c=c, nh=nh, t=t: e.matmul(
                    ps_qk[:, nh * 512:(nh + 1) * 512],
                    o_f[:, c * 2048 + t * 128: c * 2048 + (t + 1) * 128],
                    wo[:, c * D + nh * 512: c * D + nh * 512 + 512],
                    start=(c == 0), stop=(c == 7)), waits=w,
                    inc="pe" if last else None)
        epi = em.emit("vector", lambda e, t=t: e.scalar_tensor_tensor(
            g["x_res"][:, t * D:(t + 1) * D], ps_qk[:, :], cval(g, 1.0),
            g["x_res"][:, t * D:(t + 1) * D], ALU.mult, ALU.add), waits=[mm])
    return epi


def emit_l0attn_alloc(g, em, psb, pps):
    g["ps_S"] = pps("ps_S", [128, 2048])[:, :]
    g["ps_tr"] = (pps("ps_tr0", [128, 512], BF16), pps("ps_tr1", [128, 512], BF16))
    g["ps_tr_free"] = [em.ev("pe"), em.ev("pe")]
    g["ps_o"] = pps("ps_o", [128, 128])[:, :]
    g["ps_pr"] = pps("ps_pr", [128, 512])[:, :]
    g["wqkv_sb"] = psb("wqkv_sb", [128, 8 * 3072], BF16)
    g["xnf_dc"] = psb("xnf_dc", [128, NT * D], BF16)[:, :]
    g["q_sb"] = psb("q_sb", [128, 2048], BF16)
    g["k_sb"] = psb("k_sb", [128, 2048], BF16)
    g["v_sb"] = psb("v_sb", [128, 2048], BF16)
    g["embr_sb"] = psb("embr_sb", [128, 2048], BF16)
    g["PR_sb"] = psb("PR_sb", [128, 2048], BF16)
    g["A_sb"] = psb("A_sb", [128, 2048], BF16)
    g["At_sb"] = psb("At_sb", [128, 2048], BF16)
    g["Aadd"] = (psb("AaddA", [128, 2048], BF16), psb("AaddB", [128, 2048], BF16))
    g["o_st"] = psb("o_st", [128, 2048], BF16)
    g["o_ld"] = (psb("o_ldA", [128, 1024], BF16), psb("o_ldB", [128, 1024], BF16))
    g["negt"] = psb("negt", [128, 512], BF16)
    g["qb8"] = psb("qb8", [128, 8])
    g["obrow"] = psb("obrow", [1, 1024], BF16)
    g["mxl"] = psb("mxl", [128, 4])
    g["pr_pos"] = psb("pr_pos", [128, 2048], BF16)
    g["pr_sins"] = psb("pr_sins", [128, 1024], BF16)
    g["pr_stage"] = psb("pr_stage", [128, 512], BF16)
    g["wo0_sb"] = g["xnf_dc"][:, 0:8 * D]


def emit_l0attn(em, g, do_prep):
    stats = emit_ln_stats(em, g)
    emit_scale_chain_ln(em, g, stats)
    wq = g["wqkv_sb"]
    xnf = g["xnf_dc"]
    q_sb, k_sb, v_sb = g["q_sb"], g["k_sb"], g["v_sb"]
    ps_S, ps_pr, ps_o = g["ps_S"], g["ps_pr"], g["ps_o"]
    A_sb, At_sb, PR_sb = g["A_sb"], g["At_sb"], g["PR_sb"]
    Aadd = g["Aadd"]
    ident = g["ident"]
    CQ = float(C_QKV)
    R = 2560

    ng = em.emit("gpsimd", lambda e: e.memset(g["negt"][:, :], NEG))
    dW = em.sem("dL0A")
    for dc in range(8):
        em.dma("gpsimd", wq[:, dc * 3072:(dc + 1) * 3072],
               g["in_qkv"][dc * 128:(dc + 1) * 128, :], inc=dW)
    wready = em.ev(dW)
    qld = em.dma("sync", g["qb8"][:, :], g["in_qb"], inc=em.sem("dQB"))
    obld = em.dma("gpsimd", g["obrow"][0:1, :], g["in_ob"], inc=em.sem("dOB"))
    em.emit("vector", lambda e: e.tensor_scalar_mul(
        g["qb8"][:, :], g["qb8"][:, :], cval(g, 0.125)), waits=[qld])
    qbe = em.emit("vector", lambda e: e.tensor_scalar_mul(
        g["obrow"][0:1, :], g["obrow"][0:1, :], cval(g, 32.0)[0:1, :]),
        waits=[obld], inc="ve")

    dPD = em.sem("dPAD")
    for par in range(2):
        skw = g["skw"][par].rearrange("(p a) -> p a", a=R)
        em.dma("scalar", skw[:, 2048:2560],
               g["negt"][:, :].rearrange("p (a b) -> p a b", a=1)[:, 0, 0:512],
               waits=[ng], inc=dPD)
    pad_ev = em.ev(dPD)

    if do_prep:
        dE, dSin, dEW = em.sem("dEM"), em.sem("dSin"), em.sem("dEW")
        for fc in range(2):
            em.dma("gpsimd", g["pr_pos"][:, fc * 1024:(fc + 1) * 1024],
                   g["in_pos"][fc * 128:(fc + 1) * 128, :], inc=dE)
        posev = em.ev(dE)
        sin_free = em.ev("pe")
        st_free = em.ev("dEW")
        for jc in range(4):
            s2 = em.dma("gpsimd",
                        g["pr_sins"][:, :].rearrange("p (c n) -> p c n", c=2),
                        g["in_sins"].rearrange("(c p) n -> p c n", c=2)
                        [:, :, jc * 512:(jc + 1) * 512],
                        waits=[sin_free], inc=dSin)
            for hp in range(8):
                for fc in range(2):
                    mm = em.emit("tensor", lambda e, hp=hp, fc=fc: e.matmul(
                        ps_pr[:, :],
                        g["pr_pos"][:, fc * 1024 + hp * 128: fc * 1024 + (hp + 1) * 128],
                        g["pr_sins"][:, fc * 512:(fc + 1) * 512],
                        start=(fc == 0), stop=(fc == 1)),
                        waits=[s2, posev, g.get("pr_free")],
                        inc="pe" if fc == 1 else None)
                ev = em.emit("vector", lambda e: e.tensor_copy(
                    g["pr_stage"][:, :], ps_pr[:, :]),
                    waits=[mm, st_free])
                g["pr_free"] = ev
                st_free = em.dma("scalar",
                                 g["embr_s"].rearrange("(hp p n) -> hp p n", hp=8, p=128)
                                 [hp, :, jc * 512:(jc + 1) * 512],
                                 g["pr_stage"][:, :], waits=[ev], inc=dEW)
            sin_free = em.engine_done_ev("tensor")
        g["embr_ready"] = em.ev("dEW")

    xn_evs = []
    for t in range(NT):
        xn_evs.append(emit_xn_tile(em, g, t, xnf, dc_major=True))
    xn_all = xn_evs[-1]

    dEL = em.sem("dEL")
    dSW = [em.sem("dSW0"), em.sem("dSW1")]
    dSR = [em.sem("dSR0"), em.sem("dSR1")]
    dOS = em.sem("dOS")
    embr_free = em.ev("pe")
    pair_done = em.ev("pe")
    exp_prev = None
    aadd_used = [None, None]
    g["o_st_free"] = em.ev("dOS")
    for hp in range(DBG.get("l0a_pairs", 8)):
        eld = em.dma("sync", g["embr_sb"][:, :],
                     g["embr_s"].rearrange("(hp p n) -> hp p n", hp=8, p=128)[hp, :, :],
                     waits=[g["embr_ready"], embr_free], inc=dEL)
        pw = [wready, xn_all, pair_done]
        qk_ev = {}
        for which, base, scale in (("q", hp * 128, 0.125 * CQ),
                                   ("k", 1024 + hp * 128, CQ)):
            dst = q_sb if which == "q" else k_sb
            for tc in range(4):
                bank = (tc % 2) * 512
                w = list(pw) + [qk_ev.get((which, tc - 2)), qbe, exp_prev]
                for dc in range(8):
                    mm = em.emit("tensor",
                            lambda e, dc=dc, tc=tc, base=base, bank=bank: e.matmul(
                                ps_S[:, bank:bank + 512],
                                wq[:, dc * 3072 + base: dc * 3072 + base + 128],
                                xnf[:, dc * 2048 + tc * 512: dc * 2048 + (tc + 1) * 512],
                                start=(dc == 0), stop=(dc == 7),
                                skip_group_check=True),
                            waits=w, inc="pe" if dc == 7 else None)
                if which == "q":
                    ev = em.emit("scalar", lambda e, tc=tc, bank=bank, hp=hp, dst=dst: e.activation(
                        dst[:, tc * 512:(tc + 1) * 512], ps_S[:, bank:bank + 512],
                        AF.Identity, scale=FP(0.125 * CQ),
                        bias=g["qb8"][:, hp:hp + 1]), waits=[mm])
                else:
                    ev = em.emit("scalar", lambda e, tc=tc, bank=bank, dst=dst: e.activation(
                        dst[:, tc * 512:(tc + 1) * 512], ps_S[:, bank:bank + 512],
                        AF.Identity, scale=FP(CQ)), waits=[mm])
                qk_ev[(which, tc)] = ev
        for tg in range(4):
            bank = 1024 + (tg % 2) * 512
            w = list(pw) + [qk_ev.get(("v", tg - 2)), exp_prev]
            for ti in range(4):
                t = tg * 4 + ti
                for dc in range(8):
                    mm = em.emit("tensor",
                            lambda e, dc=dc, ti=ti, t=t, bank=bank, hp=hp: e.matmul(
                                ps_S[:, bank + ti * 128: bank + ti * 128 + 128],
                                xnf[:, dc * 2048 + t * 128: dc * 2048 + (t + 1) * 128],
                                wq[:, dc * 3072 + 2048 + hp * 128:
                                   dc * 3072 + 2048 + (hp + 1) * 128],
                                start=(dc == 0 and ti == 0), stop=(dc == 7),
                                skip_group_check=True),
                            waits=w, inc="pe" if (dc == 7 and ti == 3) else None)
            ev = em.emit("scalar", lambda e, tg=tg, bank=bank: e.activation(
                v_sb[:, tg * 512:(tg + 1) * 512], ps_S[:, bank:bank + 512],
                AF.Identity, scale=FP(CQ)), waits=[mm])
            qk_ev[("v", tg)] = ev
        qke = em.engine_done_ev("scalar")

        for hodd in range(DBG.get("l0a_heads", 2)):
            h = 2 * hp + hodd
            ro = hodd * 64
            rtp = (64, 0) if hodd else None

            def emit_pr(j, qke=qke, eld=eld, ro=ro, rtp=rtp, hp=hp):
                jmin = 1920 - 128 * j
                width = 2048 - jmin
                par = j % 2
                npc = (width + 511) // 512
                prev = None
                for pc in range(npc):
                    c0 = jmin + pc * 512
                    cw = min(512, 2048 - c0)
                    mm = em.emit("tensor", lambda e, j=j, c0=c0, cw=cw, ro=ro, rtp=rtp: e.matmul(
                        ps_pr[:, 0:cw],
                        q_sb[ro:ro + 64, j * 128:(j + 1) * 128],
                        g["embr_sb"][ro:ro + 64, c0:c0 + cw],
                        start=True, stop=True, tile_position=rtp),
                        waits=[qke, eld, g.get("pr_free")], inc="pe")
                    prev = em.emit("vector", lambda e, c0=c0, cw=cw: e.tensor_copy(
                        PR_sb[:, c0:c0 + cw], ps_pr[:, 0:cw]),
                        waits=[mm, g.get("prsb_free")])
                    g["pr_free"] = prev
                skw = g["skw"][par].rearrange("(p a) -> p a", a=R)
                wrev = em.dma("scalar", skw[:, jmin:2048], PR_sb[:, jmin:2048],
                              waits=[prev, pad_ev,
                                     g.get(f"skr_done{par}")], inc=dSW[par])
                g["prsb_free"] = wrev
                W = 128 * j + 128
                Wc = ((W + 511) // 512) * 512
                bview = g["skw"][par][0:128 * (R - 1)].rearrange("(p a) -> p a", a=R - 1)
                rdev = em.dma("sync", Aadd[par][:, 0:Wc],
                              bview[:, 2047 - 128 * j: 2047 - 128 * j + Wc],
                              waits=[em.ev(dSW[par]), aadd_used[par]], inc=dSR[par])
                g[f"skr_done{par}"] = em.ev(dSR[par])
                return rdev

            rd = {0: emit_pr(0), 1: emit_pr(1)}
            pend = None

            def do_tpv(i, nm):
                nkc = i + 1
                for kb in range((nkc + 3) // 4):
                    nj = min(4, nkc - kb * 4)
                    for jj in range(nj):
                        kc = kb * 4 + jj
                        pt = em.emit("tensor", lambda e, kb=kb, jj=jj, kc=kc: e.transpose(
                            g["ps_tr"][kb % 2][:, jj * 128:(jj + 1) * 128],
                            A_sb[:, kc * 128:(kc + 1) * 128], ident[:, :]),
                            waits=[nm, g["ps_tr_free"][kb % 2]],
                            inc="pe" if jj == nj - 1 else None)
                    te = em.emit("scalar", lambda e, kb=kb, nj=nj: e.copy(
                        At_sb[:, kb * 512:kb * 512 + nj * 128],
                        g["ps_tr"][kb % 2][:, 0:nj * 128]), waits=[pt])
                    g["ps_tr_free"][kb % 2] = te
                g["asb_free"] = em.engine_done_ev("tensor")
                ate = em.engine_done_ev("scalar")
                for kc in range(nkc):
                    otp = (0, 64) if hodd else None
                    mm = em.emit("tensor", lambda e, kc=kc, ro=ro, otp=otp, i=i: e.matmul(
                        ps_o[ro:ro + 64, 0:128],
                        v_sb[:, kc * 128 + ro: kc * 128 + ro + 64],
                        At_sb[:, kc * 128:(kc + 1) * 128],
                        start=(kc == 0), stop=(kc == nkc - 1),
                        skip_group_check=True, tile_position=otp),
                        waits=[ate, g.get("po_free")], inc="pe" if kc == nkc - 1 else None)
                oe = em.emit("vector", lambda e, ro=ro, i=i: e.tensor_copy(
                    g["o_st"][ro:ro + 64, i * 128:(i + 1) * 128],
                    ps_o[ro:ro + 64, 0:128]), waits=[mm, g["o_st_free"]])
                g["po_free"] = oe

            for i in range(DBG.get("l0a_tiles", NT)):
                W = 128 * i + 128
                Wc = ((W + 511) // 512) * 512
                nch = Wc // 512
                par = i % 2
                for c in range(nch):
                    em.emit("tensor", lambda e, i=i, c=c, ro=ro, rtp=rtp: e.matmul(
                        ps_S[:, c * 512:(c + 1) * 512],
                        q_sb[ro:ro + 64, i * 128:(i + 1) * 128],
                        k_sb[ro:ro + 64, c * 512:(c + 1) * 512],
                        start=True, stop=False, skip_group_check=True,
                        tile_position=rtp),
                        waits=[qke, exp_prev])
                    last = c == nch - 1
                    mm = em.emit("tensor", lambda e, i=i, c=c, par=par: e.matmul(
                        ps_S[:, c * 512:(c + 1) * 512], ident[:, :],
                        Aadd[par][:, c * 512:(c + 1) * 512],
                        start=False, stop=True, skip_group_check=True),
                        waits=[rd[i]], inc="pe" if last else None)
                aadd_used[par] = mm
                if i + 2 < DBG.get("l0a_tiles", NT):
                    rd[i + 2] = emit_pr(i + 2)
                if pend is not None:
                    do_tpv(*pend)
                mx = em.emit("vector", lambda e, Wc=Wc: e.reduce_max(
                    g["mxl"][:, 0:1], ps_S[:, 0:Wc], axis=AX.X, negate=True),
                    waits=[mm])
                ex = em.emit("scalar", lambda e, Wc=Wc: e.activation(
                    A_sb[:, 0:Wc], ps_S[:, 0:Wc], AF.Exp,
                    bias=g["mxl"][:, 0:1], accum_out=g["mxl"][:, 1:2]),
                    waits=[mx, g.get("asb_free")])
                exp_prev = ex
                em.emit("vector", lambda e: e.reciprocal(
                    g["mxl"][:, 2:3], g["mxl"][:, 1:2]), waits=[ex])
                nm = em.emit("vector", lambda e, Wc=Wc: e.tensor_scalar(
                    A_sb[:, 0:Wc], A_sb[:, 0:Wc], g["mxl"][:, 2:3], None,
                    ALU.mult))
                pend = (i, nm)
            do_tpv(*pend)
        ose = em.dma("scalar",
                     g["o_s"].rearrange("(hp p n) -> hp p n", hp=8, p=128)[hp, :, :],
                     g["o_st"][:, :], waits=[em.ev("ve")], inc=dOS)
        g["o_st_free"] = ose
        embr_free = em.engine_done_ev("tensor")
        pair_done = embr_free
    if DBG.get("dump"):
        lastall = [em.engine_done_ev(e) for e in ("vector", "scalar", "tensor")]
        em.dma("gpsimd", g["dbg1"][:, 0:2048], q_sb[:, :], waits=lastall, inc="dDB")
        em.dma("gpsimd", g["dbg1"][:, 2048:4096], k_sb[:, :], waits=lastall, inc="dDB")
        em.dma("gpsimd", g["dbg2"][:, 0:2048], Aadd[0][:, :], waits=lastall, inc="dDB")
        em.dma("gpsimd", g["dbg2"][:, 2048:4096], A_sb[:, :], waits=lastall, inc="dDB")
        em.dma("gpsimd", g["dbg3"][:, 0:1024], g["o_st"][:, 0:1024], waits=lastall, inc="dDB")
        em.dma("gpsimd", g["dbg4"][:, 0:4], g["mxl"][:, :], waits=lastall, inc="dDB",
               allow_slow_non_contiguous=True)
        em.dma("gpsimd", g["dbg3"][:, 1024:3072], g["embr_sb"][:, :], waits=lastall, inc="dDB")
        em.dma("gpsimd", g["dbg5"],
               g["skw"][0].rearrange("(p a) -> p a", a=2560),
               waits=lastall, inc="dDB")
    dWO = em.sem("dWO")
    for c in range(8):
        em.dma("gpsimd", g["wo0_sb"][:, c * D:(c + 1) * D],
               g["in_wo0"][c * 128:(c + 1) * 128, :],
               waits=[pair_done], inc=dWO)
    wo_ready = em.ev(dWO)
    dOL = em.sem("dOL")
    os_view = g["o_s"].rearrange("(hp p n) -> hp p n", hp=8, p=128).transpose([1, 0, 2])
    epi = None
    old_ev = [None, None]
    for t in range(NT):
        par = t % 2
        lw = [em.ev(dOS), old_ev[par]]
        lde = em.dma("sync", g["o_ld"][par][:, :].rearrange("p (hp n) -> p hp n", hp=8),
                     os_view[:, :, t * 128:(t + 1) * 128], waits=lw, inc=dOL)
        for c in range(8):
            for nh in range(2):
                w = [lde, wo_ready, qbe, exp_prev]
                if epi is not None and c == 0 and nh == 0:
                    w.append(epi)
                em.emit("tensor", lambda e, c=c, nh=nh, par=par, t=t: e.matmul(
                    ps_S[:, par * 1024 + nh * 512: par * 1024 + nh * 512 + 512],
                    g["o_ld"][par][:, c * 128:(c + 1) * 128],
                    g["wo0_sb"][:, c * D + nh * 512: c * D + nh * 512 + 512],
                    start=(c == 0), stop=False, skip_group_check=True), waits=w)
        for nh in range(2):
            mm = em.emit("tensor", lambda e, nh=nh, par=par: e.matmul(
                ps_S[:, par * 1024 + nh * 512: par * 1024 + nh * 512 + 512],
                g["ones_row_bf"][0:1, :],
                g["obrow"][0:1, nh * 512:(nh + 1) * 512],
                start=False, stop=True, skip_group_check=True),
                waits=[qbe], inc="pe" if nh == 1 else None)
        old_ev[par] = mm
        epi = em.emit("vector", lambda e, t=t, par=par: e.scalar_tensor_tensor(
            g["x_res"][:, t * D:(t + 1) * D], ps_S[:, par * 1024: par * 1024 + 1024],
            cval(g, 0.03125), g["x_res"][:, t * D:(t + 1) * D],
            ALU.mult, ALU.add), waits=[mm])
    return epi


def build_consts(em, g):
    for i, v in enumerate(CVALS):
        em.emit("gpsimd", lambda e, i=i, v=v: e.memset(g["cc"][:, i:i + 1], v))
    mz = em.emit("gpsimd", lambda e: e.memset(g["ident"][:, :], 0.0), inc="po")
    em.emit("gpsimd", lambda e: e.affine_select(
        out=g["ident"][:, :], in_=g["ident"][:, :], compare_op=ALU.not_equal,
        fill=1.0, base=0, pattern=[[-1, 128]], channel_multiplier=1), waits=[mz])
    em.emit("gpsimd", lambda e: e.memset(g["ones_col_bf"][:, :], 1.0))
    em.emit("gpsimd", lambda e: e.memset(g["ones_row_bf"][:, :], 1.0))
    em.emit("gpsimd", lambda e: e.memset(g["ones_col_f"][:, :], 1.0))
    em.emit("gpsimd", lambda e: e.memset(g["ones_row_f"][:, :], 1.0), inc="po")
    return em.ev("po")


def build_program(phases, unfused_io=True):
    nc = bass.Bass("TRN2", target_bir_lowering=False, detect_race_conditions=DETECT_RACES)
    g = {"nc": nc}

    x_in = nc.dram_tensor("x_in", [S, D], F32, kind="ExternalInput").ap()
    x_out = nc.dram_tensor("x_out", [S, D], F32, kind="ExternalOutput").ap()
    if DBG.get("dump"):
        g["dbg1"] = nc.dram_tensor("dbg1", [128, 4096], F32, kind="ExternalOutput").ap()
        g["dbg2"] = nc.dram_tensor("dbg2", [128, 4096], F32, kind="ExternalOutput").ap()
        g["dbg3"] = nc.dram_tensor("dbg3", [128, 3072], F32, kind="ExternalOutput").ap()
        g["dbg4"] = nc.dram_tensor("dbg4", [128, 16], F32, kind="ExternalOutput").ap()
        g["dbg5"] = nc.dram_tensor("dbg5", [128, 2560], F32, kind="ExternalOutput").ap()
    need_l0ffn = any(p.startswith("l0ffn") for p in phases)
    if "l0attn" in phases:
        g["in_qkv"] = nc.dram_tensor("qkv_r", [D, 3 * D], F32, kind="ExternalInput").ap()
        g["in_qb"] = nc.dram_tensor("qb_r", [128, 8], F32, kind="ExternalInput").ap()
        g["in_pos"] = nc.dram_tensor("pos_r", [256, D], F32, kind="ExternalInput").ap()
        g["in_sins"] = nc.dram_tensor("sins_r", [256, S], F32, kind="ExternalInput").ap()
        g["in_wo0"] = nc.dram_tensor("wo0", [D, D], F32, kind="ExternalInput").ap()
        g["in_ob"] = nc.dram_tensor("ob_r", [1, D], F32, kind="ExternalInput").ap()
        g["embr_s"] = nc.dram_tensor("embr_s", [8 * 128 * S], BF16, kind="Internal").ap()
        g["skw"] = (nc.dram_tensor("skw0", [128 * 2560], BF16, kind="Internal").ap(),
                    nc.dram_tensor("skw1", [128 * 2560], BF16, kind="Internal").ap())
        g["o_s"] = nc.dram_tensor("o_s", [8 * 128 * S], BF16, kind="Internal").ap()
    if "l1attn" in phases:
        g["in_wqkv"] = nc.dram_tensor("wqkv_g", [D, 3 * D], F32, kind="ExternalInput").ap()
        g["in_wo"] = nc.dram_tensor("wo1", [D, D], F32, kind="ExternalInput").ap()
        g["in_temp"] = nc.dram_tensor("temp_r", [16], F32, kind="ExternalInput").ap()
        g["in_gamma"] = nc.dram_tensor("gamma_r", [128, 8], F32, kind="ExternalInput").ap()
    if any(p.startswith("l1ffn") for p in phases):
        g["in_wg"] = nc.dram_tensor("wg_t", [D, FF1], I32, kind="ExternalInput").ap()
        g["in_wu"] = nc.dram_tensor("wu_t", [D, FF1], I32, kind="ExternalInput").ap()
        g["in_wd"] = nc.dram_tensor("wd_t", [FF1, D], I32, kind="ExternalInput").ap()
    if need_l0ffn:
        g["in_w_up"] = nc.dram_tensor("w_up_t", [D, FF0], F32, kind="ExternalInput").ap()
        g["in_w_dn"] = nc.dram_tensor("w_dn_t", [FF0, D], F32, kind="ExternalInput").ap()
        g["scr_wup"] = nc.dram_tensor("scr_wup", [D, FF0], BF16, kind="Internal").ap()
        g["scr_wdn"] = nc.dram_tensor("scr_wdn", [FF0, D], BF16, kind="Internal").ap()

    with ExitStack() as ctx:
        em = Em(nc, ctx)
        for s in ("pe", "ve", "se", "po", "dX"):
            em.sem(s)

        def sb(name, shape, dtype=F32):
            return ctx.enter_context(nc.sbuf_tensor(name, shape, dtype))

        g["x_res"] = sb("x_res", [128, NT * D])
        g["ident"] = sb("ident", [128, 128], BF16)
        g["ones_col_bf"] = sb("ones_col_bf", [128, 1], BF16)
        g["ones_row_bf"] = sb("ones_row_bf", [1, 128], BF16)
        g["ones_col_f"] = sb("ones_col_f", [128, 1])
        g["ones_row_f"] = sb("ones_row_f", [1, 128])
        g["bn12"] = sb("bn12", [128, 12])
        g["mv"] = sb("mv", [128, 2 * NT])
        g["scol"] = sb("scol", [128, NT])
        g["bcol"] = sb("bcol", [128, NT])
        g["c16a"] = sb("c16a", [128, NT])
        g["c16b"] = sb("c16b", [128, NT])
        g["scratch8"] = sb("scratch8", [128, 8])
        g["nsc"] = sb("nsc", [128, 4])
        g["cc"] = sb("cc", [128, len(CVALS)])
        g["shared2k"] = sb("shared2k", [128, 512])[:, :]
        g["xt_bf"] = sb("xt_bf", [128, D], BF16)[:, :]

        dX = "dX"
        for t in range(NT):
            em.dma("sync", g["x_res"][:, t * D:(t + 1) * D],
                   x_in[t * 128:(t + 1) * 128, :], inc=dX)
        xall = em.ev(dX)
        for e in ENGINES:
            em.emit(e, lambda eng: None, waits=[xall], real=False)

        cev = build_consts(em, g)
        g["consts_ev"] = cev
        for e in ("tensor", "vector", "scalar"):
            em.emit(e, lambda eng: None, waits=[cev], real=False)

        first_l0ffn_done = False
        for phase_i, ph in enumerate(phases):
            with ExitStack() as pctx:
                def psb(name, shape, dtype=F32, phase_i=phase_i):
                    return pctx.enter_context(
                        nc.sbuf_tensor(f"p{phase_i}_{name}", shape, dtype))

                def pps(name, shape, dtype=F32, phase_i=phase_i):
                    return pctx.enter_context(
                        nc.psum_tensor(f"p{phase_i}_{name}", shape, dtype))

                if ph.startswith("l0ffn"):
                    g["ps_tr"] = (pps("ps_tr0", [128, 512], BF16),
                                  pps("ps_tr1", [128, 512], BF16))
                    g["ps_tr_free"] = [em.ev("pe"), em.ev("pe")]
                    g["ps_up"] = (pps("ps_up0", [128, 512]), pps("ps_up1", [128, 512]))
                    g["ps_y"] = (pps("ps_y0", [128, 1024]), pps("ps_y1", [128, 1024]))
                    g["wup_sb"] = psb("wup_sb", [128, 8 * FF0], BF16)
                    g["wdn_sb"] = psb("wdn_sb", [128, 32 * D], BF16)
                    g["xnf2"] = (psb("xnf0", [128, D], BF16), psb("xnf1", [128, D], BF16))
                    g["hbf"] = (psb("hbf0", [128, 512], BF16), psb("hbf1", [128, 512], BF16))
                    g["htf"] = (psb("htf0", [128, 512], BF16), psb("htf1", [128, 512], BF16))
                    g["qacc"] = psb("qacc", [128, 8])
                    do_prep = not first_l0ffn_done
                    if do_prep:
                        g["ps_b"] = g["ps_up"][1]
                        g["ps_b_free"] = em.ev("pe")
                        g["stg_w"] = g["shared2k"]
                        g["stg_w2"] = g["xt_bf"].bitcast(F32)
                        g["tern_acc"] = g["nsc"]
                        g["thr_row"] = psb("thr_row", [1, 1])
                        g["thr_col"] = psb("thr_col", [128, 1])
                        g["nthr_col"] = psb("nthr_col", [128, 1])
                        g["tern_sgn"] = g["xnf2"][0]
                        g["tern_gt"] = g["xnf2"][1]
                        g["tern_stg_free"] = [em.ev("ve")] * 4
                        g["tern_sgn_free"] = [em.ev("ve")] * 4
                        g["write_scratch"] = True
                    emit_l0ffn(em, g, do_prep=do_prep,
                               from_scratch=first_l0ffn_done)
                    first_l0ffn_done = True
                elif ph.startswith("l1ffn"):
                    g["ps_tr"] = (pps("ps_tr0", [128, 512], BF16),
                                  pps("ps_tr1", [128, 512], BF16))
                    g["ps_tr_free"] = [em.ev("pe"), em.ev("pe")]
                    g["ps_gu"] = ((pps("ps_g0", [128, 512]), pps("ps_g1", [128, 512])),
                                  (pps("ps_u0", [128, 512]), pps("ps_u1", [128, 512])))
                    g["ps_y"] = pps("ps_y", [128, 1024])[:, :]
                    g["wg_sb"] = psb("wg_sb", [128, 8 * (FF1 // 2)], BF16)
                    g["wu_sb"] = psb("wu_sb", [128, 8 * (FF1 // 2)], BF16)
                    g["wd_sb"] = psb("wd_sb", [128, 12 * D], BF16)
                    g["xnf_all"] = psb("xnf_all", [128, NT * D], BF16)
                    g["hbf"] = (psb("hbf0", [128, 512], BF16), psb("hbf1", [128, 512], BF16))
                    g["htf"] = (psb("htf0", [128, 512], BF16), psb("htf1", [128, 512], BF16))
                    g.pop("l1w_free", None)
                    emit_l1ffn(em, g)
                elif ph == "l0attn":
                    emit_l0attn_alloc(g, em, psb, pps)
                    emit_l0attn(em, g, do_prep=not g.get("l0attn_prep_done"))
                    g["l0attn_prep_done"] = True
                elif ph == "l1attn":
                    emit_l1attn_alloc(g, em, psb, pps)
                    emit_l1attn(em, g)
                else:
                    raise NotImplementedError(ph)
                em.barrier()

        dO = em.sem("dO")
        for t in range(NT):
            em.dma("sync", x_out[t * 128:(t + 1) * 128, :],
                   g["x_res"][:, t * D:(t + 1) * D], inc=dO)
        em.emit("sync", lambda e: None, waits=[em.ev("dO")], real=False)
        em.drain_dmas()

        blk = ctx.enter_context(nc.Block())
        em.play(blk)
    return nc



def prep_inputs(inputs):
    d = {}
    d["w_up_t"] = np.ascontiguousarray(inputs["L0_ffn_w_up"].T)
    d["w_dn_t"] = np.ascontiguousarray(inputs["L0_ffn_w_down"].T)
    d["wg_t"] = np.ascontiguousarray(inputs["L1_ffn_w_gate"].T)
    d["wu_t"] = np.ascontiguousarray(inputs["L1_ffn_w_up"].T)
    d["wd_t"] = np.ascontiguousarray(inputs["L1_ffn_w_down"].T)
    HEADS, FREQS, MAX_PERIOD = 16, 256, 10000
    pos = np.arange(S, dtype=np.float64)[:, None]
    idx = np.arange(FREQS)[None, :]
    inv_freq = MAX_PERIOD ** (-(idx // 2 * 2) / FREQS)
    angle = pos * inv_freq
    sins = np.where(idx % 2 == 0, np.sin(angle), np.cos(angle)).astype(np.float32)
    sins = np.float32(np.sqrt(2.0)) * sins
    sins_scaled = sins * np.float32(FREQS ** (-0.5))
    d["sins_r"] = np.ascontiguousarray(sins_scaled[::-1, :].T)
    d["qkv_r"] = np.ascontiguousarray(inputs["L0_attn_qkv"].reshape(D, 3 * D))
    d["qb_r"] = np.ascontiguousarray(inputs["L0_attn_q_bias"].reshape(1024).reshape(8, 128).T)
    d["pos_r"] = np.ascontiguousarray(inputs["L0_attn_positional"].reshape(256, D))
    d["wo0"] = np.ascontiguousarray(inputs["L0_attn_out_kernel"])
    d["ob_r"] = np.ascontiguousarray(inputs["L0_attn_out_bias"].reshape(1, D))
    d["wqkv_g"] = np.ascontiguousarray(inputs["L1_attn_w_qkv"])
    d["wo1"] = np.ascontiguousarray(inputs["L1_attn_w_out"])
    d["temp_r"] = np.ascontiguousarray(inputs["L1_attn_temperature"].reshape(16))
    d["gamma_r"] = np.ascontiguousarray(inputs["L1_attn_gamma"].reshape(8, 128).T)
    return d


def kernel(**inputs):
    x = np.ascontiguousarray(inputs["L0_attn_input"])
    host = prep_inputs(inputs)
    phases = list(ALL_PHASES)
    nc = build_program(phases)
    in_maps = []
    for c in range(8):
        m = {"x_in": np.ascontiguousarray(x[c])}
        m.update(host)
        in_maps.append(m)
    res = run_bass_kernel_spmd(nc, in_maps, core_ids=list(range(8)))
    out = np.stack([r["x_out"] for r in res.results], axis=0)
    return out.astype(np.float32)


# version 05
# speedup vs baseline: 1.3173x; 1.1390x over previous
import math
import numpy as np
import ml_dtypes
from contextlib import ExitStack

import concourse.bass as bass
import concourse.mybir as mybir
from concourse.bass_utils import run_bass_kernel_spmd

F32 = mybir.dt.float32
BF16 = mybir.dt.bfloat16
I32 = mybir.dt.int32
AF = mybir.ActivationFunctionType
ALU = mybir.AluOpType
AX = mybir.AxisListType
FP = np.float32
DETECT_RACES = False

D = 1024
S = 2048
NT = 16
H = 16
HD = 64
FF0 = 4096
FF1 = 3072
C_QKV = float((3.0 * D * HD * H) ** (-0.25))
C_OUT = float((D * D) ** (-0.25))
SM = float(HD ** (-0.5))
NEG = -1e9
SKW = 2176

ENGINES = ("sync", "scalar", "vector", "tensor", "gpsimd")

DBG = {"stats": True, "prep": True, "main_tiles": NT, "fcs": 8, "epi": True, "newton": True}

ALL_PHASES = ("l0ffn_a", "l0attn", "l0ffn_b", "l1ffn_a", "l1attn", "l1ffn_b")


class Em:

    def __init__(self, nc, ctx):
        self.nc = nc
        self.ctx = ctx
        self.q = {e: [] for e in ENGINES}
        self.last_real = {e: None for e in ENGINES}
        self.semh = {}
        self.cnt = {}
        self.lastw = {}
        self.dma_eng = {}

    def sem(self, name):
        if name not in self.semh:
            self.semh[name] = self.ctx.enter_context(self.nc.semaphore(name))
            self.cnt[name] = 0
        return name

    def ev(self, name):
        return (name, self.cnt[name])

    AUTO_SER = {"vector": "ve", "scalar": "se", "gpsimd": "po"}

    def emit(self, eng, fn, waits=(), inc=None, inc_n=1, real=True, _dma=False):
        waits = list(waits)
        if real and not _dma and eng in self.AUTO_SER:
            cs = self.AUTO_SER[eng]
            self.sem(cs)
            waits.append((cs, self.cnt[cs]))
            if inc is None:
                inc = cs
            else:
                assert inc == cs, (eng, inc)
        rw = []
        for w in waits:
            if w is None:
                continue
            s, v = w
            if v <= 0:
                continue
            if self.lastw.get((eng, s), 0) >= v:
                continue
            self.lastw[(eng, s)] = v
            rw.append((s, v))
        tgt = None
        if inc is not None:
            assert real
            self.sem(inc)
            self.cnt[inc] += inc_n
            tgt = (inc, self.cnt[inc], inc_n)
        entry = [rw, fn, tgt, real]
        self.q[eng].append(entry)
        if real:
            self.last_real[eng] = entry
        return self.ev(inc) if inc is not None else None

    def dma(self, eng, out, in_, waits=(), inc=None, **kw):
        if inc is not None:
            self.dma_eng[inc] = eng
        return self.emit(eng, lambda e: e.dma_start(out=out, in_=in_, **kw),
                         waits=waits, inc=inc, inc_n=16, _dma=True)

    def drain_dmas(self):
        for sname, eng in list(self.dma_eng.items()):
            self.emit(eng, lambda e: None, waits=[self.ev(sname)], real=False)

    def engine_done_ev(self, e):
        entry = self.last_real[e]
        if entry is None:
            return None
        if entry[2] is None:
            sname = {"sync": "qS", "scalar": "qA", "vector": "ve", "tensor": "pe",
                     "gpsimd": "po"}[e]
            self.sem(sname)
            self.cnt[sname] += 1
            entry[2] = (sname, self.cnt[sname], 1)
        return (entry[2][0], entry[2][1])

    def barrier(self):
        evs = [self.engine_done_ev(e) for e in ENGINES]
        evs += [self.ev(s) for s in self.dma_eng]
        for e in ENGINES:
            self.emit(e, lambda eng: None, waits=evs, real=False)

    def play(self, blk):
        semh = self.semh

        def run(eng, stream):
            for rw, fn, tgt, _real in stream:
                for (s, v) in rw:
                    eng.wait_ge(semh[s], v)
                inst = fn(eng)
                if tgt is not None and inst is not None:
                    inst.then_inc(semh[tgt[0]], tgt[2])

        for e in ENGINES:
            stream = self.q[e]
            if not stream:
                continue
            getattr(blk, e)(lambda eng, stream=stream: run(eng, stream))


def rsqrt_newton(em, g, out, qcol, scratch, iters=2):
    r = scratch[:, 0:1]
    t = scratch[:, 1:2]
    fa = scratch[:, 2:3]
    em.emit("vector", lambda e: e.tensor_copy(fa, qcol.bitcast(I32)))
    em.emit("vector", lambda e: e.tensor_scalar(
        fa, fa, g["cc"][:, CVALS.index(-0.5):CVALS.index(-0.5) + 1],
        g["cc"][:, CVALS.index(1597463007.0):CVALS.index(1597463007.0) + 1],
        ALU.mult, ALU.add))
    em.emit("vector", lambda e: e.tensor_copy(r.bitcast(I32), fa))
    for _ in range(iters):
        em.emit("vector", lambda e: e.tensor_mul(t, r, r))
        em.emit("vector", lambda e: e.tensor_mul(t, t, qcol))
        em.emit("vector", lambda e: e.tensor_scalar(t, t, g["cc"][:, 4:5], g["cc"][:, 5:6], ALU.mult, ALU.add))
        em.emit("vector", lambda e: e.tensor_mul(r, r, t))
    return em.emit("vector", lambda e: e.tensor_copy(out, r), inc="ve")


def emit_ln_stats(em, g, tiles=range(NT)):
    x = g["x_res"]
    bn = g["bn12"]
    mv = g["mv"]
    evs = []
    for t in tiles:
        xt = x[:, t * D:(t + 1) * D]
        w = [g["xready"][t]] if "xready" in g else []
        em.emit("vector", lambda e, xt=xt: e.bn_stats(bn[:, 0:6], xt[:, 0:512]), waits=w)
        em.emit("vector", lambda e, xt=xt: e.bn_stats(bn[:, 6:12], xt[:, 512:1024]))
        evs.append(em.emit("vector", lambda e, t=t: e.bn_aggr(
            mv[:, 2 * t:2 * t + 2], bn[:, 0:12]), inc="ve"))
    return evs


def emit_scale_chain_l0ffn(em, g, stats_evs):
    mv, sc = g["mv"], g["scratch8"]
    scol, bcol = g["scol"], g["bcol"]
    ra, rb = g["c16a"], g["c16b"]
    for t in range(NT):
        v = mv[:, 2 * t + 1:2 * t + 2]
        lv1 = sc[:, 0:1]
        lv0 = sc[:, 1:2]
        w = [stats_evs[t]]
        em.emit("scalar", lambda e, v=v, lv1=lv1: e.activation(
            lv1, v, AF.Ln, bias=FP(1e-5)), waits=w)
        em.emit("scalar", lambda e, v=v, lv0=lv0: e.activation(lv0, v, AF.Ln))
        em.emit("scalar", lambda e, lv1=lv1, t=t: e.activation(
            ra[:, t:t + 1], lv1, AF.Exp, scale=FP(-0.5)))
        se = em.emit("scalar", lambda e, lv0=lv0, t=t: e.activation(
            rb[:, t:t + 1], lv0, AF.Exp, scale=FP(0.5)),
            inc="se" if t == NT - 1 else None)
    for t in range(NT):
        m = mv[:, 2 * t:2 * t + 1]
        em.emit("vector", lambda e, t=t: e.tensor_mul(
            sc[:, 4:5], rb[:, t:t + 1], ra[:, t:t + 1]), waits=[se])
        em.emit("vector", lambda e: e.tensor_scalar_add(sc[:, 4:5], sc[:, 4:5], cval(g, 1e-8)))
        em.emit("vector", lambda e: e.reciprocal(sc[:, 6:7], sc[:, 4:5]))
        em.emit("vector", lambda e, t=t: e.tensor_mul(
            scol[:, t:t + 1], ra[:, t:t + 1], sc[:, 6:7]))
        em.emit("vector", lambda e, m=m, t=t: e.tensor_mul(
            sc[:, 7:8], m, scol[:, t:t + 1]))
        em.emit("vector", lambda e, t=t: e.tensor_scalar_mul(
            bcol[:, t:t + 1], sc[:, 7:8], cval(g, -1.0)), inc="ve")
    return em.ev("ve")


def emit_scale_chain_ln(em, g, stats_evs, extra=None):
    mv, sc = g["mv"], g["scratch8"]
    scol, bcol = g["scol"], g["bcol"]
    ra, rb = g["c16a"], g["c16b"]
    for t in range(NT):
        v = mv[:, 2 * t + 1:2 * t + 2]
        lv1 = sc[:, 0:1]
        w = [stats_evs[t]]
        em.emit("scalar", lambda e, v=v, lv1=lv1: e.activation(
            lv1, v, AF.Ln, bias=FP(1e-5)), waits=w)
        em.emit("scalar", lambda e, lv1=lv1, t=t: e.activation(
            ra[:, t:t + 1], lv1, AF.Exp, scale=FP(-0.5)))
        if extra == "l1attn":
            lv0 = sc[:, 1:2]
            em.emit("scalar", lambda e, v=v, lv0=lv0: e.activation(lv0, v, AF.Ln))
            em.emit("scalar", lambda e, lv0=lv0, t=t: e.activation(
                rb[:, t:t + 1], lv0, AF.Exp, scale=FP(0.5)))
    se = em.engine_done_ev("scalar")
    for t in range(NT):
        m = mv[:, 2 * t:2 * t + 1]
        if extra == "l1attn":
            em.emit("vector", lambda e, t=t: e.tensor_mul(
                sc[:, 4:5], rb[:, t:t + 1], ra[:, t:t + 1]), waits=[se])
            em.emit("vector", lambda e: e.tensor_scalar_max(
                sc[:, 4:5], sc[:, 4:5], cval(g, 1e-12 / 32.0)))
            em.emit("vector", lambda e: e.reciprocal(sc[:, 5:6], sc[:, 4:5]))
            em.emit("vector", lambda e, t=t: e.tensor_mul(
                scol[:, t:t + 1], ra[:, t:t + 1], sc[:, 5:6]))
        else:
            em.emit("vector", lambda e, t=t: e.tensor_copy(
                scol[:, t:t + 1], ra[:, t:t + 1]), waits=[se])
        em.emit("vector", lambda e, m=m, t=t: e.tensor_mul(
            sc[:, 7:8], m, scol[:, t:t + 1]))
        em.emit("vector", lambda e, t=t: e.tensor_scalar_mul(
            bcol[:, t:t + 1], sc[:, 7:8], cval(g, -1.0)), inc="ve")
    return em.ev("ve")


def emit_xn_tile(em, g, t, xnf, par_ev=None, gamma=None, dc_major=False):
    x = g["x_res"]
    xt_bf = g["xt_bf"]
    scol, bcol = g["scol"], g["bcol"]
    ps0, ps1 = g["ps_tr"]
    ident = g["ident"]
    ve = em.emit("vector", lambda e: e.tensor_scalar(
        xt_bf[:, :], x[:, t * D:(t + 1) * D], scol[:, t:t + 1], bcol[:, t:t + 1],
        ALU.mult, ALU.add), waits=[par_ev] if par_ev else [], inc="ve")
    pe_evs = []
    for half in range(2):
        ps = (ps0, ps1)[half]
        for j in range(4):
            dc = half * 4 + j
            pe = em.emit("tensor", lambda e, ps=ps, j=j, dc=dc: e.transpose(
                ps[:, j * 128:(j + 1) * 128], xt_bf[:, dc * 128:(dc + 1) * 128],
                ident[:, :]),
                waits=[ve, g["ps_tr_free"][half]],
                inc="pe" if j == 3 else None)
        if dc_major:
            dst = xnf.rearrange("p (dc n) -> p dc n", dc=8)[
                :, half * 4:(half + 1) * 4, t * 128:(t + 1) * 128]
            veo = em.emit("vector", lambda e, ps=ps, dst=dst: e.tensor_copy(
                dst, ps[:, :].rearrange("p (c n) -> p c n", c=4)),
                waits=[pe], inc="ve")
        elif gamma is None:
            veo = em.emit("vector", lambda e, ps=ps, half=half: e.tensor_copy(
                xnf[:, half * 512:(half + 1) * 512], ps[:, :]), waits=[pe], inc="ve")
        else:
            gcol, gev = gamma
            for j in range(4):
                dc = half * 4 + j
                veo = em.emit("vector", lambda e, ps=ps, j=j, dc=dc: e.tensor_scalar(
                    xnf[:, dc * 128:(dc + 1) * 128], ps[:, j * 128:(j + 1) * 128],
                    gcol[:, dc:dc + 1], None, ALU.mult),
                    waits=[pe, gev], inc="ve")
        g["ps_tr_free"][half] = veo
        pe_evs.append(veo)
    return pe_evs[-1]


def emit_l0ffn_prep(em, g, write_scratch=True):
    acc = g["tern_acc"]
    th_ps = g["ps_b"]
    stgs = [g["stg_w"][:, 0:256], g["stg_w"][:, 256:512],
            g["stg_w2"][:, 0:256], g["stg_w2"][:, 256:512]]
    queues = ["sync", "scalar", "sync", "scalar"]
    for wi, (src, n_chunks, res, scr) in enumerate([
            (g["in_w_up"], 8, g["wup_sb"], g["scr_wup"]),
            (g["in_w_dn"], 32, g["wdn_sb"], g["scr_wdn"])]):
        ncols = FF0 if wi == 0 else D
        PW = 256
        npieces = ncols // PW
        pieces = [(c, p) for c in range(n_chunks) for p in range(npieces)]
        mz = em.emit("vector", lambda e: e.memset(acc[:, 0:2], 0.0), inc="ve")
        ve = mz
        for i, (c, p) in enumerate(pieces):
            par = i % 4
            dsem = em.sem(f"dW{wi}{par}")
            sl = stgs[par]
            ld = em.dma(queues[par], sl,
                        src[c * 128:(c + 1) * 128, p * PW:(p + 1) * PW],
                        waits=[g["tern_stg_free"][par]], inc=dsem)
            em.emit("vector", lambda e, sl=sl: e.tensor_reduce(
                acc[:, 1:2], sl, axis=AX.X, op=ALU.add,
                apply_absolute_value=True), waits=[ld])
            ve = em.emit("vector", lambda e: e.tensor_add(
                acc[:, 0:1], acc[:, 0:1], acc[:, 1:2]), inc="ve")
            g["tern_stg_free"][par] = ve
        pe = em.emit("tensor", lambda e: e.matmul(
            th_ps[0:1, 0:1], acc[:, 0:1], g["ones_col_f"][:, :],
            start=True, stop=True), waits=[ve, g["ps_b_free"]], inc="pe")
        vth = em.emit("vector", lambda e: e.tensor_scalar_mul(
            g["thr_row"][0:1, 0:1], th_ps[0:1, 0:1],
            cval(g, 0.5 / (8 * 128 * FF0))[0:1, :]), waits=[pe], inc="ve")
        pe2 = em.emit("tensor", lambda e: e.matmul(
            th_ps[:, 4:5], g["ones_row_f"][0:1, :], g["thr_row"][0:1, 0:1],
            start=True, stop=True), waits=[vth], inc="pe")
        em.emit("vector", lambda e: e.tensor_copy(
            g["thr_col"][:, 0:1], th_ps[:, 4:5]), waits=[pe2])
        vcol = em.emit("vector", lambda e: e.tensor_scalar_mul(
            g["nthr_col"][:, 0:1], g["thr_col"][:, 0:1], cval(g, -1.0)), inc="ve")
        g["ps_b_free"] = vcol
        for i, (c, p) in enumerate(pieces):
            par = i % 4
            dsem = em.sem(f"dW{wi}{par}")
            sl = stgs[par]
            ld = em.dma(queues[par], sl,
                        src[c * 128:(c + 1) * 128, p * PW:(p + 1) * PW],
                        waits=[g["tern_stg_free"][par]], inc=dsem)
            sgn = g["tern_sgn"][:, par * PW:par * PW + PW]
            gtb = g["tern_gt"][:, par * PW:par * PW + PW]
            em.emit("vector", lambda e, sl=sl, sgn=sgn: e.tensor_single_scalar(
                sgn, sl, g["thr_col"][:, 0:1], ALU.is_gt),
                waits=[ld, vcol, g["tern_sgn_free"][par]])
            em.emit("vector", lambda e, sl=sl, gtb=gtb: e.tensor_single_scalar(
                gtb, sl, g["nthr_col"][:, 0:1], ALU.is_lt))
            dst = res[:, c * ncols + p * PW: c * ncols + p * PW + PW]
            ve = em.emit("vector", lambda e, dst=dst, sgn=sgn, gtb=gtb: e.tensor_sub(
                dst, sgn, gtb), inc="ve")
            g["tern_sgn_free"][par] = ve
            g["tern_stg_free"][par] = ve
            if write_scratch:
                em.dma("gpsimd", scr[c * 128:(c + 1) * 128, p * PW:(p + 1) * PW],
                       dst, waits=[ve], inc=em.sem("dWS"))
    return em.ev("ve")


def emit_l0ffn(em, g, do_prep, from_scratch):
    nc = g["nc"]
    if DBG["stats"]:
        stats = emit_ln_stats(em, g)
        sev = emit_scale_chain_l0ffn(em, g, stats)
    if do_prep and DBG["prep"]:
        emit_l0ffn_prep(em, g, write_scratch=g["write_scratch"])
    if from_scratch:
        dW = em.sem("dWL")
        for c in range(8):
            em.dma("sync", g["wup_sb"][:, c * FF0:(c + 1) * FF0],
                   g["scr_wup"][c * 128:(c + 1) * 128, :], inc=dW)
        for c in range(32):
            em.dma("sync", g["wdn_sb"][:, c * D:(c + 1) * D],
                   g["scr_wdn"][c * 128:(c + 1) * 128, :], inc=dW)
        wready = em.ev(("dWL"))
    else:
        wready = em.ev("ve")
    wup, wdn = g["wup_sb"], g["wdn_sb"]
    ps_up = g["ps_up"]
    ps_y = g["ps_y"]
    xnf2 = g["xnf2"]
    hbf = g["hbf"]
    htf = g["htf"]
    qacc = g["qacc"]
    sc = g["scratch8"]

    xnf_free = [em.ev("pe"), em.ev("pe")]
    gelu_done = [None, None]
    sumsq_done = [None, None]
    htf_free = [em.ev("pe"), em.ev("pe")]
    epi_ev = [None, None]
    for t in range(DBG["main_tiles"]):
        par = t % 2
        xnf = xnf2[par]
        xe = emit_xn_tile(em, g, t, xnf, par_ev=xnf_free[par])
        sumsq_evs = []
        NFC = DBG["fcs"]

        def tdn(fc, ge):
            fp = fc % 2
            for j in range(4):
                pe_t = em.emit("tensor", lambda e, fp=fp, j=j: e.transpose(
                    g["ps_tr"][fp][:, j * 128:(j + 1) * 128],
                    hbf[fp][:, j * 128:(j + 1) * 128], g["ident"][:, :]),
                    waits=[ge, g["ps_tr_free"][fp]],
                    inc="pe" if j == 3 else None)
            ve_t = em.emit("vector", lambda e, fp=fp: e.tensor_copy(
                htf[fp][:, :], g["ps_tr"][fp][:, :]),
                waits=[pe_t], inc="ve")
            g["ps_tr_free"][fp] = ve_t
            out = None
            for fs in range(4):
                fcc = fc * 4 + fs
                for nh in range(2):
                    w = [ve_t, wready]
                    if fc == 0 and fs == 0 and epi_ev[par] is not None:
                        w.append(epi_ev[par])
                    last = (fc == NFC - 1 and fs == 3 and nh == 1)
                    mm = em.emit("tensor",
                            lambda e, par=par, fs=fs, nh=nh, fcc=fcc, fc=fc: e.matmul(
                                ps_y[par][:, nh * 512:(nh + 1) * 512],
                                htf[fc % 2][:, fs * 128:(fs + 1) * 128],
                                wdn[:, fcc * D + nh * 512: fcc * D + nh * 512 + 512],
                                start=(fc == 0 and fs == 0),
                                stop=(fc == NFC - 1 and fs == 3)), waits=w,
                            inc="pe" if last else None)
                    if last:
                        out = mm
            return out

        pend = None
        pe_dn = None
        for fc in range(NFC):
            fp = fc % 2
            for dc in range(8):
                w = [xe, wready]
                if gelu_done[fp] is not None and dc == 0:
                    w.append(gelu_done[fp])
                pe_up = em.emit("tensor", lambda e, fp=fp, dc=dc, fc=fc, xnf=xnf: e.matmul(
                    ps_up[fp][:, :], xnf[:, dc * 128:(dc + 1) * 128],
                    wup[:, dc * FF0 + fc * 512: dc * FF0 + fc * 512 + 512],
                    start=(dc == 0), stop=(dc == 7)), waits=w,
                    inc="pe" if dc == 7 else None)
            if pend is not None:
                r = tdn(*pend)
                if r is not None:
                    pe_dn = r
            gw = [pe_up]
            if sumsq_done[fp] is not None:
                gw.append(sumsq_done[fp])
            ge = em.emit("scalar", lambda e, fp=fp: e.activation(
                hbf[fp][:, :], ps_up[fp][:, :], AF.Gelu), waits=gw, inc="se")
            gelu_done[fp] = ge
            sq = em.emit("vector", lambda e, fp=fp, fc=fc: e.scalar_tensor_tensor(
                htf[fp][:, :], hbf[fp][:, :], cval(g, 1.0), hbf[fp][:, :], ALU.mult, ALU.mult,
                accum_out=qacc[:, fc:fc + 1]), waits=[ge, g["ps_tr_free"][fp]], inc="ve")
            sumsq_evs.append(sq)
            sumsq_done[fp] = sq
            pend = (fc, ge)
        r = tdn(*pend)
        if r is not None:
            pe_dn = r
        htf_free = [pe_dn, pe_dn]
        xnf_free[par] = pe_dn
        if not DBG["epi"]:
            epi_ev[par] = pe_dn
            continue
        em.emit("vector", lambda e: e.tensor_reduce(
            sc[:, 0:1], qacc[:, :], axis=AX.X, op=ALU.add), waits=sumsq_evs[-1:])
        em.emit("vector", lambda e: e.tensor_scalar_mul(sc[:, 0:1], sc[:, 0:1], cval(g, 1.0 / FF0)))
        em.emit("vector", lambda e: e.tensor_scalar_max(sc[:, 0:1], sc[:, 0:1], cval(g, 1e-30)))
        if DBG["newton"]:
            rsqrt_newton(em, g, sc[:, 1:2], sc[:, 0:1], g["nsc"])
        else:
            em.emit("vector", lambda e: e.tensor_copy(sc[:, 1:2], sc[:, 0:1]))
        em.emit("vector", lambda e: e.tensor_mul(sc[:, 2:3], sc[:, 0:1], sc[:, 1:2]))
        em.emit("vector", lambda e: e.tensor_scalar_add(sc[:, 2:3], sc[:, 2:3], cval(g, 1e-8)))
        em.emit("vector", lambda e: e.reciprocal(sc[:, 3:4], sc[:, 2:3]))
        em.emit("vector", lambda e: e.tensor_scalar_mul(sc[:, 3:4], sc[:, 3:4], cval(g, 0.5)))
        ep = em.emit("vector", lambda e, t=t, par=par: e.scalar_tensor_tensor(
            g["x_res"][:, t * D:(t + 1) * D], ps_y[par][:, :], sc[:, 3:4],
            g["x_res"][:, t * D:(t + 1) * D], ALU.mult, ALU.add),
            waits=[pe_dn], inc="ve")
        epi_ev[par] = ep
    if DBG.get("dump"):
        lastve = em.ev("ve")
        em.dma("gpsimd", g["dbg1"], wup[:, 0:4096], waits=[lastve], inc="dDB")
        em.dma("gpsimd", g["dbg2"], wdn[:, 0:4096], waits=[lastve], inc="dDB")
        em.dma("gpsimd", g["dbg3"], xnf2[(DBG["main_tiles"] - 1) % 2][:, :],
               waits=[lastve], inc="dDB")
        em.dma("gpsimd", g["dbg4"][:, 0:8], qacc[:, :], waits=[lastve], inc="dDB", allow_slow_non_contiguous=True)
        em.dma("gpsimd", g["dbg4"][:, 8:9], g["thr_col"][:, 0:1], waits=[lastve], inc="dDB", allow_slow_non_contiguous=True)
        em.dma("gpsimd", g["dbg4"][:, 9:10], sc[:, 3:4], waits=[lastve], inc="dDB", allow_slow_non_contiguous=True)
        em.dma("gpsimd", g["dbg4"][:, 10:11], g["scol"][:, 0:1], waits=[lastve], inc="dDB", allow_slow_non_contiguous=True)
        em.dma("gpsimd", g["dbg4"][:, 11:12], g["bcol"][:, 0:1], waits=[lastve], inc="dDB", allow_slow_non_contiguous=True)
        em.dma("gpsimd", g["dbg4"][:, 12:14], g["mv"][:, 0:2], waits=[lastve], inc="dDB", allow_slow_non_contiguous=True)
        em.dma("gpsimd", g["dbg4"][:, 14:15], g["c16a"][:, 0:1], waits=[lastve], inc="dDB", allow_slow_non_contiguous=True)
        em.dma("gpsimd", g["dbg4"][:, 15:16], g["c16b"][:, 0:1], waits=[lastve], inc="dDB", allow_slow_non_contiguous=True)
    return em.ev("ve")


CVALS = [0.0, 1.0, -1.0, 0.5, -0.5, 1.5, 1e-8, 1.0 / FF0, 1e-12 / 32.0,
         0.5 / (8 * 128 * FF0), 1.0 / 127.0, 0.5 / (127.0 * 127.0),
         1597463007.0, 1e-30, 8.0, 1e-12, 0.125, 32.0, 0.03125]


def cval(g, v):
    i = CVALS.index(v)
    return g["cc"][:, i:i + 1]


def emit_l1ffn(em, g):
    stats = emit_ln_stats(em, g)
    emit_scale_chain_ln(em, g, stats)
    xnf_all = g["xnf_all"]
    wg_sb, wu_sb, wd_sb = g["wg_sb"], g["wu_sb"], g["wd_sb"]
    ps_g, ps_u = g["ps_gu"][0], g["ps_gu"][1]
    ps_y = g["ps_y"]
    hbf = g["hbf"]
    htf = g["htf"]
    HH = FF1 // 2
    xn_evs = []
    for t in range(NT):
        xn_evs.append(emit_xn_tile(em, g, t, xnf_all[:, t * D:(t + 1) * D]))
    g["l1_epi"] = None
    gu_done = [None, None]
    for hh in range(2):
        dW = em.sem(f"dL1{hh}")
        wfree = [g.get("l1w_free")] if g.get("l1w_free") else []
        for dc in range(8):
            em.dma("gpsimd", wg_sb[:, dc * HH:(dc + 1) * HH],
                   g["in_wg"][dc * 128:(dc + 1) * 128, hh * HH:(hh + 1) * HH],
                   waits=wfree, inc=dW)
            em.dma("gpsimd", wu_sb[:, dc * HH:(dc + 1) * HH],
                   g["in_wu"][dc * 128:(dc + 1) * 128, hh * HH:(hh + 1) * HH],
                   waits=wfree, inc=dW)
        for hc in range(12):
            em.dma("gpsimd", wd_sb[:, hc * D:(hc + 1) * D],
                   g["in_wd"][hh * HH + hc * 128: hh * HH + (hc + 1) * 128, :],
                   waits=wfree, inc=dW)
        wready = em.ev(dW)
        for t in range(NT):
            xnf = xnf_all[:, t * D:(t + 1) * D]

            def tdn1(hc, mu, t=t):
                hp = hc % 2
                for j in range(4):
                    pe_t = em.emit("tensor", lambda e, hp=hp, j=j: e.transpose(
                        g["ps_tr"][hp][:, j * 128:(j + 1) * 128],
                        hbf[hp][:, j * 128:(j + 1) * 128], g["ident"][:, :]),
                        waits=[mu, g["ps_tr_free"][hp]],
                        inc="pe" if j == 3 else None)
                ve_t = em.emit("vector", lambda e, hp=hp: e.tensor_copy(
                    htf[hp][:, :], g["ps_tr"][hp][:, :]), waits=[pe_t])
                g["ps_tr_free"][hp] = ve_t
                out = None
                for fs in range(4):
                    hcc = hc * 4 + fs
                    for nh in range(2):
                        w = [ve_t, wready]
                        if hc == 0 and fs == 0 and g.get("l1_epi") is not None:
                            w.append(g["l1_epi"])
                        last = (hc == 2 and fs == 3 and nh == 1)
                        mm = em.emit("tensor",
                                lambda e, fs=fs, nh=nh, hcc=hcc, hc=hc, hp=hp: e.matmul(
                                    ps_y[:, nh * 512:(nh + 1) * 512],
                                    htf[hp][:, fs * 128:(fs + 1) * 128],
                                    wd_sb[:, hcc * D + nh * 512:
                                          hcc * D + nh * 512 + 512],
                                    start=(hc == 0 and fs == 0),
                                    stop=(hc == 2 and fs == 3)), waits=w,
                                inc="pe" if last else None)
                        if last:
                            out = mm
                return out

            pend1 = None
            dn_done = None
            for hc in range(3):
                hp = hc % 2
                for dc in range(8):
                    w = [xn_evs[t], wready]
                    if gu_done[hp] is not None and dc == 0:
                        w.append(gu_done[hp])
                    em.emit("tensor", lambda e, hp=hp, dc=dc, hc=hc, xnf=xnf: e.matmul(
                        ps_g[hp][:, :], xnf[:, dc * 128:(dc + 1) * 128],
                        wg_sb[:, dc * HH + hc * 512: dc * HH + hc * 512 + 512],
                        start=(dc == 0), stop=(dc == 7)), waits=w)
                for dc in range(8):
                    last = dc == 7
                    mm = em.emit("tensor", lambda e, hp=hp, dc=dc, hc=hc, xnf=xnf: e.matmul(
                        ps_u[hp][:, :], xnf[:, dc * 128:(dc + 1) * 128],
                        wu_sb[:, dc * HH + hc * 512: dc * HH + hc * 512 + 512],
                        start=(dc == 0), stop=(dc == 7)),
                        waits=[xn_evs[t], wready],
                        inc="pe" if last else None)
                pe_gu = mm
                if pend1 is not None:
                    r = tdn1(*pend1)
                    if r is not None:
                        dn_done = r
                ge = em.emit("scalar", lambda e, hp=hp: e.activation(
                    hbf[hp][:, :], ps_g[hp][:, :], AF.Silu, scale=FP(1.0 / 127.0)),
                    waits=[pe_gu])
                mu = em.emit("vector", lambda e, hp=hp: e.tensor_tensor(
                    hbf[hp][:, :], hbf[hp][:, :], ps_u[hp][:, :], ALU.mult),
                    waits=[ge])
                gu_done[hp] = mu
                pend1 = (hc, mu)
            r = tdn1(*pend1)
            if r is not None:
                dn_done = r
            g["l1_epi"] = em.emit("vector", lambda e, t=t: e.scalar_tensor_tensor(
                g["x_res"][:, t * D:(t + 1) * D], ps_y[:, :],
                cval(g, 0.5 / (127.0 * 127.0)),
                g["x_res"][:, t * D:(t + 1) * D], ALU.mult, ALU.add),
                waits=[dn_done])
        g["l1w_free"] = em.engine_done_ev("tensor")
    return g["l1_epi"]


def emit_l1attn_alloc(g, em, psb, pps):
    g["ps_tr"] = (pps("ps_tr0", [128, 512], BF16), pps("ps_tr1", [128, 512], BF16))
    g["ps_tr_free"] = [em.ev("pe"), em.ev("pe")]
    g["ps_qk"] = pps("ps_qkA", [128, 1024])[:, :]
    g["ps_sim"] = (pps("ps_simA", [64, 512]), pps("ps_simB", [64, 512]))
    g["ps_ss"] = pps("ps_ss", [128, 512])[:, :]
    g["ps_pv"] = pps("ps_pv", [128, 512])[:, :]
    g["ps_ss_free"] = em.ev("pe")
    g["xnf_free"] = [em.ev("pe"), em.ev("pe")]
    g["wq_sb"] = psb("wq_sb", [128, 8 * 3072], BF16)
    g["wo_sb"] = psb("wo_sb", [128, 8 * D], BF16)
    g["v_fm"] = psb("v_fm", [128, 8 * 2048], BF16)
    g["o_f"] = psb("o_f", [128, 8 * 2048], BF16)[:, :]
    g["qksq"] = g["o_f"][:, 0:2048]
    g["qk_t"] = g["o_f"][:, 2048:4096]
    g["xnf2"] = (g["o_f"][:, 4096:5120], g["o_f"][:, 5120:6144])
    g["t16"] = psb("t16", [16, 2])
    g["rowbuf"] = psb("rowbuf", [1, 16])
    g["expT_bc"] = psb("expT_bc", [64, 16])
    g["sqcol"] = psb("sqcol", [64, 16])
    g["skb"] = psb("skb", [64, 64])
    g["simf"] = psb("simf", [64, 64])
    g["mxc"] = psb("mxc", [64, 4])
    g["attn_bf"] = psb("attn_bf", [64, 64], BF16)
    g["attnT"] = psb("attnT", [128, 64], BF16)
    g["ones128_bf"] = psb("ones128_bf", [128, 128], BF16)
    g["identf"] = psb("identf", [128, 128])
    g["gcol"] = psb("gcol", [128, 8])
    g["ss_sb"] = psb("ss_sb", [1, 2048])


def emit_l1attn(em, g):
    stats = emit_ln_stats(em, g)
    emit_scale_chain_ln(em, g, stats)
    wq, wo = g["wq_sb"], g["wo_sb"]
    qk_t, qksq = g["qk_t"], g["qksq"]
    v_fm, o_f = g["v_fm"], g["o_f"]
    ps_qk = g["ps_qk"]
    ps_sim = g["ps_sim"]
    ps_ss = g["ps_ss"]
    ps_pv = g["ps_pv"]
    xnf2 = g["xnf2"]
    ones128 = g["ones128_bf"]
    em.emit("gpsimd", lambda e: e.memset(ones128[:, :], 1.0))
    em.emit("vector", lambda e: e.tensor_copy(g["identf"][:, :], g["ident"][:, :]),
            waits=[g["consts_ev"]])
    dG = em.sem("dG1")
    gld = em.dma("sync", g["gcol"][:, :], g["in_gamma"], inc=dG)
    dW = em.sem("dL1A")
    for dc in range(8):
        em.dma("gpsimd", wq[:, dc * 3072:(dc + 1) * 3072],
               g["in_wqkv"][dc * 128:(dc + 1) * 128, :], inc=dW)
    for c in range(8):
        em.dma("gpsimd", wo[:, c * D:(c + 1) * D],
               g["in_wo"][c * 128:(c + 1) * 128, :], inc=dW)
    wready = em.ev(dW)
    dT = em.sem("dT")
    tld = em.dma("sync", g["t16"][0:16, 0:1],
                 g["in_temp"].rearrange("(a b) -> a b", b=1), inc=dT)
    eT = em.emit("scalar", lambda e: e.activation(
        g["t16"][0:16, 1:2], g["t16"][0:16, 0:1], AF.Exp), waits=[tld])
    idf = em.engine_done_ev("vector")
    peT = em.emit("tensor", lambda e: e.transpose(
        ps_ss[0:1, 0:16], g["t16"][0:16, 1:2], g["identf"][0:16, 0:16]),
        waits=[eT, idf, g["ps_ss_free"]], inc="pe")
    veT = em.emit("vector", lambda e: e.tensor_copy(
        g["rowbuf"][0:1, 0:16], ps_ss[0:1, 0:16]), waits=[peT])
    peB = em.emit("tensor", lambda e: e.matmul(
        ps_ss[0:64, 16:32], g["ones_row_f"][0:1, 0:64], g["rowbuf"][0:1, 0:16],
        start=True, stop=True), waits=[veT], inc="pe")
    eBC = em.emit("vector", lambda e: e.tensor_copy(
        g["expT_bc"][:, :], ps_ss[0:64, 16:32]), waits=[peB])
    g["ps_ss_free"] = eBC

    sim_last = None
    qk_free = em.ev("pe")
    ss_free = eBC
    ss_add_ev = None
    for t in range(NT):
        par = t % 2
        xe = emit_xn_tile(em, g, t, xnf2[par], par_ev=g["xnf_free"][par],
                          gamma=(g["gcol"], (dG, 16)))
        for gq in range(2):
            for nh in range(2):
                for dc in range(8):
                    w = [xe, wready, qk_free]
                    last = dc == 7 and nh == 1
                    mm = em.emit("tensor", lambda e, gq=gq, nh=nh, dc=dc, par=par: e.matmul(
                        ps_qk[:, nh * 512:(nh + 1) * 512],
                        xnf2[par][:, dc * 128:(dc + 1) * 128],
                        wq[:, dc * 3072 + gq * 1024 + nh * 512:
                           dc * 3072 + gq * 1024 + nh * 512 + 512],
                        start=(dc == 0), stop=(dc == 7)), waits=w,
                        inc="pe" if last else None)
            ev_qk = em.emit("vector", lambda e, gq=gq: e.tensor_copy(
                qk_t[:, gq * 1024:(gq + 1) * 1024], ps_qk[:, :]), waits=[mm])
            qk_free = ev_qk
            em.emit("vector", lambda e, gq=gq: e.scalar_tensor_tensor(
                qksq[:, gq * 1024:(gq + 1) * 1024], qk_t[:, gq * 1024:(gq + 1) * 1024],
                cval(g, 1.0), qk_t[:, gq * 1024:(gq + 1) * 1024], ALU.mult, ALU.mult))
        sq_ev = em.engine_done_ev("vector")
        for h in range(H):
            b = h // 8
            last = h % 8 == 7 and b == 1
            mm = em.emit("tensor", lambda e, h=h, b=b, t=t: e.matmul(
                ps_sim[b][0:64, (h % 8) * 64:(h % 8) * 64 + 64],
                qk_t[:, h * 64:h * 64 + 64],
                qk_t[:, 1024 + h * 64:1024 + h * 64 + 64],
                start=(t == 0 and h % 8 == 0), stop=(t == NT - 1),
                skip_group_check=True), waits=[sq_ev],
                inc="pe" if last and t == NT - 1 else None)
            if last and t == NT - 1:
                sim_last = mm
        for p in range(4):
            pm = em.emit("tensor", lambda e, p=p: e.matmul(
                ps_ss[:, 0:512], ones128[:, :], qksq[:, p * 512:(p + 1) * 512],
                start=True, stop=True), waits=[sq_ev, ss_free], inc="pe")
            if t == 0:
                ae = em.emit("vector", lambda e, p=p: e.tensor_copy(
                    g["ss_sb"][0:1, p * 512:(p + 1) * 512], ps_ss[0:1, 0:512]),
                    waits=[pm])
            else:
                ae = em.emit("vector", lambda e, p=p: e.tensor_add(
                    g["ss_sb"][0:1, p * 512:(p + 1) * 512],
                    g["ss_sb"][0:1, p * 512:(p + 1) * 512], ps_ss[0:1, 0:512]),
                    waits=[pm])
            ss_free = ae
        ss_add_ev = ae
        for half in range(2):
            for ci in range(4):
                c = half * 4 + ci
                for dc in range(8):
                    w = [xe, wready, qk_free]
                    last = dc == 7 and ci == 3
                    mm = em.emit("tensor", lambda e, c=c, ci=ci, half=half, dc=dc, par=par: e.matmul(
                        ps_qk[:, half * 512 + ci * 128: half * 512 + ci * 128 + 128],
                        wq[:, dc * 3072 + 2048 + c * 128: dc * 3072 + 2048 + (c + 1) * 128],
                        xnf2[par][:, dc * 128:(dc + 1) * 128],
                        start=(dc == 0 and ci == 0),
                        stop=(dc == 7), skip_group_check=True), waits=w,
                        inc="pe" if last else None)
            vo = em.emit("vector", lambda e, half=half, t=t: e.tensor_copy(
                v_fm[:, :].rearrange("p (c n) -> p c n", c=8)
                [:, half * 4:(half + 1) * 4, t * 128:(t + 1) * 128],
                ps_qk[:, half * 512:(half + 1) * 512].rearrange(
                    "p (c n) -> p c n", c=4)), waits=[mm])
            qk_free = vo
        g["xnf_free"][par] = em.engine_done_ev("tensor")
    srow = g["ss_sb"]
    lnr = em.emit("scalar", lambda e: e.activation(
        srow[0:1, 0:2048], srow[0:1, 0:2048], AF.Ln), waits=[ss_add_ev])
    sqr = em.emit("scalar", lambda e: e.activation(
        srow[0:1, 0:2048], srow[0:1, 0:2048], AF.Exp, scale=FP(0.5)), waits=[lnr])
    em.emit("vector", lambda e: e.tensor_scalar_max(
        srow[0:1, 0:2048], srow[0:1, 0:2048], cval(g, 1e-12)[0:1, :]),
        waits=[sqr])
    em.emit("vector", lambda e: e.reciprocal(
        srow[0:1, 0:2048], srow[0:1, 0:2048]))
    for h in range(H):
        ve_now = em.ev("ve")
        pt = em.emit("tensor", lambda e, h=h: e.transpose(
            ps_ss[0:64, 32:33], g["ss_sb"][0:1, h * 64:(h + 1) * 64],
            g["identf"][0:1, 0:1]),
            waits=[(("ve"), ve_now[1]), g["ps_ss_free"]], inc="pe")
        ce = em.emit("vector", lambda e, h=h: e.tensor_copy(
            g["sqcol"][0:64, h:h + 1], ps_ss[0:64, 32:33]), waits=[pt])
        g["ps_ss_free"] = ce
    em.emit("vector", lambda e: e.tensor_mul(
        g["sqcol"][0:64, 0:16], g["sqcol"][0:64, 0:16], g["expT_bc"][:, :]))
    sqc_ev = em.emit("vector", lambda e: e.tensor_scalar_mul(
        g["sqcol"][0:64, 0:16], g["sqcol"][0:64, 0:16], cval(g, 8.0)[0:64, :]))
    pv_free = em.ev("ve")
    at_free = em.ev("ve")
    for h in range(H):
        b, col = h // 8, (h % 8) * 64
        sim_ap = ps_sim[b][0:64, col:col + 64]
        pb = em.emit("tensor", lambda e, h=h: e.matmul(
            ps_ss[0:64, 64:128], g["ones_row_f"][0:1, 0:64],
            g["ss_sb"][0:1, 1024 + h * 64:1024 + (h + 1) * 64],
            start=True, stop=True), waits=[sqc_ev, g["ps_ss_free"]], inc="pe")
        cb = em.emit("vector", lambda e: e.tensor_copy(
            g["skb"][0:64, 0:64], ps_ss[0:64, 64:128]), waits=[pb])
        g["ps_ss_free"] = cb
        em.emit("vector", lambda e, h=h, sim_ap=sim_ap: e.tensor_scalar(
            g["simf"][0:64, 0:64], sim_ap, g["sqcol"][0:64, h:h + 1], None,
            ALU.mult), waits=[sim_last, sqc_ev])
        em.emit("vector", lambda e: e.tensor_mul(
            g["simf"][0:64, 0:64], g["simf"][0:64, 0:64], g["skb"][0:64, 0:64]))
        mx = em.emit("vector", lambda e: e.reduce_max(
            g["mxc"][0:64, 0:1], g["simf"][0:64, 0:64], axis=AX.X, negate=True))
        ex = em.emit("scalar", lambda e: e.activation(
            g["simf"][0:64, 0:64], g["simf"][0:64, 0:64], AF.Exp,
            bias=g["mxc"][0:64, 0:1], accum_out=g["mxc"][0:64, 1:2]),
            waits=[mx])
        iv = em.emit("vector", lambda e: e.reciprocal(
            g["mxc"][0:64, 2:3], g["mxc"][0:64, 1:2]), waits=[ex])
        nm = em.emit("vector", lambda e: e.tensor_scalar(
            g["attn_bf"][0:64, 0:64], g["simf"][0:64, 0:64], g["mxc"][0:64, 2:3],
            None, ALU.mult), waits=[at_free])
        ro = (h % 2) * 64
        ttp = (0, 64) if ro else None
        ptr = em.emit("tensor", lambda e, ro=ro, ttp=ttp: e.transpose(
            g["ps_tr"][0][ro:ro + 64, 0:64], g["attn_bf"][0:64, 0:64],
            g["ident"][0:64, 0:64], tile_position=ttp),
            waits=[nm, g["ps_tr_free"][0]], inc="pe")
        ct = em.emit("vector", lambda e, ro=ro: e.tensor_copy(
            g["attnT"][ro:ro + 64, 0:64], g["ps_tr"][0][ro:ro + 64, 0:64]),
            waits=[ptr])
        g["ps_tr_free"][0] = ct
        at_free = ct
        for nchunk in range(4):
            tp = (64, 64) if ro else None
            pv = em.emit("tensor", lambda e, h=h, nchunk=nchunk, tp=tp, ro=ro: e.matmul(
                ps_pv[ro:ro + 64, 0:512],
                g["attnT"][ro:ro + 64, 0:64],
                v_fm[ro:ro + 64,
                     (h // 2) * 2048 + nchunk * 512:(h // 2) * 2048 + (nchunk + 1) * 512],
                start=True, stop=True, tile_position=tp),
                waits=[ct, pv_free], inc="pe")
            oe = em.emit("vector", lambda e, h=h, nchunk=nchunk, ro=ro: e.tensor_copy(
                o_f[ro:ro + 64,
                    (h // 2) * 2048 + nchunk * 512:(h // 2) * 2048 + (nchunk + 1) * 512],
                ps_pv[ro:ro + 64, 0:512]), waits=[pv])
            pv_free = oe
    of_ev = em.engine_done_ev("vector")
    epi = None
    for t in range(NT):
        for c in range(8):
            for nh in range(2):
                w = [of_ev, wready, qk_free]
                if epi is not None and c == 0 and nh == 0:
                    w.append(epi)
                last = c == 7 and nh == 1
                mm = em.emit("tensor", lambda e, c=c, nh=nh, t=t: e.matmul(
                    ps_qk[:, nh * 512:(nh + 1) * 512],
                    o_f[:, c * 2048 + t * 128: c * 2048 + (t + 1) * 128],
                    wo[:, c * D + nh * 512: c * D + nh * 512 + 512],
                    start=(c == 0), stop=(c == 7)), waits=w,
                    inc="pe" if last else None)
        epi = em.emit("vector", lambda e, t=t: e.scalar_tensor_tensor(
            g["x_res"][:, t * D:(t + 1) * D], ps_qk[:, :], cval(g, 1.0),
            g["x_res"][:, t * D:(t + 1) * D], ALU.mult, ALU.add), waits=[mm])
    return epi


def emit_l0attn_alloc(g, em, psb, pps):
    g["ps_S"] = pps("ps_S", [128, 2048])[:, :]
    g["ps_tr"] = (pps("ps_tr0", [128, 512], BF16), pps("ps_tr1", [128, 512], BF16))
    g["ps_tr_free"] = [em.ev("pe"), em.ev("pe")]
    g["ps_o"] = pps("ps_o", [128, 128])[:, :]
    g["ps_pr"] = pps("ps_pr", [128, 512])[:, :]
    g["wqkv_sb"] = psb("wqkv_sb", [128, 8 * 3072], BF16)
    g["xnf_dc"] = psb("xnf_dc", [128, NT * D], BF16)[:, :]
    g["q_sb"] = psb("q_sb", [128, 2048], BF16)
    g["k_sb"] = psb("k_sb", [128, 2048], BF16)
    g["v_sb"] = psb("v_sb", [128, 2048], BF16)
    g["embr_sb"] = psb("embr_sb", [128, 2048], BF16)
    g["PR_sb"] = (psb("PR_sb0", [128, 2048], BF16), psb("PR_sb1", [128, 2048], BF16))
    g["A_sb"] = psb("A_sb", [128, 2048], BF16)
    g["At_sb"] = psb("At_sb", [128, 2048], BF16)
    g["Aadd"] = (psb("AaddA", [128, 2048], BF16), psb("AaddB", [128, 2048], BF16))
    g["o_st"] = psb("o_st", [128, 2048], BF16)
    g["o_ld"] = (psb("o_ldA", [128, 1024], BF16), psb("o_ldB", [128, 1024], BF16))
    g["negt"] = psb("negt", [128, 512], BF16)
    g["qb8"] = psb("qb8", [128, 8])
    g["obrow"] = psb("obrow", [1, 1024], BF16)
    g["mxl"] = psb("mxl", [128, 4])
    g["pr_pos"] = psb("pr_pos", [128, 2048], BF16)
    g["pr_sins"] = psb("pr_sins", [128, 1024], BF16)
    g["pr_stage"] = psb("pr_stage", [128, 512], BF16)
    g["wo0_sb"] = g["xnf_dc"][:, 0:8 * D]


def emit_l0attn(em, g, do_prep):
    stats = emit_ln_stats(em, g)
    emit_scale_chain_ln(em, g, stats)
    wq = g["wqkv_sb"]
    xnf = g["xnf_dc"]
    q_sb, k_sb, v_sb = g["q_sb"], g["k_sb"], g["v_sb"]
    ps_S, ps_pr, ps_o = g["ps_S"], g["ps_pr"], g["ps_o"]
    A_sb, At_sb = g["A_sb"], g["At_sb"]
    PR_sb2 = g["PR_sb"]
    Aadd = g["Aadd"]
    ident = g["ident"]
    CQ = float(C_QKV)
    R = 2560

    ng = em.emit("gpsimd", lambda e: e.memset(g["negt"][:, :], NEG))
    dW = em.sem("dL0A")
    for dc in range(8):
        em.dma("gpsimd", wq[:, dc * 3072:(dc + 1) * 3072],
               g["in_qkv"][dc * 128:(dc + 1) * 128, :], inc=dW)
    wready = em.ev(dW)
    qld = em.dma("sync", g["qb8"][:, :], g["in_qb"], inc=em.sem("dQB"))
    obld = em.dma("gpsimd", g["obrow"][0:1, :], g["in_ob"], inc=em.sem("dOB"))
    em.emit("vector", lambda e: e.tensor_scalar_mul(
        g["qb8"][:, :], g["qb8"][:, :], cval(g, 0.125)), waits=[qld])
    qbe = em.emit("vector", lambda e: e.tensor_scalar_mul(
        g["obrow"][0:1, :], g["obrow"][0:1, :], cval(g, 32.0)[0:1, :]),
        waits=[obld], inc="ve")

    dPD = em.sem("dPAD")
    for par in range(2):
        skw = g["skw"][par].rearrange("(p a) -> p a", a=R)
        em.dma("scalar", skw[:, 2048:2560],
               g["negt"][:, :].rearrange("p (a b) -> p a b", a=1)[:, 0, 0:512],
               waits=[ng], inc=dPD)
    pad_ev = em.ev(dPD)

    if do_prep:
        dE, dSin, dEW = em.sem("dEM"), em.sem("dSin"), em.sem("dEW")
        for fc in range(2):
            em.dma("gpsimd", g["pr_pos"][:, fc * 1024:(fc + 1) * 1024],
                   g["in_pos"][fc * 128:(fc + 1) * 128, :], inc=dE)
        posev = em.ev(dE)
        sin_free = em.ev("pe")
        st_free = em.ev("dEW")
        for jc in range(4):
            s2 = em.dma("gpsimd",
                        g["pr_sins"][:, :].rearrange("p (c n) -> p c n", c=2),
                        g["in_sins"].rearrange("(c p) n -> p c n", c=2)
                        [:, :, jc * 512:(jc + 1) * 512],
                        waits=[sin_free], inc=dSin)
            for hp in range(8):
                for fc in range(2):
                    mm = em.emit("tensor", lambda e, hp=hp, fc=fc: e.matmul(
                        ps_pr[:, :],
                        g["pr_pos"][:, fc * 1024 + hp * 128: fc * 1024 + (hp + 1) * 128],
                        g["pr_sins"][:, fc * 512:(fc + 1) * 512],
                        start=(fc == 0), stop=(fc == 1)),
                        waits=[s2, posev, g.get("pr_free")],
                        inc="pe" if fc == 1 else None)
                ev = em.emit("vector", lambda e: e.tensor_copy(
                    g["pr_stage"][:, :], ps_pr[:, :]),
                    waits=[mm, st_free])
                g["pr_free"] = ev
                st_free = em.dma("scalar",
                                 g["embr_s"].rearrange("(hp p n) -> hp p n", hp=8, p=128)
                                 [hp, :, jc * 512:(jc + 1) * 512],
                                 g["pr_stage"][:, :], waits=[ev], inc=dEW)
            sin_free = em.engine_done_ev("tensor")
        g["embr_ready"] = em.ev("dEW")

    xn_evs = []
    for t in range(NT):
        xn_evs.append(emit_xn_tile(em, g, t, xnf, dc_major=True))
    xn_all = xn_evs[-1]

    dEL = em.sem("dEL")
    dSW = [em.sem("dSW0"), em.sem("dSW1")]
    dSR = [em.sem("dSR0"), em.sem("dSR1")]
    dOS = em.sem("dOS")
    embr_free = em.ev("pe")
    pair_done = em.ev("pe")
    exp_prev = None
    aadd_used = [None, None]
    g["o_st_free"] = em.ev("dOS")
    for hp in range(DBG.get("l0a_pairs", 8)):
        eld = em.dma("sync", g["embr_sb"][:, :],
                     g["embr_s"].rearrange("(hp p n) -> hp p n", hp=8, p=128)[hp, :, :],
                     waits=[g["embr_ready"], embr_free], inc=dEL)
        pw = [wready, xn_all, pair_done]
        qk_ev = {}
        for which, base, scale in (("q", hp * 128, 0.125 * CQ),
                                   ("k", 1024 + hp * 128, CQ)):
            dst = q_sb if which == "q" else k_sb
            for tc in range(4):
                bank = (tc % 2) * 512
                w = list(pw) + [qk_ev.get((which, tc - 2)), qbe, exp_prev]
                for dc in range(8):
                    mm = em.emit("tensor",
                            lambda e, dc=dc, tc=tc, base=base, bank=bank: e.matmul(
                                ps_S[:, bank:bank + 512],
                                wq[:, dc * 3072 + base: dc * 3072 + base + 128],
                                xnf[:, dc * 2048 + tc * 512: dc * 2048 + (tc + 1) * 512],
                                start=(dc == 0), stop=(dc == 7),
                                skip_group_check=True),
                            waits=w, inc="pe" if dc == 7 else None)
                if which == "q":
                    ev = em.emit("scalar", lambda e, tc=tc, bank=bank, hp=hp, dst=dst: e.activation(
                        dst[:, tc * 512:(tc + 1) * 512], ps_S[:, bank:bank + 512],
                        AF.Identity, scale=FP(0.125 * CQ),
                        bias=g["qb8"][:, hp:hp + 1]), waits=[mm])
                else:
                    ev = em.emit("scalar", lambda e, tc=tc, bank=bank, dst=dst: e.activation(
                        dst[:, tc * 512:(tc + 1) * 512], ps_S[:, bank:bank + 512],
                        AF.Identity, scale=FP(CQ)), waits=[mm])
                qk_ev[(which, tc)] = ev
        for tg in range(4):
            bank = 1024 + (tg % 2) * 512
            w = list(pw) + [qk_ev.get(("v", tg - 2)), exp_prev]
            for ti in range(4):
                t = tg * 4 + ti
                for dc in range(8):
                    mm = em.emit("tensor",
                            lambda e, dc=dc, ti=ti, t=t, bank=bank, hp=hp: e.matmul(
                                ps_S[:, bank + ti * 128: bank + ti * 128 + 128],
                                xnf[:, dc * 2048 + t * 128: dc * 2048 + (t + 1) * 128],
                                wq[:, dc * 3072 + 2048 + hp * 128:
                                   dc * 3072 + 2048 + (hp + 1) * 128],
                                start=(dc == 0 and ti == 0), stop=(dc == 7),
                                skip_group_check=True),
                            waits=w, inc="pe" if (dc == 7 and ti == 3) else None)
            ev = em.emit("scalar", lambda e, tg=tg, bank=bank: e.activation(
                v_sb[:, tg * 512:(tg + 1) * 512], ps_S[:, bank:bank + 512],
                AF.Identity, scale=FP(CQ)), waits=[mm])
            qk_ev[("v", tg)] = ev
        qke = em.engine_done_ev("scalar")

        for hodd in range(DBG.get("l0a_heads", 2)):
            h = 2 * hp + hodd
            ro = hodd * 64
            rtp = (64, 0) if hodd else None

            def emit_pr(j, qke=qke, eld=eld, ro=ro, rtp=rtp, hp=hp):
                jmin = 1920 - 128 * j
                width = 2048 - jmin
                par = j % 2
                npc = (width + 511) // 512
                prev = None
                PR_sb = PR_sb2[par]
                for pc in range(npc):
                    c0 = jmin + pc * 512
                    cw = min(512, 2048 - c0)
                    mm = em.emit("tensor", lambda e, j=j, c0=c0, cw=cw, ro=ro, rtp=rtp: e.matmul(
                        ps_pr[:, 0:cw],
                        q_sb[ro:ro + 64, j * 128:(j + 1) * 128],
                        g["embr_sb"][ro:ro + 64, c0:c0 + cw],
                        start=True, stop=True, tile_position=rtp),
                        waits=[qke, eld, g.get("pr_free")], inc="pe")
                    prev = em.emit("vector", lambda e, c0=c0, cw=cw, PR_sb=PR_sb: e.tensor_copy(
                        PR_sb[:, c0:c0 + cw], ps_pr[:, 0:cw]),
                        waits=[mm, g.get(f"prsb_free{par}")])
                    g["pr_free"] = prev
                skw = g["skw"][par].rearrange("(p a) -> p a", a=R)
                wrev = em.dma("scalar", skw[:, jmin:2048], PR_sb[:, jmin:2048],
                              waits=[prev, pad_ev,
                                     g.get(f"skr_done{par}")], inc=dSW[par])
                g[f"prsb_free{par}"] = wrev
                W = 128 * j + 128
                Wc = ((W + 511) // 512) * 512
                bview = g["skw"][par][0:128 * (R - 1)].rearrange("(p a) -> p a", a=R - 1)
                rdev = em.dma("sync", Aadd[par][:, 0:Wc],
                              bview[:, 2047 - 128 * j: 2047 - 128 * j + Wc],
                              waits=[em.ev(dSW[par]), aadd_used[par]], inc=dSR[par])
                g[f"skr_done{par}"] = em.ev(dSR[par])
                return rdev

            rd = {0: emit_pr(0), 1: emit_pr(1)}
            pend = None

            def do_tpv(i, nm):
                nkc = i + 1
                for kb in range((nkc + 3) // 4):
                    nj = min(4, nkc - kb * 4)
                    for jj in range(nj):
                        kc = kb * 4 + jj
                        pt = em.emit("tensor", lambda e, kb=kb, jj=jj, kc=kc: e.transpose(
                            g["ps_tr"][kb % 2][:, jj * 128:(jj + 1) * 128],
                            A_sb[:, kc * 128:(kc + 1) * 128], ident[:, :]),
                            waits=[nm, g["ps_tr_free"][kb % 2]],
                            inc="pe" if jj == nj - 1 else None)
                    te = em.emit("scalar", lambda e, kb=kb, nj=nj: e.copy(
                        At_sb[:, kb * 512:kb * 512 + nj * 128],
                        g["ps_tr"][kb % 2][:, 0:nj * 128]), waits=[pt])
                    g["ps_tr_free"][kb % 2] = te
                g["asb_free"] = em.engine_done_ev("tensor")
                ate = em.engine_done_ev("scalar")
                for kc in range(nkc):
                    otp = (0, 64) if hodd else None
                    mm = em.emit("tensor", lambda e, kc=kc, ro=ro, otp=otp, i=i: e.matmul(
                        ps_o[ro:ro + 64, 0:128],
                        v_sb[:, kc * 128 + ro: kc * 128 + ro + 64],
                        At_sb[:, kc * 128:(kc + 1) * 128],
                        start=(kc == 0), stop=(kc == nkc - 1),
                        skip_group_check=True, tile_position=otp),
                        waits=[ate, g.get("po_free")], inc="pe" if kc == nkc - 1 else None)
                oe = em.emit("vector", lambda e, ro=ro, i=i: e.tensor_copy(
                    g["o_st"][ro:ro + 64, i * 128:(i + 1) * 128],
                    ps_o[ro:ro + 64, 0:128]), waits=[mm, g["o_st_free"]])
                g["po_free"] = oe

            for i in range(DBG.get("l0a_tiles", NT)):
                W = 128 * i + 128
                Wc = ((W + 511) // 512) * 512
                nch = Wc // 512
                par = i % 2
                for c in range(nch):
                    em.emit("tensor", lambda e, i=i, c=c, ro=ro, rtp=rtp: e.matmul(
                        ps_S[:, c * 512:(c + 1) * 512],
                        q_sb[ro:ro + 64, i * 128:(i + 1) * 128],
                        k_sb[ro:ro + 64, c * 512:(c + 1) * 512],
                        start=True, stop=False, skip_group_check=True,
                        tile_position=rtp),
                        waits=[qke, exp_prev])
                    last = c == nch - 1
                    mm = em.emit("tensor", lambda e, i=i, c=c, par=par: e.matmul(
                        ps_S[:, c * 512:(c + 1) * 512], ident[:, :],
                        Aadd[par][:, c * 512:(c + 1) * 512],
                        start=False, stop=True, skip_group_check=True),
                        waits=[rd[i]], inc="pe" if last else None)
                aadd_used[par] = mm
                if i + 2 < DBG.get("l0a_tiles", NT):
                    rd[i + 2] = emit_pr(i + 2)
                if pend is not None:
                    do_tpv(*pend)
                mx = em.emit("vector", lambda e, Wc=Wc: e.reduce_max(
                    g["mxl"][:, 0:1], ps_S[:, 0:Wc], axis=AX.X, negate=True),
                    waits=[mm])
                ex = em.emit("scalar", lambda e, Wc=Wc: e.activation(
                    A_sb[:, 0:Wc], ps_S[:, 0:Wc], AF.Exp,
                    bias=g["mxl"][:, 0:1], accum_out=g["mxl"][:, 1:2]),
                    waits=[mx, g.get("asb_free")])
                exp_prev = ex
                em.emit("vector", lambda e: e.reciprocal(
                    g["mxl"][:, 2:3], g["mxl"][:, 1:2]), waits=[ex])
                nm = em.emit("vector", lambda e, Wc=Wc: e.tensor_scalar(
                    A_sb[:, 0:Wc], A_sb[:, 0:Wc], g["mxl"][:, 2:3], None,
                    ALU.mult))
                pend = (i, nm)
            do_tpv(*pend)
        ose = em.dma("scalar",
                     g["o_s"].rearrange("(hp p n) -> hp p n", hp=8, p=128)[hp, :, :],
                     g["o_st"][:, :], waits=[em.ev("ve")], inc=dOS)
        g["o_st_free"] = ose
        embr_free = em.engine_done_ev("tensor")
        pair_done = embr_free
    if DBG.get("dump"):
        lastall = [em.engine_done_ev(e) for e in ("vector", "scalar", "tensor")]
        em.dma("gpsimd", g["dbg1"][:, 0:2048], q_sb[:, :], waits=lastall, inc="dDB")
        em.dma("gpsimd", g["dbg1"][:, 2048:4096], k_sb[:, :], waits=lastall, inc="dDB")
        em.dma("gpsimd", g["dbg2"][:, 0:2048], Aadd[0][:, :], waits=lastall, inc="dDB")
        em.dma("gpsimd", g["dbg2"][:, 2048:4096], A_sb[:, :], waits=lastall, inc="dDB")
        em.dma("gpsimd", g["dbg3"][:, 0:1024], g["o_st"][:, 0:1024], waits=lastall, inc="dDB")
        em.dma("gpsimd", g["dbg4"][:, 0:4], g["mxl"][:, :], waits=lastall, inc="dDB",
               allow_slow_non_contiguous=True)
        em.dma("gpsimd", g["dbg3"][:, 1024:3072], g["embr_sb"][:, :], waits=lastall, inc="dDB")
        em.dma("gpsimd", g["dbg5"],
               g["skw"][0].rearrange("(p a) -> p a", a=2560),
               waits=lastall, inc="dDB")
    dWO = em.sem("dWO")
    for c in range(8):
        em.dma("gpsimd", g["wo0_sb"][:, c * D:(c + 1) * D],
               g["in_wo0"][c * 128:(c + 1) * 128, :],
               waits=[pair_done], inc=dWO)
    wo_ready = em.ev(dWO)
    dOL = em.sem("dOL")
    os_view = g["o_s"].rearrange("(hp p n) -> hp p n", hp=8, p=128).transpose([1, 0, 2])
    epi = None
    old_ev = [None, None]
    for t in range(NT):
        par = t % 2
        lw = [em.ev(dOS), old_ev[par]]
        lde = em.dma("sync", g["o_ld"][par][:, :].rearrange("p (hp n) -> p hp n", hp=8),
                     os_view[:, :, t * 128:(t + 1) * 128], waits=lw, inc=dOL)
        for c in range(8):
            for nh in range(2):
                w = [lde, wo_ready, qbe, exp_prev]
                if epi is not None and c == 0 and nh == 0:
                    w.append(epi)
                em.emit("tensor", lambda e, c=c, nh=nh, par=par, t=t: e.matmul(
                    ps_S[:, par * 1024 + nh * 512: par * 1024 + nh * 512 + 512],
                    g["o_ld"][par][:, c * 128:(c + 1) * 128],
                    g["wo0_sb"][:, c * D + nh * 512: c * D + nh * 512 + 512],
                    start=(c == 0), stop=False, skip_group_check=True), waits=w)
        for nh in range(2):
            mm = em.emit("tensor", lambda e, nh=nh, par=par: e.matmul(
                ps_S[:, par * 1024 + nh * 512: par * 1024 + nh * 512 + 512],
                g["ones_row_bf"][0:1, :],
                g["obrow"][0:1, nh * 512:(nh + 1) * 512],
                start=False, stop=True, skip_group_check=True),
                waits=[qbe], inc="pe" if nh == 1 else None)
        old_ev[par] = mm
        epi = em.emit("vector", lambda e, t=t, par=par: e.scalar_tensor_tensor(
            g["x_res"][:, t * D:(t + 1) * D], ps_S[:, par * 1024: par * 1024 + 1024],
            cval(g, 0.03125), g["x_res"][:, t * D:(t + 1) * D],
            ALU.mult, ALU.add), waits=[mm])
    return epi


def build_consts(em, g):
    for i, v in enumerate(CVALS):
        em.emit("gpsimd", lambda e, i=i, v=v: e.memset(g["cc"][:, i:i + 1], v))
    mz = em.emit("gpsimd", lambda e: e.memset(g["ident"][:, :], 0.0), inc="po")
    em.emit("gpsimd", lambda e: e.affine_select(
        out=g["ident"][:, :], in_=g["ident"][:, :], compare_op=ALU.not_equal,
        fill=1.0, base=0, pattern=[[-1, 128]], channel_multiplier=1), waits=[mz])
    em.emit("gpsimd", lambda e: e.memset(g["ones_col_bf"][:, :], 1.0))
    em.emit("gpsimd", lambda e: e.memset(g["ones_row_bf"][:, :], 1.0))
    em.emit("gpsimd", lambda e: e.memset(g["ones_col_f"][:, :], 1.0))
    em.emit("gpsimd", lambda e: e.memset(g["ones_row_f"][:, :], 1.0), inc="po")
    return em.ev("po")


def build_program(phases, unfused_io=True):
    nc = bass.Bass("TRN2", target_bir_lowering=False, detect_race_conditions=DETECT_RACES)
    g = {"nc": nc}

    x_in = nc.dram_tensor("x_in", [S, D], F32, kind="ExternalInput").ap()
    x_out = nc.dram_tensor("x_out", [S, D], F32, kind="ExternalOutput").ap()
    if DBG.get("dump"):
        g["dbg1"] = nc.dram_tensor("dbg1", [128, 4096], F32, kind="ExternalOutput").ap()
        g["dbg2"] = nc.dram_tensor("dbg2", [128, 4096], F32, kind="ExternalOutput").ap()
        g["dbg3"] = nc.dram_tensor("dbg3", [128, 3072], F32, kind="ExternalOutput").ap()
        g["dbg4"] = nc.dram_tensor("dbg4", [128, 16], F32, kind="ExternalOutput").ap()
        g["dbg5"] = nc.dram_tensor("dbg5", [128, 2560], F32, kind="ExternalOutput").ap()
    need_l0ffn = any(p.startswith("l0ffn") for p in phases)
    if "l0attn" in phases:
        g["in_qkv"] = nc.dram_tensor("qkv_r", [D, 3 * D], F32, kind="ExternalInput").ap()
        g["in_qb"] = nc.dram_tensor("qb_r", [128, 8], F32, kind="ExternalInput").ap()
        g["in_pos"] = nc.dram_tensor("pos_r", [256, D], F32, kind="ExternalInput").ap()
        g["in_sins"] = nc.dram_tensor("sins_r", [256, S], F32, kind="ExternalInput").ap()
        g["in_wo0"] = nc.dram_tensor("wo0", [D, D], F32, kind="ExternalInput").ap()
        g["in_ob"] = nc.dram_tensor("ob_r", [1, D], F32, kind="ExternalInput").ap()
        g["embr_s"] = nc.dram_tensor("embr_s", [8 * 128 * S], BF16, kind="Internal").ap()
        g["skw"] = (nc.dram_tensor("skw0", [128 * 2560], BF16, kind="Internal").ap(),
                    nc.dram_tensor("skw1", [128 * 2560], BF16, kind="Internal").ap())
        g["o_s"] = nc.dram_tensor("o_s", [8 * 128 * S], BF16, kind="Internal").ap()
    if "l1attn" in phases:
        g["in_wqkv"] = nc.dram_tensor("wqkv_g", [D, 3 * D], F32, kind="ExternalInput").ap()
        g["in_wo"] = nc.dram_tensor("wo1", [D, D], F32, kind="ExternalInput").ap()
        g["in_temp"] = nc.dram_tensor("temp_r", [16], F32, kind="ExternalInput").ap()
        g["in_gamma"] = nc.dram_tensor("gamma_r", [128, 8], F32, kind="ExternalInput").ap()
    if any(p.startswith("l1ffn") for p in phases):
        g["in_wg"] = nc.dram_tensor("wg_t", [D, FF1], I32, kind="ExternalInput").ap()
        g["in_wu"] = nc.dram_tensor("wu_t", [D, FF1], I32, kind="ExternalInput").ap()
        g["in_wd"] = nc.dram_tensor("wd_t", [FF1, D], I32, kind="ExternalInput").ap()
    if need_l0ffn:
        g["in_w_up"] = nc.dram_tensor("w_up_t", [D, FF0], F32, kind="ExternalInput").ap()
        g["in_w_dn"] = nc.dram_tensor("w_dn_t", [FF0, D], F32, kind="ExternalInput").ap()
        g["scr_wup"] = nc.dram_tensor("scr_wup", [D, FF0], BF16, kind="Internal").ap()
        g["scr_wdn"] = nc.dram_tensor("scr_wdn", [FF0, D], BF16, kind="Internal").ap()

    with ExitStack() as ctx:
        em = Em(nc, ctx)
        for s in ("pe", "ve", "se", "po", "dX"):
            em.sem(s)

        def sb(name, shape, dtype=F32):
            return ctx.enter_context(nc.sbuf_tensor(name, shape, dtype))

        g["x_res"] = sb("x_res", [128, NT * D])
        g["ident"] = sb("ident", [128, 128], BF16)
        g["ones_col_bf"] = sb("ones_col_bf", [128, 1], BF16)
        g["ones_row_bf"] = sb("ones_row_bf", [1, 128], BF16)
        g["ones_col_f"] = sb("ones_col_f", [128, 1])
        g["ones_row_f"] = sb("ones_row_f", [1, 128])
        g["bn12"] = sb("bn12", [128, 12])
        g["mv"] = sb("mv", [128, 2 * NT])
        g["scol"] = sb("scol", [128, NT])
        g["bcol"] = sb("bcol", [128, NT])
        g["c16a"] = sb("c16a", [128, NT])
        g["c16b"] = sb("c16b", [128, NT])
        g["scratch8"] = sb("scratch8", [128, 8])
        g["nsc"] = sb("nsc", [128, 4])
        g["cc"] = sb("cc", [128, len(CVALS)])
        g["shared2k"] = sb("shared2k", [128, 512])[:, :]
        g["xt_bf"] = sb("xt_bf", [128, D], BF16)[:, :]

        dX = "dX"
        for t in range(NT):
            em.dma("sync", g["x_res"][:, t * D:(t + 1) * D],
                   x_in[t * 128:(t + 1) * 128, :], inc=dX)
        xall = em.ev(dX)
        for e in ENGINES:
            em.emit(e, lambda eng: None, waits=[xall], real=False)

        cev = build_consts(em, g)
        g["consts_ev"] = cev
        for e in ("tensor", "vector", "scalar"):
            em.emit(e, lambda eng: None, waits=[cev], real=False)

        first_l0ffn_done = False
        for phase_i, ph in enumerate(phases):
            with ExitStack() as pctx:
                def psb(name, shape, dtype=F32, phase_i=phase_i):
                    return pctx.enter_context(
                        nc.sbuf_tensor(f"p{phase_i}_{name}", shape, dtype))

                def pps(name, shape, dtype=F32, phase_i=phase_i):
                    return pctx.enter_context(
                        nc.psum_tensor(f"p{phase_i}_{name}", shape, dtype))

                if ph.startswith("l0ffn"):
                    g["ps_tr"] = (pps("ps_tr0", [128, 512], BF16),
                                  pps("ps_tr1", [128, 512], BF16))
                    g["ps_tr_free"] = [em.ev("pe"), em.ev("pe")]
                    g["ps_up"] = (pps("ps_up0", [128, 512]), pps("ps_up1", [128, 512]))
                    g["ps_y"] = (pps("ps_y0", [128, 1024]), pps("ps_y1", [128, 1024]))
                    g["wup_sb"] = psb("wup_sb", [128, 8 * FF0], BF16)
                    g["wdn_sb"] = psb("wdn_sb", [128, 32 * D], BF16)
                    g["xnf2"] = (psb("xnf0", [128, D], BF16), psb("xnf1", [128, D], BF16))
                    g["hbf"] = (psb("hbf0", [128, 512], BF16), psb("hbf1", [128, 512], BF16))
                    g["htf"] = (psb("htf0", [128, 512], BF16), psb("htf1", [128, 512], BF16))
                    g["qacc"] = psb("qacc", [128, 8])
                    do_prep = not first_l0ffn_done
                    if do_prep:
                        g["ps_b"] = g["ps_up"][1]
                        g["ps_b_free"] = em.ev("pe")
                        g["stg_w"] = g["shared2k"]
                        g["stg_w2"] = g["xt_bf"].bitcast(F32)
                        g["tern_acc"] = g["nsc"]
                        g["thr_row"] = psb("thr_row", [1, 1])
                        g["thr_col"] = psb("thr_col", [128, 1])
                        g["nthr_col"] = psb("nthr_col", [128, 1])
                        g["tern_sgn"] = g["xnf2"][0]
                        g["tern_gt"] = g["xnf2"][1]
                        g["tern_stg_free"] = [em.ev("ve")] * 4
                        g["tern_sgn_free"] = [em.ev("ve")] * 4
                        g["write_scratch"] = True
                    emit_l0ffn(em, g, do_prep=do_prep,
                               from_scratch=first_l0ffn_done)
                    first_l0ffn_done = True
                elif ph.startswith("l1ffn"):
                    g["ps_tr"] = (pps("ps_tr0", [128, 512], BF16),
                                  pps("ps_tr1", [128, 512], BF16))
                    g["ps_tr_free"] = [em.ev("pe"), em.ev("pe")]
                    g["ps_gu"] = ((pps("ps_g0", [128, 512]), pps("ps_g1", [128, 512])),
                                  (pps("ps_u0", [128, 512]), pps("ps_u1", [128, 512])))
                    g["ps_y"] = pps("ps_y", [128, 1024])[:, :]
                    g["wg_sb"] = psb("wg_sb", [128, 8 * (FF1 // 2)], BF16)
                    g["wu_sb"] = psb("wu_sb", [128, 8 * (FF1 // 2)], BF16)
                    g["wd_sb"] = psb("wd_sb", [128, 12 * D], BF16)
                    g["xnf_all"] = psb("xnf_all", [128, NT * D], BF16)
                    g["hbf"] = (psb("hbf0", [128, 512], BF16), psb("hbf1", [128, 512], BF16))
                    g["htf"] = (psb("htf0", [128, 512], BF16), psb("htf1", [128, 512], BF16))
                    g.pop("l1w_free", None)
                    emit_l1ffn(em, g)
                elif ph == "l0attn":
                    emit_l0attn_alloc(g, em, psb, pps)
                    emit_l0attn(em, g, do_prep=not g.get("l0attn_prep_done"))
                    g["l0attn_prep_done"] = True
                elif ph == "l1attn":
                    emit_l1attn_alloc(g, em, psb, pps)
                    emit_l1attn(em, g)
                else:
                    raise NotImplementedError(ph)
                em.barrier()

        dO = em.sem("dO")
        for t in range(NT):
            em.dma("sync", x_out[t * 128:(t + 1) * 128, :],
                   g["x_res"][:, t * D:(t + 1) * D], inc=dO)
        em.emit("sync", lambda e: None, waits=[em.ev("dO")], real=False)
        em.drain_dmas()

        blk = ctx.enter_context(nc.Block())
        em.play(blk)
    return nc



def prep_inputs(inputs):
    d = {}
    d["w_up_t"] = np.ascontiguousarray(inputs["L0_ffn_w_up"].T)
    d["w_dn_t"] = np.ascontiguousarray(inputs["L0_ffn_w_down"].T)
    d["wg_t"] = np.ascontiguousarray(inputs["L1_ffn_w_gate"].T)
    d["wu_t"] = np.ascontiguousarray(inputs["L1_ffn_w_up"].T)
    d["wd_t"] = np.ascontiguousarray(inputs["L1_ffn_w_down"].T)
    HEADS, FREQS, MAX_PERIOD = 16, 256, 10000
    pos = np.arange(S, dtype=np.float64)[:, None]
    idx = np.arange(FREQS)[None, :]
    inv_freq = MAX_PERIOD ** (-(idx // 2 * 2) / FREQS)
    angle = pos * inv_freq
    sins = np.where(idx % 2 == 0, np.sin(angle), np.cos(angle)).astype(np.float32)
    sins = np.float32(np.sqrt(2.0)) * sins
    sins_scaled = sins * np.float32(FREQS ** (-0.5))
    d["sins_r"] = np.ascontiguousarray(sins_scaled[::-1, :].T)
    d["qkv_r"] = np.ascontiguousarray(inputs["L0_attn_qkv"].reshape(D, 3 * D))
    d["qb_r"] = np.ascontiguousarray(inputs["L0_attn_q_bias"].reshape(1024).reshape(8, 128).T)
    d["pos_r"] = np.ascontiguousarray(inputs["L0_attn_positional"].reshape(256, D))
    d["wo0"] = np.ascontiguousarray(inputs["L0_attn_out_kernel"])
    d["ob_r"] = np.ascontiguousarray(inputs["L0_attn_out_bias"].reshape(1, D))
    d["wqkv_g"] = np.ascontiguousarray(inputs["L1_attn_w_qkv"])
    d["wo1"] = np.ascontiguousarray(inputs["L1_attn_w_out"])
    d["temp_r"] = np.ascontiguousarray(inputs["L1_attn_temperature"].reshape(16))
    d["gamma_r"] = np.ascontiguousarray(inputs["L1_attn_gamma"].reshape(8, 128).T)
    return d


def kernel(**inputs):
    x = np.ascontiguousarray(inputs["L0_attn_input"])
    host = prep_inputs(inputs)
    phases = list(ALL_PHASES)
    nc = build_program(phases)
    in_maps = []
    for c in range(8):
        m = {"x_in": np.ascontiguousarray(x[c])}
        m.update(host)
        in_maps.append(m)
    res = run_bass_kernel_spmd(nc, in_maps, core_ids=list(range(8)))
    out = np.stack([r["x_out"] for r in res.results], axis=0)
    return out.astype(np.float32)


# version 06 (final)
# speedup vs baseline: 1.3774x; 1.0456x over previous
import math
import numpy as np
import ml_dtypes
from contextlib import ExitStack

import concourse.bass as bass
import concourse.mybir as mybir
from concourse.bass_utils import run_bass_kernel_spmd

F32 = mybir.dt.float32
BF16 = mybir.dt.bfloat16
I32 = mybir.dt.int32
AF = mybir.ActivationFunctionType
ALU = mybir.AluOpType
AX = mybir.AxisListType
FP = np.float32
DETECT_RACES = False

D = 1024
S = 2048
NT = 16
H = 16
HD = 64
FF0 = 4096
FF1 = 3072
C_QKV = float((3.0 * D * HD * H) ** (-0.25))
C_OUT = float((D * D) ** (-0.25))
SM = float(HD ** (-0.5))
NEG = -1e9
SKW = 2176

ENGINES = ("sync", "scalar", "vector", "tensor", "gpsimd")

DBG = {"stats": True, "prep": True, "main_tiles": NT, "fcs": 8, "epi": True, "newton": True}

ALL_PHASES = ("l0ffn_a", "l0attn", "l0ffn_b", "l1ffn_a", "l1attn", "l1ffn_b")


class Em:

    def __init__(self, nc, ctx):
        self.nc = nc
        self.ctx = ctx
        self.q = {e: [] for e in ENGINES}
        self.last_real = {e: None for e in ENGINES}
        self.semh = {}
        self.cnt = {}
        self.lastw = {}
        self.dma_eng = {}

    def sem(self, name):
        if name not in self.semh:
            self.semh[name] = self.ctx.enter_context(self.nc.semaphore(name))
            self.cnt[name] = 0
        return name

    def ev(self, name):
        return (name, self.cnt[name])

    AUTO_SER = {"vector": "ve", "scalar": "se", "gpsimd": "po"}

    def emit(self, eng, fn, waits=(), inc=None, inc_n=1, real=True, _dma=False):
        waits = list(waits)
        if real and not _dma and eng in self.AUTO_SER:
            cs = self.AUTO_SER[eng]
            self.sem(cs)
            waits.append((cs, self.cnt[cs]))
            if inc is None:
                inc = cs
            else:
                assert inc == cs, (eng, inc)
        rw = []
        for w in waits:
            if w is None:
                continue
            s, v = w
            if v <= 0:
                continue
            if self.lastw.get((eng, s), 0) >= v:
                continue
            self.lastw[(eng, s)] = v
            rw.append((s, v))
        tgt = None
        if inc is not None:
            assert real
            self.sem(inc)
            self.cnt[inc] += inc_n
            tgt = (inc, self.cnt[inc], inc_n)
        entry = [rw, fn, tgt, real]
        self.q[eng].append(entry)
        if real:
            self.last_real[eng] = entry
        return self.ev(inc) if inc is not None else None

    def dma(self, eng, out, in_, waits=(), inc=None, **kw):
        if inc is not None:
            self.dma_eng[inc] = eng
        return self.emit(eng, lambda e: e.dma_start(out=out, in_=in_, **kw),
                         waits=waits, inc=inc, inc_n=16, _dma=True)

    def drain_dmas(self):
        for sname, eng in list(self.dma_eng.items()):
            self.emit(eng, lambda e: None, waits=[self.ev(sname)], real=False)

    def engine_done_ev(self, e):
        entry = self.last_real[e]
        if entry is None:
            return None
        if entry[2] is None:
            sname = {"sync": "qS", "scalar": "qA", "vector": "ve", "tensor": "pe",
                     "gpsimd": "po"}[e]
            self.sem(sname)
            self.cnt[sname] += 1
            entry[2] = (sname, self.cnt[sname], 1)
        return (entry[2][0], entry[2][1])

    def barrier(self):
        evs = [self.engine_done_ev(e) for e in ENGINES]
        evs += [self.ev(s) for s in self.dma_eng]
        for e in ENGINES:
            self.emit(e, lambda eng: None, waits=evs, real=False)

    def play(self, blk):
        semh = self.semh

        def run(eng, stream):
            for rw, fn, tgt, _real in stream:
                for (s, v) in rw:
                    eng.wait_ge(semh[s], v)
                inst = fn(eng)
                if tgt is not None and inst is not None:
                    inst.then_inc(semh[tgt[0]], tgt[2])

        for e in ENGINES:
            stream = self.q[e]
            if not stream:
                continue
            getattr(blk, e)(lambda eng, stream=stream: run(eng, stream))


def rsqrt_newton(em, g, out, qcol, scratch, iters=2):
    r = scratch[:, 0:1]
    t = scratch[:, 1:2]
    fa = scratch[:, 2:3]
    em.emit("vector", lambda e: e.tensor_copy(fa, qcol.bitcast(I32)))
    em.emit("vector", lambda e: e.tensor_scalar(
        fa, fa, g["cc"][:, CVALS.index(-0.5):CVALS.index(-0.5) + 1],
        g["cc"][:, CVALS.index(1597463007.0):CVALS.index(1597463007.0) + 1],
        ALU.mult, ALU.add))
    em.emit("vector", lambda e: e.tensor_copy(r.bitcast(I32), fa))
    for _ in range(iters):
        em.emit("vector", lambda e: e.tensor_mul(t, r, r))
        em.emit("vector", lambda e: e.tensor_mul(t, t, qcol))
        em.emit("vector", lambda e: e.tensor_scalar(t, t, g["cc"][:, 4:5], g["cc"][:, 5:6], ALU.mult, ALU.add))
        em.emit("vector", lambda e: e.tensor_mul(r, r, t))
    return em.emit("vector", lambda e: e.tensor_copy(out, r), inc="ve")


def emit_ln_stats(em, g, tiles=range(NT)):
    x = g["x_res"]
    bn = g["bn12"]
    mv = g["mv"]
    evs = []
    for t in tiles:
        xt = x[:, t * D:(t + 1) * D]
        w = [g["xready"][t]] if "xready" in g else []
        em.emit("vector", lambda e, xt=xt: e.bn_stats(bn[:, 0:6], xt[:, 0:512]), waits=w)
        em.emit("vector", lambda e, xt=xt: e.bn_stats(bn[:, 6:12], xt[:, 512:1024]))
        evs.append(em.emit("vector", lambda e, t=t: e.bn_aggr(
            mv[:, 2 * t:2 * t + 2], bn[:, 0:12]), inc="ve"))
    return evs


def emit_scale_chain_l0ffn(em, g, stats_evs):
    mv, sc = g["mv"], g["scratch8"]
    scol, bcol = g["scol"], g["bcol"]
    ra, rb = g["c16a"], g["c16b"]
    for t in range(NT):
        v = mv[:, 2 * t + 1:2 * t + 2]
        lv1 = sc[:, 0:1]
        lv0 = sc[:, 1:2]
        w = [stats_evs[t]]
        em.emit("scalar", lambda e, v=v, lv1=lv1: e.activation(
            lv1, v, AF.Ln, bias=FP(1e-5)), waits=w)
        em.emit("scalar", lambda e, v=v, lv0=lv0: e.activation(lv0, v, AF.Ln))
        em.emit("scalar", lambda e, lv1=lv1, t=t: e.activation(
            ra[:, t:t + 1], lv1, AF.Exp, scale=FP(-0.5)))
        se = em.emit("scalar", lambda e, lv0=lv0, t=t: e.activation(
            rb[:, t:t + 1], lv0, AF.Exp, scale=FP(0.5)),
            inc="se" if t == NT - 1 else None)
    for t in range(NT):
        m = mv[:, 2 * t:2 * t + 1]
        em.emit("vector", lambda e, t=t: e.tensor_mul(
            sc[:, 4:5], rb[:, t:t + 1], ra[:, t:t + 1]), waits=[se])
        em.emit("vector", lambda e: e.tensor_scalar_add(sc[:, 4:5], sc[:, 4:5], cval(g, 1e-8)))
        em.emit("vector", lambda e: e.reciprocal(sc[:, 6:7], sc[:, 4:5]))
        em.emit("vector", lambda e, t=t: e.tensor_mul(
            scol[:, t:t + 1], ra[:, t:t + 1], sc[:, 6:7]))
        em.emit("vector", lambda e, m=m, t=t: e.tensor_mul(
            sc[:, 7:8], m, scol[:, t:t + 1]))
        em.emit("vector", lambda e, t=t: e.tensor_scalar_mul(
            bcol[:, t:t + 1], sc[:, 7:8], cval(g, -1.0)), inc="ve")
    return em.ev("ve")


def emit_scale_chain_ln(em, g, stats_evs, extra=None):
    mv, sc = g["mv"], g["scratch8"]
    scol, bcol = g["scol"], g["bcol"]
    ra, rb = g["c16a"], g["c16b"]
    for t in range(NT):
        v = mv[:, 2 * t + 1:2 * t + 2]
        lv1 = sc[:, 0:1]
        w = [stats_evs[t]]
        em.emit("scalar", lambda e, v=v, lv1=lv1: e.activation(
            lv1, v, AF.Ln, bias=FP(1e-5)), waits=w)
        em.emit("scalar", lambda e, lv1=lv1, t=t: e.activation(
            ra[:, t:t + 1], lv1, AF.Exp, scale=FP(-0.5)))
        if extra == "l1attn":
            lv0 = sc[:, 1:2]
            em.emit("scalar", lambda e, v=v, lv0=lv0: e.activation(lv0, v, AF.Ln))
            em.emit("scalar", lambda e, lv0=lv0, t=t: e.activation(
                rb[:, t:t + 1], lv0, AF.Exp, scale=FP(0.5)))
    se = em.engine_done_ev("scalar")
    for t in range(NT):
        m = mv[:, 2 * t:2 * t + 1]
        if extra == "l1attn":
            em.emit("vector", lambda e, t=t: e.tensor_mul(
                sc[:, 4:5], rb[:, t:t + 1], ra[:, t:t + 1]), waits=[se])
            em.emit("vector", lambda e: e.tensor_scalar_max(
                sc[:, 4:5], sc[:, 4:5], cval(g, 1e-12 / 32.0)))
            em.emit("vector", lambda e: e.reciprocal(sc[:, 5:6], sc[:, 4:5]))
            em.emit("vector", lambda e, t=t: e.tensor_mul(
                scol[:, t:t + 1], ra[:, t:t + 1], sc[:, 5:6]))
        else:
            em.emit("vector", lambda e, t=t: e.tensor_copy(
                scol[:, t:t + 1], ra[:, t:t + 1]), waits=[se])
        em.emit("vector", lambda e, m=m, t=t: e.tensor_mul(
            sc[:, 7:8], m, scol[:, t:t + 1]))
        em.emit("vector", lambda e, t=t: e.tensor_scalar_mul(
            bcol[:, t:t + 1], sc[:, 7:8], cval(g, -1.0)), inc="ve")
    return em.ev("ve")


def emit_xn_tile(em, g, t, xnf, par_ev=None, gamma=None, dc_major=False):
    x = g["x_res"]
    xt_bf = g["xt_bf"]
    scol, bcol = g["scol"], g["bcol"]
    ps0, ps1 = g["ps_tr"]
    ident = g["ident"]
    ve = em.emit("vector", lambda e: e.tensor_scalar(
        xt_bf[:, :], x[:, t * D:(t + 1) * D], scol[:, t:t + 1], bcol[:, t:t + 1],
        ALU.mult, ALU.add), waits=[par_ev] if par_ev else [], inc="ve")
    pe_evs = []
    for half in range(2):
        ps = (ps0, ps1)[half]
        for j in range(4):
            dc = half * 4 + j
            pe = em.emit("tensor", lambda e, ps=ps, j=j, dc=dc: e.transpose(
                ps[:, j * 128:(j + 1) * 128], xt_bf[:, dc * 128:(dc + 1) * 128],
                ident[:, :]),
                waits=[ve, g["ps_tr_free"][half]],
                inc="pe" if j == 3 else None)
        if dc_major:
            dst = xnf.rearrange("p (dc n) -> p dc n", dc=8)[
                :, half * 4:(half + 1) * 4, t * 128:(t + 1) * 128]
            veo = em.emit("vector", lambda e, ps=ps, dst=dst: e.tensor_copy(
                dst, ps[:, :].rearrange("p (c n) -> p c n", c=4)),
                waits=[pe], inc="ve")
        elif gamma is None:
            veo = em.emit("vector", lambda e, ps=ps, half=half: e.tensor_copy(
                xnf[:, half * 512:(half + 1) * 512], ps[:, :]), waits=[pe], inc="ve")
        else:
            gcol, gev = gamma
            for j in range(4):
                dc = half * 4 + j
                veo = em.emit("vector", lambda e, ps=ps, j=j, dc=dc: e.tensor_scalar(
                    xnf[:, dc * 128:(dc + 1) * 128], ps[:, j * 128:(j + 1) * 128],
                    gcol[:, dc:dc + 1], None, ALU.mult),
                    waits=[pe, gev], inc="ve")
        g["ps_tr_free"][half] = veo
        pe_evs.append(veo)
    return pe_evs[-1]


def emit_l0ffn_prep(em, g, write_scratch=True):
    acc = g["tern_acc"]
    th_ps = g["ps_b"]
    stgs = [g["stg_w"][:, 0:256], g["stg_w"][:, 256:512],
            g["stg_w2"][:, 0:256], g["stg_w2"][:, 256:512]]
    queues = ["sync", "scalar", "sync", "scalar"]
    for wi, (src, n_chunks, res, scr) in enumerate([
            (g["in_w_up"], 8, g["wup_sb"], g["scr_wup"]),
            (g["in_w_dn"], 32, g["wdn_sb"], g["scr_wdn"])]):
        ncols = FF0 if wi == 0 else D
        PW = 256
        npieces = ncols // PW
        pieces = [(c, p) for c in range(n_chunks) for p in range(npieces)]
        mz = em.emit("vector", lambda e: e.memset(acc[:, 0:2], 0.0), inc="ve")
        ve = mz
        for i, (c, p) in enumerate(pieces):
            par = i % 4
            dsem = em.sem(f"dW{wi}{par}")
            sl = stgs[par]
            ld = em.dma(queues[par], sl,
                        src[c * 128:(c + 1) * 128, p * PW:(p + 1) * PW],
                        waits=[g["tern_stg_free"][par]], inc=dsem)
            em.emit("vector", lambda e, sl=sl: e.tensor_reduce(
                acc[:, 1:2], sl, axis=AX.X, op=ALU.add,
                apply_absolute_value=True), waits=[ld])
            ve = em.emit("vector", lambda e: e.tensor_add(
                acc[:, 0:1], acc[:, 0:1], acc[:, 1:2]), inc="ve")
            g["tern_stg_free"][par] = ve
        pe = em.emit("tensor", lambda e: e.matmul(
            th_ps[0:1, 0:1], acc[:, 0:1], g["ones_col_f"][:, :],
            start=True, stop=True), waits=[ve, g["ps_b_free"]], inc="pe")
        vth = em.emit("vector", lambda e: e.tensor_scalar_mul(
            g["thr_row"][0:1, 0:1], th_ps[0:1, 0:1],
            cval(g, 0.5 / (8 * 128 * FF0))[0:1, :]), waits=[pe], inc="ve")
        pe2 = em.emit("tensor", lambda e: e.matmul(
            th_ps[:, 4:5], g["ones_row_f"][0:1, :], g["thr_row"][0:1, 0:1],
            start=True, stop=True), waits=[vth], inc="pe")
        em.emit("vector", lambda e: e.tensor_copy(
            g["thr_col"][:, 0:1], th_ps[:, 4:5]), waits=[pe2])
        vcol = em.emit("vector", lambda e: e.tensor_scalar_mul(
            g["nthr_col"][:, 0:1], g["thr_col"][:, 0:1], cval(g, -1.0)), inc="ve")
        g["ps_b_free"] = vcol
        for i, (c, p) in enumerate(pieces):
            par = i % 4
            dsem = em.sem(f"dW{wi}{par}")
            sl = stgs[par]
            ld = em.dma(queues[par], sl,
                        src[c * 128:(c + 1) * 128, p * PW:(p + 1) * PW],
                        waits=[g["tern_stg_free"][par]], inc=dsem)
            sgn = g["tern_sgn"][:, par * PW:par * PW + PW]
            gtb = g["tern_gt"][:, par * PW:par * PW + PW]
            em.emit("vector", lambda e, sl=sl, sgn=sgn: e.tensor_single_scalar(
                sgn, sl, g["thr_col"][:, 0:1], ALU.is_gt),
                waits=[ld, vcol, g["tern_sgn_free"][par]])
            em.emit("vector", lambda e, sl=sl, gtb=gtb: e.tensor_single_scalar(
                gtb, sl, g["nthr_col"][:, 0:1], ALU.is_lt))
            dst = res[:, c * ncols + p * PW: c * ncols + p * PW + PW]
            ve = em.emit("vector", lambda e, dst=dst, sgn=sgn, gtb=gtb: e.tensor_sub(
                dst, sgn, gtb), inc="ve")
            g["tern_sgn_free"][par] = ve
            g["tern_stg_free"][par] = ve
            if write_scratch:
                em.dma("gpsimd", scr[c * 128:(c + 1) * 128, p * PW:(p + 1) * PW],
                       dst, waits=[ve], inc=em.sem("dWS"))
    return em.ev("ve")


def emit_l0ffn(em, g, do_prep, from_scratch):
    nc = g["nc"]
    if DBG["stats"]:
        stats = emit_ln_stats(em, g)
        sev = emit_scale_chain_l0ffn(em, g, stats)
    if do_prep and DBG["prep"]:
        emit_l0ffn_prep(em, g, write_scratch=g["write_scratch"])
    if from_scratch:
        dW = em.sem("dWL")
        for c in range(8):
            em.dma("sync", g["wup_sb"][:, c * FF0:(c + 1) * FF0],
                   g["scr_wup"][c * 128:(c + 1) * 128, :], inc=dW)
        for c in range(32):
            em.dma("sync", g["wdn_sb"][:, c * D:(c + 1) * D],
                   g["scr_wdn"][c * 128:(c + 1) * 128, :], inc=dW)
        wready = em.ev(("dWL"))
    else:
        wready = em.ev("ve")
    wup, wdn = g["wup_sb"], g["wdn_sb"]
    ps_up = g["ps_up"]
    ps_y = g["ps_y"]
    xnf2 = g["xnf2"]
    hbf = g["hbf"]
    htf = g["htf"]
    qacc = g["qacc"]
    sc = g["scratch8"]

    xnf_free = [em.ev("pe"), em.ev("pe")]
    gelu_done = [None, None]
    sumsq_done = [None, None]
    htf_free = [em.ev("pe"), em.ev("pe")]
    epi_ev = [None, None]
    for t in range(DBG["main_tiles"]):
        par = t % 2
        xnf = xnf2[par]
        xe = emit_xn_tile(em, g, t, xnf, par_ev=xnf_free[par])
        sumsq_evs = []
        NFC = DBG["fcs"]

        def tdn(fc, ge):
            fp = fc % 2
            for j in range(4):
                pe_t = em.emit("tensor", lambda e, fp=fp, j=j: e.transpose(
                    g["ps_tr"][fp][:, j * 128:(j + 1) * 128],
                    hbf[fp][:, j * 128:(j + 1) * 128], g["ident"][:, :]),
                    waits=[ge, g["ps_tr_free"][fp]],
                    inc="pe" if j == 3 else None)
            ve_t = em.emit("vector", lambda e, fp=fp: e.tensor_copy(
                htf[fp][:, :], g["ps_tr"][fp][:, :]),
                waits=[pe_t], inc="ve")
            g["ps_tr_free"][fp] = ve_t
            out = None
            for fs in range(4):
                fcc = fc * 4 + fs
                for nh in range(2):
                    w = [ve_t, wready]
                    if fc == 0 and fs == 0 and epi_ev[par] is not None:
                        w.append(epi_ev[par])
                    last = (fc == NFC - 1 and fs == 3 and nh == 1)
                    mm = em.emit("tensor",
                            lambda e, par=par, fs=fs, nh=nh, fcc=fcc, fc=fc: e.matmul(
                                ps_y[par][:, nh * 512:(nh + 1) * 512],
                                htf[fc % 2][:, fs * 128:(fs + 1) * 128],
                                wdn[:, fcc * D + nh * 512: fcc * D + nh * 512 + 512],
                                start=(fc == 0 and fs == 0),
                                stop=(fc == NFC - 1 and fs == 3)), waits=w,
                            inc="pe" if last else None)
                    if last:
                        out = mm
            return out

        pend = None
        pe_dn = None
        for fc in range(NFC):
            fp = fc % 2
            for dc in range(8):
                w = [xe, wready]
                if gelu_done[fp] is not None and dc == 0:
                    w.append(gelu_done[fp])
                pe_up = em.emit("tensor", lambda e, fp=fp, dc=dc, fc=fc, xnf=xnf: e.matmul(
                    ps_up[fp][:, :], xnf[:, dc * 128:(dc + 1) * 128],
                    wup[:, dc * FF0 + fc * 512: dc * FF0 + fc * 512 + 512],
                    start=(dc == 0), stop=(dc == 7)), waits=w,
                    inc="pe" if dc == 7 else None)
            if pend is not None:
                r = tdn(*pend)
                if r is not None:
                    pe_dn = r
            gw = [pe_up]
            if sumsq_done[fp] is not None:
                gw.append(sumsq_done[fp])
            ge = em.emit("scalar", lambda e, fp=fp: e.activation(
                hbf[fp][:, :], ps_up[fp][:, :], AF.Gelu), waits=gw, inc="se")
            gelu_done[fp] = ge
            sq = em.emit("vector", lambda e, fp=fp, fc=fc: e.scalar_tensor_tensor(
                htf[fp][:, :], hbf[fp][:, :], cval(g, 1.0), hbf[fp][:, :], ALU.mult, ALU.mult,
                accum_out=qacc[:, fc:fc + 1]), waits=[ge, g["ps_tr_free"][fp]], inc="ve")
            sumsq_evs.append(sq)
            sumsq_done[fp] = sq
            pend = (fc, ge)
        r = tdn(*pend)
        if r is not None:
            pe_dn = r
        htf_free = [pe_dn, pe_dn]
        xnf_free[par] = pe_dn
        if not DBG["epi"]:
            epi_ev[par] = pe_dn
            continue
        em.emit("vector", lambda e: e.tensor_reduce(
            sc[:, 0:1], qacc[:, :], axis=AX.X, op=ALU.add), waits=sumsq_evs[-1:])
        em.emit("vector", lambda e: e.tensor_scalar_mul(sc[:, 0:1], sc[:, 0:1], cval(g, 1.0 / FF0)))
        em.emit("vector", lambda e: e.tensor_scalar_max(sc[:, 0:1], sc[:, 0:1], cval(g, 1e-30)))
        if DBG["newton"]:
            rsqrt_newton(em, g, sc[:, 1:2], sc[:, 0:1], g["nsc"])
        else:
            em.emit("vector", lambda e: e.tensor_copy(sc[:, 1:2], sc[:, 0:1]))
        em.emit("vector", lambda e: e.tensor_mul(sc[:, 2:3], sc[:, 0:1], sc[:, 1:2]))
        em.emit("vector", lambda e: e.tensor_scalar_add(sc[:, 2:3], sc[:, 2:3], cval(g, 1e-8)))
        em.emit("vector", lambda e: e.reciprocal(sc[:, 3:4], sc[:, 2:3]))
        em.emit("vector", lambda e: e.tensor_scalar_mul(sc[:, 3:4], sc[:, 3:4], cval(g, 0.5)))
        ep = em.emit("vector", lambda e, t=t, par=par: e.scalar_tensor_tensor(
            g["x_res"][:, t * D:(t + 1) * D], ps_y[par][:, :], sc[:, 3:4],
            g["x_res"][:, t * D:(t + 1) * D], ALU.mult, ALU.add),
            waits=[pe_dn], inc="ve")
        epi_ev[par] = ep
    if DBG.get("dump"):
        lastve = em.ev("ve")
        em.dma("gpsimd", g["dbg1"], wup[:, 0:4096], waits=[lastve], inc="dDB")
        em.dma("gpsimd", g["dbg2"], wdn[:, 0:4096], waits=[lastve], inc="dDB")
        em.dma("gpsimd", g["dbg3"], xnf2[(DBG["main_tiles"] - 1) % 2][:, :],
               waits=[lastve], inc="dDB")
        em.dma("gpsimd", g["dbg4"][:, 0:8], qacc[:, :], waits=[lastve], inc="dDB", allow_slow_non_contiguous=True)
        em.dma("gpsimd", g["dbg4"][:, 8:9], g["thr_col"][:, 0:1], waits=[lastve], inc="dDB", allow_slow_non_contiguous=True)
        em.dma("gpsimd", g["dbg4"][:, 9:10], sc[:, 3:4], waits=[lastve], inc="dDB", allow_slow_non_contiguous=True)
        em.dma("gpsimd", g["dbg4"][:, 10:11], g["scol"][:, 0:1], waits=[lastve], inc="dDB", allow_slow_non_contiguous=True)
        em.dma("gpsimd", g["dbg4"][:, 11:12], g["bcol"][:, 0:1], waits=[lastve], inc="dDB", allow_slow_non_contiguous=True)
        em.dma("gpsimd", g["dbg4"][:, 12:14], g["mv"][:, 0:2], waits=[lastve], inc="dDB", allow_slow_non_contiguous=True)
        em.dma("gpsimd", g["dbg4"][:, 14:15], g["c16a"][:, 0:1], waits=[lastve], inc="dDB", allow_slow_non_contiguous=True)
        em.dma("gpsimd", g["dbg4"][:, 15:16], g["c16b"][:, 0:1], waits=[lastve], inc="dDB", allow_slow_non_contiguous=True)
    return em.ev("ve")


CVALS = [0.0, 1.0, -1.0, 0.5, -0.5, 1.5, 1e-8, 1.0 / FF0, 1e-12 / 32.0,
         0.5 / (8 * 128 * FF0), 1.0 / 127.0, 0.5 / (127.0 * 127.0),
         1597463007.0, 1e-30, 8.0, 1e-12, 0.125, 32.0, 0.03125]


def cval(g, v):
    i = CVALS.index(v)
    return g["cc"][:, i:i + 1]


def emit_l1ffn(em, g):
    stats = emit_ln_stats(em, g)
    emit_scale_chain_ln(em, g, stats)
    xnf_all = g["xnf_all"]
    wg_sb, wu_sb, wd_sb = g["wg_sb"], g["wu_sb"], g["wd_sb"]
    ps_g, ps_u = g["ps_gu"][0], g["ps_gu"][1]
    ps_y = g["ps_y"]
    hbf = g["hbf"]
    htf = g["htf"]
    HH = FF1 // 2
    xn_evs = []
    for t in range(NT):
        xn_evs.append(emit_xn_tile(em, g, t, xnf_all[:, t * D:(t + 1) * D]))
    g["l1_epi"] = None
    gu_done = [None, None]
    for hh in range(2):
        dW = em.sem(f"dL1{hh}")
        wfree = [g.get("l1w_free")] if g.get("l1w_free") else []
        for dc in range(8):
            em.dma("gpsimd", wg_sb[:, dc * HH:(dc + 1) * HH],
                   g["in_wg"][dc * 128:(dc + 1) * 128, hh * HH:(hh + 1) * HH],
                   waits=wfree, inc=dW)
            em.dma("gpsimd", wu_sb[:, dc * HH:(dc + 1) * HH],
                   g["in_wu"][dc * 128:(dc + 1) * 128, hh * HH:(hh + 1) * HH],
                   waits=wfree, inc=dW)
        for hc in range(12):
            em.dma("gpsimd", wd_sb[:, hc * D:(hc + 1) * D],
                   g["in_wd"][hh * HH + hc * 128: hh * HH + (hc + 1) * 128, :],
                   waits=wfree, inc=dW)
        wready = em.ev(dW)
        for t in range(NT):
            xnf = xnf_all[:, t * D:(t + 1) * D]

            def tdn1(hc, mu, t=t):
                hp = hc % 2
                for j in range(4):
                    pe_t = em.emit("tensor", lambda e, hp=hp, j=j: e.transpose(
                        g["ps_tr"][hp][:, j * 128:(j + 1) * 128],
                        hbf[hp][:, j * 128:(j + 1) * 128], g["ident"][:, :]),
                        waits=[mu, g["ps_tr_free"][hp]],
                        inc="pe" if j == 3 else None)
                ve_t = em.emit("vector", lambda e, hp=hp: e.tensor_copy(
                    htf[hp][:, :], g["ps_tr"][hp][:, :]), waits=[pe_t])
                g["ps_tr_free"][hp] = ve_t
                out = None
                for fs in range(4):
                    hcc = hc * 4 + fs
                    for nh in range(2):
                        w = [ve_t, wready]
                        if hc == 0 and fs == 0 and g.get("l1_epi") is not None:
                            w.append(g["l1_epi"])
                        last = (hc == 2 and fs == 3 and nh == 1)
                        mm = em.emit("tensor",
                                lambda e, fs=fs, nh=nh, hcc=hcc, hc=hc, hp=hp: e.matmul(
                                    ps_y[:, nh * 512:(nh + 1) * 512],
                                    htf[hp][:, fs * 128:(fs + 1) * 128],
                                    wd_sb[:, hcc * D + nh * 512:
                                          hcc * D + nh * 512 + 512],
                                    start=(hc == 0 and fs == 0),
                                    stop=(hc == 2 and fs == 3)), waits=w,
                                inc="pe" if last else None)
                        if last:
                            out = mm
                return out

            pend1 = None
            dn_done = None
            for hc in range(3):
                hp = hc % 2
                for dc in range(8):
                    w = [xn_evs[t], wready]
                    if gu_done[hp] is not None and dc == 0:
                        w.append(gu_done[hp])
                    em.emit("tensor", lambda e, hp=hp, dc=dc, hc=hc, xnf=xnf: e.matmul(
                        ps_g[hp][:, :], xnf[:, dc * 128:(dc + 1) * 128],
                        wg_sb[:, dc * HH + hc * 512: dc * HH + hc * 512 + 512],
                        start=(dc == 0), stop=(dc == 7)), waits=w)
                for dc in range(8):
                    last = dc == 7
                    mm = em.emit("tensor", lambda e, hp=hp, dc=dc, hc=hc, xnf=xnf: e.matmul(
                        ps_u[hp][:, :], xnf[:, dc * 128:(dc + 1) * 128],
                        wu_sb[:, dc * HH + hc * 512: dc * HH + hc * 512 + 512],
                        start=(dc == 0), stop=(dc == 7)),
                        waits=[xn_evs[t], wready],
                        inc="pe" if last else None)
                pe_gu = mm
                if pend1 is not None:
                    r = tdn1(*pend1)
                    if r is not None:
                        dn_done = r
                ge = em.emit("scalar", lambda e, hp=hp: e.activation(
                    hbf[hp][:, :], ps_g[hp][:, :], AF.Silu, scale=FP(1.0 / 127.0)),
                    waits=[pe_gu])
                mu = em.emit("vector", lambda e, hp=hp: e.tensor_tensor(
                    hbf[hp][:, :], hbf[hp][:, :], ps_u[hp][:, :], ALU.mult),
                    waits=[ge])
                gu_done[hp] = mu
                pend1 = (hc, mu)
            r = tdn1(*pend1)
            if r is not None:
                dn_done = r
            g["l1_epi"] = em.emit("vector", lambda e, t=t: e.scalar_tensor_tensor(
                g["x_res"][:, t * D:(t + 1) * D], ps_y[:, :],
                cval(g, 0.5 / (127.0 * 127.0)),
                g["x_res"][:, t * D:(t + 1) * D], ALU.mult, ALU.add),
                waits=[dn_done])
        g["l1w_free"] = em.engine_done_ev("tensor")
    return g["l1_epi"]


def emit_l1attn_alloc(g, em, psb, pps):
    g["ps_tr"] = (pps("ps_tr0", [128, 512], BF16), pps("ps_tr1", [128, 512], BF16))
    g["ps_tr_free"] = [em.ev("pe"), em.ev("pe")]
    g["ps_qk"] = pps("ps_qkA", [128, 1024])[:, :]
    g["ps_sim"] = (pps("ps_simA", [64, 512]), pps("ps_simB", [64, 512]))
    g["ps_ss"] = pps("ps_ss", [128, 512])[:, :]
    g["ps_pv"] = pps("ps_pv", [128, 512])[:, :]
    g["ps_ss_free"] = em.ev("pe")
    g["xnf_free"] = [em.ev("pe"), em.ev("pe")]
    g["wq_sb"] = psb("wq_sb", [128, 8 * 3072], BF16)
    g["wo_sb"] = psb("wo_sb", [128, 8 * D], BF16)
    g["v_fm"] = psb("v_fm", [128, 8 * 2048], BF16)
    g["o_f"] = psb("o_f", [128, 8 * 2048], BF16)[:, :]
    g["qksq"] = g["o_f"][:, 0:2048]
    g["qk_t"] = g["o_f"][:, 2048:4096]
    g["xnf2"] = (g["o_f"][:, 4096:5120], g["o_f"][:, 5120:6144])
    g["t16"] = psb("t16", [16, 2])
    g["rowbuf"] = psb("rowbuf", [1, 16])
    g["expT_bc"] = psb("expT_bc", [64, 16])
    g["sqcol"] = psb("sqcol", [64, 16])
    g["skb"] = psb("skb", [64, 64])
    g["simf"] = psb("simf", [64, 64])
    g["mxc"] = psb("mxc", [64, 4])
    g["attn_bf"] = psb("attn_bf", [64, 64], BF16)
    g["attnT"] = psb("attnT", [128, 64], BF16)
    g["ones128_bf"] = psb("ones128_bf", [128, 128], BF16)
    g["identf"] = psb("identf", [128, 128])
    g["gcol"] = psb("gcol", [128, 8])
    g["ss_sb"] = psb("ss_sb", [1, 2048])


def emit_l1attn(em, g):
    stats = emit_ln_stats(em, g)
    emit_scale_chain_ln(em, g, stats)
    wq, wo = g["wq_sb"], g["wo_sb"]
    qk_t, qksq = g["qk_t"], g["qksq"]
    v_fm, o_f = g["v_fm"], g["o_f"]
    ps_qk = g["ps_qk"]
    ps_sim = g["ps_sim"]
    ps_ss = g["ps_ss"]
    ps_pv = g["ps_pv"]
    xnf2 = g["xnf2"]
    ones128 = g["ones128_bf"]
    em.emit("gpsimd", lambda e: e.memset(ones128[:, :], 1.0))
    em.emit("vector", lambda e: e.tensor_copy(g["identf"][:, :], g["ident"][:, :]),
            waits=[g["consts_ev"]])
    dG = em.sem("dG1")
    gld = em.dma("sync", g["gcol"][:, :], g["in_gamma"], inc=dG)
    dW = em.sem("dL1A")
    for dc in range(8):
        em.dma("gpsimd", wq[:, dc * 3072:(dc + 1) * 3072],
               g["in_wqkv"][dc * 128:(dc + 1) * 128, :], inc=dW)
    for c in range(8):
        em.dma("gpsimd", wo[:, c * D:(c + 1) * D],
               g["in_wo"][c * 128:(c + 1) * 128, :], inc=dW)
    wready = em.ev(dW)
    dT = em.sem("dT")
    tld = em.dma("sync", g["t16"][0:16, 0:1],
                 g["in_temp"].rearrange("(a b) -> a b", b=1), inc=dT)
    eT = em.emit("scalar", lambda e: e.activation(
        g["t16"][0:16, 1:2], g["t16"][0:16, 0:1], AF.Exp), waits=[tld])
    idf = em.engine_done_ev("vector")
    peT = em.emit("tensor", lambda e: e.transpose(
        ps_ss[0:1, 0:16], g["t16"][0:16, 1:2], g["identf"][0:16, 0:16]),
        waits=[eT, idf, g["ps_ss_free"]], inc="pe")
    veT = em.emit("vector", lambda e: e.tensor_copy(
        g["rowbuf"][0:1, 0:16], ps_ss[0:1, 0:16]), waits=[peT])
    peB = em.emit("tensor", lambda e: e.matmul(
        ps_ss[0:64, 16:32], g["ones_row_f"][0:1, 0:64], g["rowbuf"][0:1, 0:16],
        start=True, stop=True), waits=[veT], inc="pe")
    eBC = em.emit("vector", lambda e: e.tensor_copy(
        g["expT_bc"][:, :], ps_ss[0:64, 16:32]), waits=[peB])
    g["ps_ss_free"] = eBC

    sim_last = None
    qk_free = em.ev("pe")
    ss_free = eBC
    ss_add_ev = None
    for t in range(NT):
        par = t % 2
        xe = emit_xn_tile(em, g, t, xnf2[par], par_ev=g["xnf_free"][par],
                          gamma=(g["gcol"], (dG, 16)))
        for gq in range(2):
            for nh in range(2):
                for dc in range(8):
                    w = [xe, wready, qk_free]
                    last = dc == 7 and nh == 1
                    mm = em.emit("tensor", lambda e, gq=gq, nh=nh, dc=dc, par=par: e.matmul(
                        ps_qk[:, nh * 512:(nh + 1) * 512],
                        xnf2[par][:, dc * 128:(dc + 1) * 128],
                        wq[:, dc * 3072 + gq * 1024 + nh * 512:
                           dc * 3072 + gq * 1024 + nh * 512 + 512],
                        start=(dc == 0), stop=(dc == 7)), waits=w,
                        inc="pe" if last else None)
            ev_qk = em.emit("vector", lambda e, gq=gq: e.tensor_copy(
                qk_t[:, gq * 1024:(gq + 1) * 1024], ps_qk[:, :]), waits=[mm])
            qk_free = ev_qk
            em.emit("vector", lambda e, gq=gq: e.scalar_tensor_tensor(
                qksq[:, gq * 1024:(gq + 1) * 1024], qk_t[:, gq * 1024:(gq + 1) * 1024],
                cval(g, 1.0), qk_t[:, gq * 1024:(gq + 1) * 1024], ALU.mult, ALU.mult))
        sq_ev = em.engine_done_ev("vector")
        for h in range(H):
            b = h // 8
            last = h % 8 == 7 and b == 1
            mm = em.emit("tensor", lambda e, h=h, b=b, t=t: e.matmul(
                ps_sim[b][0:64, (h % 8) * 64:(h % 8) * 64 + 64],
                qk_t[:, h * 64:h * 64 + 64],
                qk_t[:, 1024 + h * 64:1024 + h * 64 + 64],
                start=(t == 0 and h % 8 == 0), stop=(t == NT - 1),
                skip_group_check=True), waits=[sq_ev],
                inc="pe" if last and t == NT - 1 else None)
            if last and t == NT - 1:
                sim_last = mm
        for p in range(4):
            pm = em.emit("tensor", lambda e, p=p: e.matmul(
                ps_ss[:, 0:512], ones128[:, :], qksq[:, p * 512:(p + 1) * 512],
                start=True, stop=True), waits=[sq_ev, ss_free], inc="pe")
            if t == 0:
                ae = em.emit("vector", lambda e, p=p: e.tensor_copy(
                    g["ss_sb"][0:1, p * 512:(p + 1) * 512], ps_ss[0:1, 0:512]),
                    waits=[pm])
            else:
                ae = em.emit("vector", lambda e, p=p: e.tensor_add(
                    g["ss_sb"][0:1, p * 512:(p + 1) * 512],
                    g["ss_sb"][0:1, p * 512:(p + 1) * 512], ps_ss[0:1, 0:512]),
                    waits=[pm])
            ss_free = ae
        ss_add_ev = ae
        for half in range(2):
            for ci in range(4):
                c = half * 4 + ci
                for dc in range(8):
                    w = [xe, wready, qk_free]
                    last = dc == 7 and ci == 3
                    mm = em.emit("tensor", lambda e, c=c, ci=ci, half=half, dc=dc, par=par: e.matmul(
                        ps_qk[:, half * 512 + ci * 128: half * 512 + ci * 128 + 128],
                        wq[:, dc * 3072 + 2048 + c * 128: dc * 3072 + 2048 + (c + 1) * 128],
                        xnf2[par][:, dc * 128:(dc + 1) * 128],
                        start=(dc == 0 and ci == 0),
                        stop=(dc == 7), skip_group_check=True), waits=w,
                        inc="pe" if last else None)
            vo = em.emit("vector", lambda e, half=half, t=t: e.tensor_copy(
                v_fm[:, :].rearrange("p (c n) -> p c n", c=8)
                [:, half * 4:(half + 1) * 4, t * 128:(t + 1) * 128],
                ps_qk[:, half * 512:(half + 1) * 512].rearrange(
                    "p (c n) -> p c n", c=4)), waits=[mm])
            qk_free = vo
        g["xnf_free"][par] = em.engine_done_ev("tensor")
    srow = g["ss_sb"]
    lnr = em.emit("scalar", lambda e: e.activation(
        srow[0:1, 0:2048], srow[0:1, 0:2048], AF.Ln), waits=[ss_add_ev])
    sqr = em.emit("scalar", lambda e: e.activation(
        srow[0:1, 0:2048], srow[0:1, 0:2048], AF.Exp, scale=FP(0.5)), waits=[lnr])
    em.emit("vector", lambda e: e.tensor_scalar_max(
        srow[0:1, 0:2048], srow[0:1, 0:2048], cval(g, 1e-12)[0:1, :]),
        waits=[sqr])
    em.emit("vector", lambda e: e.reciprocal(
        srow[0:1, 0:2048], srow[0:1, 0:2048]))
    for h in range(H):
        ve_now = em.ev("ve")
        pt = em.emit("tensor", lambda e, h=h: e.transpose(
            ps_ss[0:64, 32:33], g["ss_sb"][0:1, h * 64:(h + 1) * 64],
            g["identf"][0:1, 0:1]),
            waits=[(("ve"), ve_now[1]), g["ps_ss_free"]], inc="pe")
        ce = em.emit("vector", lambda e, h=h: e.tensor_copy(
            g["sqcol"][0:64, h:h + 1], ps_ss[0:64, 32:33]), waits=[pt])
        g["ps_ss_free"] = ce
    em.emit("vector", lambda e: e.tensor_mul(
        g["sqcol"][0:64, 0:16], g["sqcol"][0:64, 0:16], g["expT_bc"][:, :]))
    sqc_ev = em.emit("vector", lambda e: e.tensor_scalar_mul(
        g["sqcol"][0:64, 0:16], g["sqcol"][0:64, 0:16], cval(g, 8.0)[0:64, :]))
    pv_free = em.ev("ve")
    at_free = em.ev("ve")
    for h in range(H):
        b, col = h // 8, (h % 8) * 64
        sim_ap = ps_sim[b][0:64, col:col + 64]
        pb = em.emit("tensor", lambda e, h=h: e.matmul(
            ps_ss[0:64, 64:128], g["ones_row_f"][0:1, 0:64],
            g["ss_sb"][0:1, 1024 + h * 64:1024 + (h + 1) * 64],
            start=True, stop=True), waits=[sqc_ev, g["ps_ss_free"]], inc="pe")
        cb = em.emit("vector", lambda e: e.tensor_copy(
            g["skb"][0:64, 0:64], ps_ss[0:64, 64:128]), waits=[pb])
        g["ps_ss_free"] = cb
        em.emit("vector", lambda e, h=h, sim_ap=sim_ap: e.tensor_scalar(
            g["simf"][0:64, 0:64], sim_ap, g["sqcol"][0:64, h:h + 1], None,
            ALU.mult), waits=[sim_last, sqc_ev])
        em.emit("vector", lambda e: e.tensor_mul(
            g["simf"][0:64, 0:64], g["simf"][0:64, 0:64], g["skb"][0:64, 0:64]))
        mx = em.emit("vector", lambda e: e.reduce_max(
            g["mxc"][0:64, 0:1], g["simf"][0:64, 0:64], axis=AX.X, negate=True))
        ex = em.emit("scalar", lambda e: e.activation(
            g["simf"][0:64, 0:64], g["simf"][0:64, 0:64], AF.Exp,
            bias=g["mxc"][0:64, 0:1], accum_out=g["mxc"][0:64, 1:2]),
            waits=[mx])
        iv = em.emit("vector", lambda e: e.reciprocal(
            g["mxc"][0:64, 2:3], g["mxc"][0:64, 1:2]), waits=[ex])
        nm = em.emit("vector", lambda e: e.tensor_scalar(
            g["attn_bf"][0:64, 0:64], g["simf"][0:64, 0:64], g["mxc"][0:64, 2:3],
            None, ALU.mult), waits=[at_free])
        ro = (h % 2) * 64
        ttp = (0, 64) if ro else None
        ptr = em.emit("tensor", lambda e, ro=ro, ttp=ttp: e.transpose(
            g["ps_tr"][0][ro:ro + 64, 0:64], g["attn_bf"][0:64, 0:64],
            g["ident"][0:64, 0:64], tile_position=ttp),
            waits=[nm, g["ps_tr_free"][0]], inc="pe")
        ct = em.emit("vector", lambda e, ro=ro: e.tensor_copy(
            g["attnT"][ro:ro + 64, 0:64], g["ps_tr"][0][ro:ro + 64, 0:64]),
            waits=[ptr])
        g["ps_tr_free"][0] = ct
        at_free = ct
        for nchunk in range(4):
            tp = (64, 64) if ro else None
            pv = em.emit("tensor", lambda e, h=h, nchunk=nchunk, tp=tp, ro=ro: e.matmul(
                ps_pv[ro:ro + 64, 0:512],
                g["attnT"][ro:ro + 64, 0:64],
                v_fm[ro:ro + 64,
                     (h // 2) * 2048 + nchunk * 512:(h // 2) * 2048 + (nchunk + 1) * 512],
                start=True, stop=True, tile_position=tp),
                waits=[ct, pv_free], inc="pe")
            oe = em.emit("vector", lambda e, h=h, nchunk=nchunk, ro=ro: e.tensor_copy(
                o_f[ro:ro + 64,
                    (h // 2) * 2048 + nchunk * 512:(h // 2) * 2048 + (nchunk + 1) * 512],
                ps_pv[ro:ro + 64, 0:512]), waits=[pv])
            pv_free = oe
    of_ev = em.engine_done_ev("vector")
    epi = None
    for t in range(NT):
        for c in range(8):
            for nh in range(2):
                w = [of_ev, wready, qk_free]
                if epi is not None and c == 0 and nh == 0:
                    w.append(epi)
                last = c == 7 and nh == 1
                mm = em.emit("tensor", lambda e, c=c, nh=nh, t=t: e.matmul(
                    ps_qk[:, nh * 512:(nh + 1) * 512],
                    o_f[:, c * 2048 + t * 128: c * 2048 + (t + 1) * 128],
                    wo[:, c * D + nh * 512: c * D + nh * 512 + 512],
                    start=(c == 0), stop=(c == 7)), waits=w,
                    inc="pe" if last else None)
        epi = em.emit("vector", lambda e, t=t: e.scalar_tensor_tensor(
            g["x_res"][:, t * D:(t + 1) * D], ps_qk[:, :], cval(g, 1.0),
            g["x_res"][:, t * D:(t + 1) * D], ALU.mult, ALU.add), waits=[mm])
    return epi


def emit_l0attn_alloc(g, em, psb, pps):
    g["ps_S"] = pps("ps_S", [128, 2048])[:, :]
    g["ps_tr"] = (pps("ps_tr0", [128, 512], BF16), pps("ps_tr1", [128, 512], BF16))
    g["ps_tr_free"] = [em.ev("pe"), em.ev("pe")]
    g["ps_o"] = pps("ps_o", [128, 128])[:, :]
    g["ps_pr"] = pps("ps_pr", [128, 512])[:, :]
    g["wqkv_sb"] = psb("wqkv_sb", [128, 8 * 3072], BF16)
    g["xnf_dc"] = psb("xnf_dc", [128, NT * D], BF16)[:, :]
    g["q_sb"] = psb("q_sb", [128, 2048], BF16)
    g["k_sb"] = psb("k_sb", [128, 2048], BF16)
    g["v_sb"] = psb("v_sb", [128, 2048], BF16)
    g["embr_sb"] = psb("embr_sb", [128, 2048], BF16)
    g["PR_sb"] = (psb("PR_sb0", [128, 2048], BF16), psb("PR_sb1", [128, 2048], BF16))
    g["A_sb"] = psb("A_sb", [128, 2048], BF16)
    g["At_sb"] = psb("At_sb", [128, 2048], BF16)
    g["Aadd"] = (psb("AaddA", [128, 2048], BF16), psb("AaddB", [128, 2048], BF16))
    g["o_st"] = psb("o_st", [128, 2048], BF16)
    g["o_ld"] = (psb("o_ldA", [128, 1024], BF16), psb("o_ldB", [128, 1024], BF16))
    g["negt"] = psb("negt", [128, 512], BF16)
    g["qb8"] = psb("qb8", [128, 8])
    g["obrow"] = psb("obrow", [1, 1024], BF16)
    g["mxl"] = psb("mxl", [128, 8])
    g["pr_pos"] = psb("pr_pos", [128, 2048], BF16)
    g["pr_sins"] = psb("pr_sins", [128, 1024], BF16)
    g["pr_stage"] = psb("pr_stage", [128, 512], BF16)
    g["wo0_sb"] = g["xnf_dc"][:, 0:8 * D]


def emit_l0attn(em, g, do_prep):
    stats = emit_ln_stats(em, g)
    emit_scale_chain_ln(em, g, stats)
    wq = g["wqkv_sb"]
    xnf = g["xnf_dc"]
    q_sb, k_sb, v_sb = g["q_sb"], g["k_sb"], g["v_sb"]
    ps_S, ps_pr, ps_o = g["ps_S"], g["ps_pr"], g["ps_o"]
    A_sb, At_sb = g["A_sb"], g["At_sb"]
    PR_sb2 = g["PR_sb"]
    Aadd = g["Aadd"]
    ident = g["ident"]
    CQ = float(C_QKV)
    R = 2560

    ng = em.emit("gpsimd", lambda e: e.memset(g["negt"][:, :], NEG))
    dW = em.sem("dL0A")
    for dc in range(8):
        em.dma("gpsimd", wq[:, dc * 3072:(dc + 1) * 3072],
               g["in_qkv"][dc * 128:(dc + 1) * 128, :], inc=dW)
    wready = em.ev(dW)
    qld = em.dma("sync", g["qb8"][:, :], g["in_qb"], inc=em.sem("dQB"))
    obld = em.dma("gpsimd", g["obrow"][0:1, :], g["in_ob"], inc=em.sem("dOB"))
    em.emit("vector", lambda e: e.tensor_scalar_mul(
        g["qb8"][:, :], g["qb8"][:, :], cval(g, 0.125)), waits=[qld])
    qbe = em.emit("vector", lambda e: e.tensor_scalar_mul(
        g["obrow"][0:1, :], g["obrow"][0:1, :], cval(g, 32.0)[0:1, :]),
        waits=[obld], inc="ve")

    dPD = em.sem("dPAD")
    for par in range(2):
        skw = g["skw"][par].rearrange("(p a) -> p a", a=R)
        em.dma("scalar", skw[:, 2048:2560],
               g["negt"][:, :].rearrange("p (a b) -> p a b", a=1)[:, 0, 0:512],
               waits=[ng], inc=dPD)
    pad_ev = em.ev(dPD)

    if do_prep:
        dE, dSin, dEW = em.sem("dEM"), em.sem("dSin"), em.sem("dEW")
        for fc in range(2):
            em.dma("gpsimd", g["pr_pos"][:, fc * 1024:(fc + 1) * 1024],
                   g["in_pos"][fc * 128:(fc + 1) * 128, :], inc=dE)
        posev = em.ev(dE)
        sin_free = em.ev("pe")
        st_free = em.ev("dEW")
        for jc in range(4):
            s2 = em.dma("gpsimd",
                        g["pr_sins"][:, :].rearrange("p (c n) -> p c n", c=2),
                        g["in_sins"].rearrange("(c p) n -> p c n", c=2)
                        [:, :, jc * 512:(jc + 1) * 512],
                        waits=[sin_free], inc=dSin)
            for hp in range(8):
                for fc in range(2):
                    mm = em.emit("tensor", lambda e, hp=hp, fc=fc: e.matmul(
                        ps_pr[:, :],
                        g["pr_pos"][:, fc * 1024 + hp * 128: fc * 1024 + (hp + 1) * 128],
                        g["pr_sins"][:, fc * 512:(fc + 1) * 512],
                        start=(fc == 0), stop=(fc == 1)),
                        waits=[s2, posev, g.get("pr_free")],
                        inc="pe" if fc == 1 else None)
                ev = em.emit("vector", lambda e: e.tensor_copy(
                    g["pr_stage"][:, :], ps_pr[:, :]),
                    waits=[mm, st_free])
                g["pr_free"] = ev
                st_free = em.dma("scalar",
                                 g["embr_s"].rearrange("(hp p n) -> hp p n", hp=8, p=128)
                                 [hp, :, jc * 512:(jc + 1) * 512],
                                 g["pr_stage"][:, :], waits=[ev], inc=dEW)
            sin_free = em.engine_done_ev("tensor")
        g["embr_ready"] = em.ev("dEW")

    xn_evs = []
    for t in range(NT):
        xn_evs.append(emit_xn_tile(em, g, t, xnf, dc_major=True))
    xn_all = xn_evs[-1]

    dEL = em.sem("dEL")
    dSW = [em.sem("dSW0"), em.sem("dSW1")]
    dSR = [em.sem("dSR0"), em.sem("dSR1")]
    dOS = em.sem("dOS")
    embr_free = em.ev("pe")
    pair_done = em.ev("pe")
    exp_prev = None
    g["exp_prev_b"] = [None, None]
    aadd_used = [None, None]
    g["o_st_free"] = em.ev("dOS")
    for hp in range(DBG.get("l0a_pairs", 8)):
        eld = em.dma("sync", g["embr_sb"][:, :],
                     g["embr_s"].rearrange("(hp p n) -> hp p n", hp=8, p=128)[hp, :, :],
                     waits=[g["embr_ready"], embr_free], inc=dEL)
        pw = [wready, xn_all, pair_done]
        qk_ev = {}
        for which, base, scale in (("q", hp * 128, 0.125 * CQ),
                                   ("k", 1024 + hp * 128, CQ)):
            dst = q_sb if which == "q" else k_sb
            for tc in range(4):
                bank = (tc % 2) * 512
                w = list(pw) + [qk_ev.get((which, tc - 2)), qbe, exp_prev]
                for dc in range(8):
                    mm = em.emit("tensor",
                            lambda e, dc=dc, tc=tc, base=base, bank=bank: e.matmul(
                                ps_S[:, bank:bank + 512],
                                wq[:, dc * 3072 + base: dc * 3072 + base + 128],
                                xnf[:, dc * 2048 + tc * 512: dc * 2048 + (tc + 1) * 512],
                                start=(dc == 0), stop=(dc == 7),
                                skip_group_check=True),
                            waits=w, inc="pe" if dc == 7 else None)
                if which == "q":
                    ev = em.emit("scalar", lambda e, tc=tc, bank=bank, hp=hp, dst=dst: e.activation(
                        dst[:, tc * 512:(tc + 1) * 512], ps_S[:, bank:bank + 512],
                        AF.Identity, scale=FP(0.125 * CQ),
                        bias=g["qb8"][:, hp:hp + 1]), waits=[mm])
                else:
                    ev = em.emit("scalar", lambda e, tc=tc, bank=bank, dst=dst: e.activation(
                        dst[:, tc * 512:(tc + 1) * 512], ps_S[:, bank:bank + 512],
                        AF.Identity, scale=FP(CQ)), waits=[mm])
                qk_ev[(which, tc)] = ev
        for tg in range(4):
            bank = 1024 + (tg % 2) * 512
            w = list(pw) + [qk_ev.get(("v", tg - 2)), exp_prev]
            for ti in range(4):
                t = tg * 4 + ti
                for dc in range(8):
                    mm = em.emit("tensor",
                            lambda e, dc=dc, ti=ti, t=t, bank=bank, hp=hp: e.matmul(
                                ps_S[:, bank + ti * 128: bank + ti * 128 + 128],
                                xnf[:, dc * 2048 + t * 128: dc * 2048 + (t + 1) * 128],
                                wq[:, dc * 3072 + 2048 + hp * 128:
                                   dc * 3072 + 2048 + (hp + 1) * 128],
                                start=(dc == 0 and ti == 0), stop=(dc == 7),
                                skip_group_check=True),
                            waits=w, inc="pe" if (dc == 7 and ti == 3) else None)
            ev = em.emit("scalar", lambda e, tg=tg, bank=bank: e.activation(
                v_sb[:, tg * 512:(tg + 1) * 512], ps_S[:, bank:bank + 512],
                AF.Identity, scale=FP(CQ)), waits=[mm])
            qk_ev[("v", tg)] = ev
        qke = em.engine_done_ev("scalar")

        for hodd in range(DBG.get("l0a_heads", 2)):
            h = 2 * hp + hodd
            ro = hodd * 64
            rtp = (64, 0) if hodd else None

            def emit_pr(j, qke=qke, eld=eld, ro=ro, rtp=rtp, hp=hp):
                jmin = 1920 - 128 * j
                width = 2048 - jmin
                par = j % 2
                npc = (width + 511) // 512
                prev = None
                PR_sb = PR_sb2[par]
                for pc in range(npc):
                    c0 = jmin + pc * 512
                    cw = min(512, 2048 - c0)
                    mm = em.emit("tensor", lambda e, j=j, c0=c0, cw=cw, ro=ro, rtp=rtp: e.matmul(
                        ps_pr[:, 0:cw],
                        q_sb[ro:ro + 64, j * 128:(j + 1) * 128],
                        g["embr_sb"][ro:ro + 64, c0:c0 + cw],
                        start=True, stop=True, tile_position=rtp),
                        waits=[qke, eld, g.get("pr_free")], inc="pe")
                    prev = em.emit("vector", lambda e, c0=c0, cw=cw, PR_sb=PR_sb: e.tensor_copy(
                        PR_sb[:, c0:c0 + cw], ps_pr[:, 0:cw]),
                        waits=[mm, g.get(f"prsb_free{par}")])
                    g["pr_free"] = prev
                skw = g["skw"][par].rearrange("(p a) -> p a", a=R)
                wrev = em.dma("scalar", skw[:, jmin:2048], PR_sb[:, jmin:2048],
                              waits=[prev, pad_ev,
                                     g.get(f"skr_done{par}")], inc=dSW[par])
                g[f"prsb_free{par}"] = wrev
                W = 128 * j + 128
                Wc = ((W + 511) // 512) * 512
                bview = g["skw"][par][0:128 * (R - 1)].rearrange("(p a) -> p a", a=R - 1)
                rdev = em.dma("sync", Aadd[par][:, 0:Wc],
                              bview[:, 2047 - 128 * j: 2047 - 128 * j + Wc],
                              waits=[em.ev(dSW[par]), aadd_used[par]], inc=dSR[par])
                g[f"skr_done{par}"] = em.ev(dSR[par])
                return rdev

            rd = {0: emit_pr(0), 1: emit_pr(1)}
            pend = None

            def do_tpv(i, nm):
                nkc = i + 1
                for kb in range((nkc + 3) // 4):
                    nj = min(4, nkc - kb * 4)
                    for jj in range(nj):
                        kc = kb * 4 + jj
                        pt = em.emit("tensor", lambda e, kb=kb, jj=jj, kc=kc: e.transpose(
                            g["ps_tr"][kb % 2][:, jj * 128:(jj + 1) * 128],
                            A_sb[:, kc * 128:(kc + 1) * 128], ident[:, :]),
                            waits=[nm, g["ps_tr_free"][kb % 2]],
                            inc="pe" if jj == nj - 1 else None)
                    te = em.emit("scalar", lambda e, kb=kb, nj=nj: e.copy(
                        At_sb[:, kb * 512:kb * 512 + nj * 128],
                        g["ps_tr"][kb % 2][:, 0:nj * 128]), waits=[pt])
                    g["ps_tr_free"][kb % 2] = te
                g["asb_free"] = em.engine_done_ev("tensor")
                ate = em.engine_done_ev("scalar")
                for kc in range(nkc):
                    otp = (0, 64) if hodd else None
                    mm = em.emit("tensor", lambda e, kc=kc, ro=ro, otp=otp, i=i: e.matmul(
                        ps_o[ro:ro + 64, 0:128],
                        v_sb[:, kc * 128 + ro: kc * 128 + ro + 64],
                        At_sb[:, kc * 128:(kc + 1) * 128],
                        start=(kc == 0), stop=(kc == nkc - 1),
                        skip_group_check=True, tile_position=otp),
                        waits=[ate, g.get("po_free")], inc="pe" if kc == nkc - 1 else None)
                oe = em.emit("vector", lambda e, ro=ro, i=i: e.tensor_copy(
                    g["o_st"][ro:ro + 64, i * 128:(i + 1) * 128],
                    ps_o[ro:ro + 64, 0:128]), waits=[mm, g["o_st_free"]])
                g["po_free"] = oe

            for i in range(DBG.get("l0a_tiles", NT)):
                W = 128 * i + 128
                Wc = ((W + 511) // 512) * 512
                nch = Wc // 512
                par = i % 2
                if Wc <= 1024:
                    base = 1024 * (i % 2)
                    bset = (i % 2,)
                else:
                    base = 0
                    bset = (0, 1)
                sw = [qke] + [g["exp_prev_b"][b2] for b2 in bset]
                for c in range(nch):
                    em.emit("tensor", lambda e, i=i, c=c, ro=ro, rtp=rtp, base=base: e.matmul(
                        ps_S[:, base + c * 512:base + (c + 1) * 512],
                        q_sb[ro:ro + 64, i * 128:(i + 1) * 128],
                        k_sb[ro:ro + 64, c * 512:(c + 1) * 512],
                        start=True, stop=False, skip_group_check=True,
                        tile_position=rtp),
                        waits=sw)
                    mm = em.emit("tensor", lambda e, i=i, c=c, par=par, base=base: e.matmul(
                        ps_S[:, base + c * 512:base + (c + 1) * 512], ident[:, :],
                        Aadd[par][:, c * 512:(c + 1) * 512],
                        start=False, stop=True, skip_group_check=True),
                        waits=[rd[i]], inc="pe")
                    em.emit("vector", lambda e, c=c, base=base: e.reduce_max(
                        g["mxl"][:, 4 + c:5 + c],
                        ps_S[:, base + c * 512:base + (c + 1) * 512],
                        axis=AX.X), waits=[mm])
                aadd_used[par] = mm
                if i + 2 < DBG.get("l0a_tiles", NT):
                    rd[i + 2] = emit_pr(i + 2)
                if pend is not None:
                    do_tpv(*pend)
                if nch == 1:
                    mx = em.emit("vector", lambda e: e.tensor_scalar_mul(
                        g["mxl"][:, 0:1], g["mxl"][:, 4:5], cval(g, -1.0)))
                else:
                    mx = em.emit("vector", lambda e, nch=nch: e.tensor_reduce(
                        g["mxl"][:, 0:1], g["mxl"][:, 4:4 + nch], axis=AX.X,
                        op=ALU.max, negate=True))
                ex = em.emit("scalar", lambda e, Wc=Wc, base=base: e.activation(
                    A_sb[:, 0:Wc], ps_S[:, base:base + Wc], AF.Exp,
                    bias=g["mxl"][:, 0:1], accum_out=g["mxl"][:, 1:2]),
                    waits=[mx, g.get("asb_free")])
                for b2 in bset:
                    g["exp_prev_b"][b2] = ex
                exp_prev = ex
                em.emit("vector", lambda e: e.reciprocal(
                    g["mxl"][:, 2:3], g["mxl"][:, 1:2]), waits=[ex])
                nm = em.emit("vector", lambda e, Wc=Wc: e.tensor_scalar(
                    A_sb[:, 0:Wc], A_sb[:, 0:Wc], g["mxl"][:, 2:3], None,
                    ALU.mult))
                pend = (i, nm)
            do_tpv(*pend)
        ose = em.dma("scalar",
                     g["o_s"].rearrange("(hp p n) -> hp p n", hp=8, p=128)[hp, :, :],
                     g["o_st"][:, :], waits=[em.ev("ve")], inc=dOS)
        g["o_st_free"] = ose
        embr_free = em.engine_done_ev("tensor")
        pair_done = embr_free
    if DBG.get("dump"):
        lastall = [em.engine_done_ev(e) for e in ("vector", "scalar", "tensor")]
        em.dma("gpsimd", g["dbg1"][:, 0:2048], q_sb[:, :], waits=lastall, inc="dDB")
        em.dma("gpsimd", g["dbg1"][:, 2048:4096], k_sb[:, :], waits=lastall, inc="dDB")
        em.dma("gpsimd", g["dbg2"][:, 0:2048], Aadd[0][:, :], waits=lastall, inc="dDB")
        em.dma("gpsimd", g["dbg2"][:, 2048:4096], A_sb[:, :], waits=lastall, inc="dDB")
        em.dma("gpsimd", g["dbg3"][:, 0:1024], g["o_st"][:, 0:1024], waits=lastall, inc="dDB")
        em.dma("gpsimd", g["dbg4"][:, 0:4], g["mxl"][:, :], waits=lastall, inc="dDB",
               allow_slow_non_contiguous=True)
        em.dma("gpsimd", g["dbg3"][:, 1024:3072], g["embr_sb"][:, :], waits=lastall, inc="dDB")
        em.dma("gpsimd", g["dbg5"],
               g["skw"][0].rearrange("(p a) -> p a", a=2560),
               waits=lastall, inc="dDB")
    dWO = em.sem("dWO")
    for c in range(8):
        em.dma("gpsimd", g["wo0_sb"][:, c * D:(c + 1) * D],
               g["in_wo0"][c * 128:(c + 1) * 128, :],
               waits=[pair_done], inc=dWO)
    wo_ready = em.ev(dWO)
    dOL = em.sem("dOL")
    os_view = g["o_s"].rearrange("(hp p n) -> hp p n", hp=8, p=128).transpose([1, 0, 2])
    epi = None
    old_ev = [None, None]
    for t in range(NT):
        par = t % 2
        lw = [em.ev(dOS), old_ev[par]]
        lde = em.dma("sync", g["o_ld"][par][:, :].rearrange("p (hp n) -> p hp n", hp=8),
                     os_view[:, :, t * 128:(t + 1) * 128], waits=lw, inc=dOL)
        for c in range(8):
            for nh in range(2):
                w = [lde, wo_ready, qbe, exp_prev]
                if epi is not None and c == 0 and nh == 0:
                    w.append(epi)
                em.emit("tensor", lambda e, c=c, nh=nh, par=par, t=t: e.matmul(
                    ps_S[:, par * 1024 + nh * 512: par * 1024 + nh * 512 + 512],
                    g["o_ld"][par][:, c * 128:(c + 1) * 128],
                    g["wo0_sb"][:, c * D + nh * 512: c * D + nh * 512 + 512],
                    start=(c == 0), stop=False, skip_group_check=True), waits=w)
        for nh in range(2):
            mm = em.emit("tensor", lambda e, nh=nh, par=par: e.matmul(
                ps_S[:, par * 1024 + nh * 512: par * 1024 + nh * 512 + 512],
                g["ones_row_bf"][0:1, :],
                g["obrow"][0:1, nh * 512:(nh + 1) * 512],
                start=False, stop=True, skip_group_check=True),
                waits=[qbe], inc="pe" if nh == 1 else None)
        old_ev[par] = mm
        epi = em.emit("vector", lambda e, t=t, par=par: e.scalar_tensor_tensor(
            g["x_res"][:, t * D:(t + 1) * D], ps_S[:, par * 1024: par * 1024 + 1024],
            cval(g, 0.03125), g["x_res"][:, t * D:(t + 1) * D],
            ALU.mult, ALU.add), waits=[mm])
    return epi


def build_consts(em, g):
    for i, v in enumerate(CVALS):
        em.emit("gpsimd", lambda e, i=i, v=v: e.memset(g["cc"][:, i:i + 1], v))
    mz = em.emit("gpsimd", lambda e: e.memset(g["ident"][:, :], 0.0), inc="po")
    em.emit("gpsimd", lambda e: e.affine_select(
        out=g["ident"][:, :], in_=g["ident"][:, :], compare_op=ALU.not_equal,
        fill=1.0, base=0, pattern=[[-1, 128]], channel_multiplier=1), waits=[mz])
    em.emit("gpsimd", lambda e: e.memset(g["ones_col_bf"][:, :], 1.0))
    em.emit("gpsimd", lambda e: e.memset(g["ones_row_bf"][:, :], 1.0))
    em.emit("gpsimd", lambda e: e.memset(g["ones_col_f"][:, :], 1.0))
    em.emit("gpsimd", lambda e: e.memset(g["ones_row_f"][:, :], 1.0), inc="po")
    return em.ev("po")


def build_program(phases, unfused_io=True):
    nc = bass.Bass("TRN2", target_bir_lowering=False, detect_race_conditions=DETECT_RACES)
    g = {"nc": nc}

    x_in = nc.dram_tensor("x_in", [S, D], F32, kind="ExternalInput").ap()
    x_out = nc.dram_tensor("x_out", [S, D], F32, kind="ExternalOutput").ap()
    if DBG.get("dump"):
        g["dbg1"] = nc.dram_tensor("dbg1", [128, 4096], F32, kind="ExternalOutput").ap()
        g["dbg2"] = nc.dram_tensor("dbg2", [128, 4096], F32, kind="ExternalOutput").ap()
        g["dbg3"] = nc.dram_tensor("dbg3", [128, 3072], F32, kind="ExternalOutput").ap()
        g["dbg4"] = nc.dram_tensor("dbg4", [128, 16], F32, kind="ExternalOutput").ap()
        g["dbg5"] = nc.dram_tensor("dbg5", [128, 2560], F32, kind="ExternalOutput").ap()
    need_l0ffn = any(p.startswith("l0ffn") for p in phases)
    if "l0attn" in phases:
        g["in_qkv"] = nc.dram_tensor("qkv_r", [D, 3 * D], F32, kind="ExternalInput").ap()
        g["in_qb"] = nc.dram_tensor("qb_r", [128, 8], F32, kind="ExternalInput").ap()
        g["in_pos"] = nc.dram_tensor("pos_r", [256, D], F32, kind="ExternalInput").ap()
        g["in_sins"] = nc.dram_tensor("sins_r", [256, S], F32, kind="ExternalInput").ap()
        g["in_wo0"] = nc.dram_tensor("wo0", [D, D], F32, kind="ExternalInput").ap()
        g["in_ob"] = nc.dram_tensor("ob_r", [1, D], F32, kind="ExternalInput").ap()
        g["embr_s"] = nc.dram_tensor("embr_s", [8 * 128 * S], BF16, kind="Internal").ap()
        g["skw"] = (nc.dram_tensor("skw0", [128 * 2560], BF16, kind="Internal").ap(),
                    nc.dram_tensor("skw1", [128 * 2560], BF16, kind="Internal").ap())
        g["o_s"] = nc.dram_tensor("o_s", [8 * 128 * S], BF16, kind="Internal").ap()
    if "l1attn" in phases:
        g["in_wqkv"] = nc.dram_tensor("wqkv_g", [D, 3 * D], F32, kind="ExternalInput").ap()
        g["in_wo"] = nc.dram_tensor("wo1", [D, D], F32, kind="ExternalInput").ap()
        g["in_temp"] = nc.dram_tensor("temp_r", [16], F32, kind="ExternalInput").ap()
        g["in_gamma"] = nc.dram_tensor("gamma_r", [128, 8], F32, kind="ExternalInput").ap()
    if any(p.startswith("l1ffn") for p in phases):
        g["in_wg"] = nc.dram_tensor("wg_t", [D, FF1], I32, kind="ExternalInput").ap()
        g["in_wu"] = nc.dram_tensor("wu_t", [D, FF1], I32, kind="ExternalInput").ap()
        g["in_wd"] = nc.dram_tensor("wd_t", [FF1, D], I32, kind="ExternalInput").ap()
    if need_l0ffn:
        g["in_w_up"] = nc.dram_tensor("w_up_t", [D, FF0], F32, kind="ExternalInput").ap()
        g["in_w_dn"] = nc.dram_tensor("w_dn_t", [FF0, D], F32, kind="ExternalInput").ap()
        g["scr_wup"] = nc.dram_tensor("scr_wup", [D, FF0], BF16, kind="Internal").ap()
        g["scr_wdn"] = nc.dram_tensor("scr_wdn", [FF0, D], BF16, kind="Internal").ap()

    with ExitStack() as ctx:
        em = Em(nc, ctx)
        for s in ("pe", "ve", "se", "po", "dX"):
            em.sem(s)

        def sb(name, shape, dtype=F32):
            return ctx.enter_context(nc.sbuf_tensor(name, shape, dtype))

        g["x_res"] = sb("x_res", [128, NT * D])
        g["ident"] = sb("ident", [128, 128], BF16)
        g["ones_col_bf"] = sb("ones_col_bf", [128, 1], BF16)
        g["ones_row_bf"] = sb("ones_row_bf", [1, 128], BF16)
        g["ones_col_f"] = sb("ones_col_f", [128, 1])
        g["ones_row_f"] = sb("ones_row_f", [1, 128])
        g["bn12"] = sb("bn12", [128, 12])
        g["mv"] = sb("mv", [128, 2 * NT])
        g["scol"] = sb("scol", [128, NT])
        g["bcol"] = sb("bcol", [128, NT])
        g["c16a"] = sb("c16a", [128, NT])
        g["c16b"] = sb("c16b", [128, NT])
        g["scratch8"] = sb("scratch8", [128, 8])
        g["nsc"] = sb("nsc", [128, 4])
        g["cc"] = sb("cc", [128, len(CVALS)])
        g["shared2k"] = sb("shared2k", [128, 512])[:, :]
        g["xt_bf"] = sb("xt_bf", [128, D], BF16)[:, :]

        dX = "dX"
        for t in range(NT):
            em.dma("sync", g["x_res"][:, t * D:(t + 1) * D],
                   x_in[t * 128:(t + 1) * 128, :], inc=dX)
        xall = em.ev(dX)
        for e in ENGINES:
            em.emit(e, lambda eng: None, waits=[xall], real=False)

        cev = build_consts(em, g)
        g["consts_ev"] = cev
        for e in ("tensor", "vector", "scalar"):
            em.emit(e, lambda eng: None, waits=[cev], real=False)

        first_l0ffn_done = False
        for phase_i, ph in enumerate(phases):
            with ExitStack() as pctx:
                def psb(name, shape, dtype=F32, phase_i=phase_i):
                    return pctx.enter_context(
                        nc.sbuf_tensor(f"p{phase_i}_{name}", shape, dtype))

                def pps(name, shape, dtype=F32, phase_i=phase_i):
                    return pctx.enter_context(
                        nc.psum_tensor(f"p{phase_i}_{name}", shape, dtype))

                if ph.startswith("l0ffn"):
                    g["ps_tr"] = (pps("ps_tr0", [128, 512], BF16),
                                  pps("ps_tr1", [128, 512], BF16))
                    g["ps_tr_free"] = [em.ev("pe"), em.ev("pe")]
                    g["ps_up"] = (pps("ps_up0", [128, 512]), pps("ps_up1", [128, 512]))
                    g["ps_y"] = (pps("ps_y0", [128, 1024]), pps("ps_y1", [128, 1024]))
                    g["wup_sb"] = psb("wup_sb", [128, 8 * FF0], BF16)
                    g["wdn_sb"] = psb("wdn_sb", [128, 32 * D], BF16)
                    g["xnf2"] = (psb("xnf0", [128, D], BF16), psb("xnf1", [128, D], BF16))
                    g["hbf"] = (psb("hbf0", [128, 512], BF16), psb("hbf1", [128, 512], BF16))
                    g["htf"] = (psb("htf0", [128, 512], BF16), psb("htf1", [128, 512], BF16))
                    g["qacc"] = psb("qacc", [128, 8])
                    do_prep = not first_l0ffn_done
                    if do_prep:
                        g["ps_b"] = g["ps_up"][1]
                        g["ps_b_free"] = em.ev("pe")
                        g["stg_w"] = g["shared2k"]
                        g["stg_w2"] = g["xt_bf"].bitcast(F32)
                        g["tern_acc"] = g["nsc"]
                        g["thr_row"] = psb("thr_row", [1, 1])
                        g["thr_col"] = psb("thr_col", [128, 1])
                        g["nthr_col"] = psb("nthr_col", [128, 1])
                        g["tern_sgn"] = g["xnf2"][0]
                        g["tern_gt"] = g["xnf2"][1]
                        g["tern_stg_free"] = [em.ev("ve")] * 4
                        g["tern_sgn_free"] = [em.ev("ve")] * 4
                        g["write_scratch"] = True
                    emit_l0ffn(em, g, do_prep=do_prep,
                               from_scratch=first_l0ffn_done)
                    first_l0ffn_done = True
                elif ph.startswith("l1ffn"):
                    g["ps_tr"] = (pps("ps_tr0", [128, 512], BF16),
                                  pps("ps_tr1", [128, 512], BF16))
                    g["ps_tr_free"] = [em.ev("pe"), em.ev("pe")]
                    g["ps_gu"] = ((pps("ps_g0", [128, 512]), pps("ps_g1", [128, 512])),
                                  (pps("ps_u0", [128, 512]), pps("ps_u1", [128, 512])))
                    g["ps_y"] = pps("ps_y", [128, 1024])[:, :]
                    g["wg_sb"] = psb("wg_sb", [128, 8 * (FF1 // 2)], BF16)
                    g["wu_sb"] = psb("wu_sb", [128, 8 * (FF1 // 2)], BF16)
                    g["wd_sb"] = psb("wd_sb", [128, 12 * D], BF16)
                    g["xnf_all"] = psb("xnf_all", [128, NT * D], BF16)
                    g["hbf"] = (psb("hbf0", [128, 512], BF16), psb("hbf1", [128, 512], BF16))
                    g["htf"] = (psb("htf0", [128, 512], BF16), psb("htf1", [128, 512], BF16))
                    g.pop("l1w_free", None)
                    emit_l1ffn(em, g)
                elif ph == "l0attn":
                    emit_l0attn_alloc(g, em, psb, pps)
                    emit_l0attn(em, g, do_prep=not g.get("l0attn_prep_done"))
                    g["l0attn_prep_done"] = True
                elif ph == "l1attn":
                    emit_l1attn_alloc(g, em, psb, pps)
                    emit_l1attn(em, g)
                else:
                    raise NotImplementedError(ph)
                em.barrier()

        dO = em.sem("dO")
        for t in range(NT):
            em.dma("sync", x_out[t * 128:(t + 1) * 128, :],
                   g["x_res"][:, t * D:(t + 1) * D], inc=dO)
        em.emit("sync", lambda e: None, waits=[em.ev("dO")], real=False)
        em.drain_dmas()

        blk = ctx.enter_context(nc.Block())
        em.play(blk)
    return nc



def prep_inputs(inputs):
    d = {}
    d["w_up_t"] = np.ascontiguousarray(inputs["L0_ffn_w_up"].T)
    d["w_dn_t"] = np.ascontiguousarray(inputs["L0_ffn_w_down"].T)
    d["wg_t"] = np.ascontiguousarray(inputs["L1_ffn_w_gate"].T)
    d["wu_t"] = np.ascontiguousarray(inputs["L1_ffn_w_up"].T)
    d["wd_t"] = np.ascontiguousarray(inputs["L1_ffn_w_down"].T)
    HEADS, FREQS, MAX_PERIOD = 16, 256, 10000
    pos = np.arange(S, dtype=np.float64)[:, None]
    idx = np.arange(FREQS)[None, :]
    inv_freq = MAX_PERIOD ** (-(idx // 2 * 2) / FREQS)
    angle = pos * inv_freq
    sins = np.where(idx % 2 == 0, np.sin(angle), np.cos(angle)).astype(np.float32)
    sins = np.float32(np.sqrt(2.0)) * sins
    sins_scaled = sins * np.float32(FREQS ** (-0.5))
    d["sins_r"] = np.ascontiguousarray(sins_scaled[::-1, :].T)
    d["qkv_r"] = np.ascontiguousarray(inputs["L0_attn_qkv"].reshape(D, 3 * D))
    d["qb_r"] = np.ascontiguousarray(inputs["L0_attn_q_bias"].reshape(1024).reshape(8, 128).T)
    d["pos_r"] = np.ascontiguousarray(inputs["L0_attn_positional"].reshape(256, D))
    d["wo0"] = np.ascontiguousarray(inputs["L0_attn_out_kernel"])
    d["ob_r"] = np.ascontiguousarray(inputs["L0_attn_out_bias"].reshape(1, D))
    d["wqkv_g"] = np.ascontiguousarray(inputs["L1_attn_w_qkv"])
    d["wo1"] = np.ascontiguousarray(inputs["L1_attn_w_out"])
    d["temp_r"] = np.ascontiguousarray(inputs["L1_attn_temperature"].reshape(16))
    d["gamma_r"] = np.ascontiguousarray(inputs["L1_attn_gamma"].reshape(8, 128).T)
    return d


def kernel(**inputs):
    x = np.ascontiguousarray(inputs["L0_attn_input"])
    host = prep_inputs(inputs)
    phases = list(ALL_PHASES)
    nc = build_program(phases)
    in_maps = []
    for c in range(8):
        m = {"x_in": np.ascontiguousarray(x[c])}
        m.update(host)
        in_maps.append(m)
    res = run_bass_kernel_spmd(nc, in_maps, core_ids=list(range(8)))
    out = np.stack([r["x_out"] for r in res.results], axis=0)
    return out.astype(np.float32)
